# version 05
# baseline (speedup 1.0000x reference)
import numpy as np
import concourse.bass as bass
import concourse.mybir as mybir
from concourse.bass_utils import run_bass_kernel_spmd

F32 = mybir.dt.float32
BF16 = mybir.dt.bfloat16
AF = mybir.ActivationFunctionType
ALU = mybir.AluOpType
AX = mybir.AxisListType

B, L, D, H, DH = 4, 8192, 1024, 4, 256
TBUF, TAIL = 8192, 4096
TSB = 512
NSB = TBUF // TSB
CK = 128
NCPS = TSB // CK
NCK = TBUF // CK
NW = 3 * D
EPS_LN = 1e-5
EPS_L2 = 1e-12
LN127 = float(np.log(np.float64(127.0)))

DBG = ()
CONSTVALS = [1e-5, 1e-12, 1e-4, 1.0, -0.5, -1.0, 1.0 / 1024, LN127, -LN127,
             0.5, -1e-12, 2.0, 0.0, 3.0, 4.0, 5.0]


def ts(i, n):
    return slice(i * n, (i + 1) * n)


class Sem:
    def __init__(self, cm):
        self._cm = cm
        self.h = cm.__enter__()
        self.n = 0

    def inc(self, k=1):
        self.n += k
        return self.h


class P:
    ENGINES = ("tensor", "vector", "scalar", "gpsimd", "sync")

    def __init__(self, nc):
        self.nc = nc
        self.q = {e: [] for e in self.ENGINES}
        self.sems = []

    def sem(self, name):
        s = Sem(self.nc.semaphore(name))
        self.sems.append(s)
        return s

    def on(self, engine, fn):
        self.q[engine].append(fn)

    def wait(self, engine, sem, val=None):
        v = sem.n if val is None else val
        if v <= 0:
            return
        self.q[engine].append(lambda eng, s=sem.h, vv=v: eng.wait_ge(s, vv))

    def barrier(self, bar):
        target = bar.n + len(self.ENGINES)
        for e in self.ENGINES:
            self.q[e].append(lambda eng, s=bar.h: eng.sem_inc(s, 1))
        bar.n = target
        for e in self.ENGINES:
            self.q[e].append(lambda eng, s=bar.h, v=target: eng.wait_ge(s, v))

    def emit(self):
        def mk(e):
            def run(eng):
                for f in self.q[e]:
                    f(eng)
            return run
        with self.nc.Block() as block:
            block.tensor(mk("tensor"))
            block.vector(mk("vector"))
            block.scalar(mk("scalar"))
            block.gpsimd(mk("gpsimd"))
            block.sync(mk("sync"))


def build_program(debug=DBG, stop_after=None):
    nc = bass.Bass("TRN2", target_bir_lowering=False, debug=False)
    p = P(nc)
    alloc = []

    def sbuf(name, shape, dt=F32):
        cm = nc.sbuf_tensor(name, shape, dt)
        alloc.append(cm)
        return cm.__enter__()

    def psumt(name, shape, dt=F32):
        cm = nc.psum_tensor(name, shape, dt)
        alloc.append(cm)
        return cm.__enter__()

    def I(name, shape, dt=F32):
        return nc.dram_tensor(name, shape, dt, kind="ExternalInput").ap()

    x_tm = I("x_tm", [TBUF, D])
    x_fm = I("x_fm", [D, TBUF])
    wqkvb = I("wqkvb", [D, NW])
    wbeta = I("wbeta", [D, 32])
    convw = I("convw", [3 * D, 4])
    fir_toep = I("fir_toep", [D, 128, 128])
    w1h = I("w1h", [D, D])
    w1s = I("w1s", [16, D])
    b1 = I("b1", [128, 8])
    w2 = I("w2", [D, 4])
    b2 = I("b2", [4])
    wo = I("wo", [D, D])
    ffn_rhs = I("ffn_rhs", [D, D])
    o_norm = I("o_norm", [DH])
    scal = I("scal", [4])
    ident = I("ident", [128, 128])
    ones = I("ones", [128, 128])
    mask_low = I("mask_low", [128, 128])
    mask_up0 = I("mask_up0", [128, 128])
    cvec = I("cvec", [128, 16])

    out_fm = nc.dram_tensor("out_fm", [D, TAIL], F32, kind="ExternalOutput").ap()
    dbg = {}
    for name, shape in [
        ("d_qn", [NSB, 128, 16 * TSB]), ("d_v", [NSB, 128, 8 * TSB]),
        ("d_beta", [NSB, 4, TSB]), ("d_delta", [TAIL, D]),
        ("d_stats", [TAIL // 128, 128, 64]), ("d_fw", [TAIL // 128, 128, 16]),
        ("d_o", [TAIL, D]), ("d_y1", [8, 128, 8 * TSB]), ("d_L", [TAIL, 2 * D]),
        ("d_qlhs", [8, 128, 8 * TSB]),
    ]:
        if name in debug:
            dbg[name] = nc.dram_tensor(name, shape, F32,
                                       kind="ExternalOutput").ap()

    wlo_slab = nc.dram_tensor("wlo_slab", [25, 8, 128, 128], BF16).ap()
    w1h_hi_d = nc.dram_tensor("w1h_hi_d", [8, 128, D], BF16).ap()
    w1h_lo_d = nc.dram_tensor("w1h_lo_d", [8, 128, D], BF16).ap()
    wo_hi_d = nc.dram_tensor("wo_hi_d", [8, 128, D], BF16).ap()
    wo_lo_d = nc.dram_tensor("wo_lo_d", [8, 128, D], BF16).ap()
    v_pad = nc.dram_tensor("v_pad", [64 + TBUF, D], F32).ap()
    delta_d = nc.dram_tensor("delta_d", [TAIL, D], F32).ap()
    l_d = nc.dram_tensor("l_d", [TAIL, 2 * D], F32).ap()
    xn_d = nc.dram_tensor("xn_d", [2, 8, 128, TAIL], BF16).ap()
    o_d = nc.dram_tensor("o_d", [2, TAIL, D], BF16).ap()
    qrhs_d = nc.dram_tensor("qrhs_d", [8, 128, D], BF16).ap()

    bar = p.sem("bar")
    s_mm = p.sem("s_mm")
    s_ve = p.sem("s_ve")
    s_ac = p.sem("s_ac")
    s_gp = p.sem("s_gp")

    class DmaQ:
        def __init__(self, engine, name, depth=6):
            self.engine = engine
            self.depth = depth
            self.sems = [p.sem(f"{name}{i}") for i in range(depth)]
            self.cnt = 0

        def dma(self, dst, src):
            s = self.sems[self.cnt % self.depth]
            if self.cnt >= self.depth:
                p.wait(self.engine, s)
            if self.engine == "scalar":
                p.wait("scalar", s_ac)
                p.wait("scalar", s_ve)
            self.cnt += 1
            p.on(self.engine, lambda eng, d=dst, ss=src, h=s.inc(16):
                 eng.dma_start(out=d, in_=ss).then_inc(h, 16))

        def wait_all(self, engine):
            for s in self.sems:
                p.wait(engine, s)

    ldq = DmaQ("sync", "qld", 6)
    stq = DmaQ("scalar", "qst", 6)
    gpq = DmaQ("scalar", "qgp", 4)

    def ld(dst, src):
        ldq.dma(dst, src)

    def st(dst, src):
        stq.dma(dst, src)

    def g2(dst, src):
        gpq.dma(dst, src)

    osems = {"vector": s_ve, "scalar": s_ac, "gpsimd": s_gp}

    def op(engine, meth, *a, inc=None, **kw):
        if engine in osems:
            osem = osems[engine]
            prior = osem.n
            h = osem.inc()

            def f(eng, meth=meth, a=a, kw=kw, h=h, prior=prior):
                if prior > 0:
                    eng.wait_ge(h, prior)
                getattr(eng, meth)(*a, **kw).then_inc(h, 1)
            p.on(engine, f)
        else:
            h = inc.inc() if inc is not None else None

            def f(eng, meth=meth, a=a, kw=kw, h=h):
                inst = getattr(eng, meth)(*a, **kw)
                if h is not None:
                    inst.then_inc(h, 1)
            p.on(engine, f)

    def _finish():
        stq.wait_all("scalar")
        ldq.wait_all("sync")
        gpq.wait_all("gpsimd")
        p.barrier(bar)
        p.emit()
        for cm in reversed(alloc):
            cm.__exit__(None, None, None)
        for s in reversed(p.sems):
            s._cm.__exit__(None, None, None)
        return nc, dbg

    w_hi = sbuf("w_hi", [128, 8 * NW], BF16)
    wbeta_s = sbuf("wbeta_s", [128, 8 * 32], BF16)
    w_hiF = w_hi[:].bitcast(F32)
    wkl = [sbuf(f"wkl{i}", [128, 1024], BF16) for i in range(2)]
    ident_s = sbuf("ident_s", [128, 128])
    identb = sbuf("identb", [128, 128], BF16)
    consts_s = sbuf("consts_s", [128, 16])
    for _i, _v in enumerate(CONSTVALS):
        nc.const_aps.register(F32, _v, consts_s[:, _i:_i + 1])
    ones_s = sbuf("ones_s", [128, 128])
    mlow_s = sbuf("mlow_s", [128, 128])
    mup0_s = sbuf("mup0_s", [128, 128])
    cw_s = sbuf("cw_s", [128, 96])
    w1s_s = sbuf("w1s_s", [16, D])
    b1_s = sbuf("b1_s", [128, 8])
    w2_s = sbuf("w2_s", [128, 32])
    b2_s = sbuf("b2_s", [4, 1])
    scal_s = sbuf("scal_s", [4, 1])
    sc_rep = sbuf("sc_rep", [128, 4])
    onorm_b = sbuf("onorm_b", [128, DH])
    rsc_c = sbuf("rsc_c", [128, 8])
    xtm_t = sbuf("xtm_t", [128, D])
    stat_c = sbuf("stat_c", [128, 16])
    scr = sbuf("scr", [128, D])
    l2row = sbuf("l2row", [1, 1024])
    mu_b = sbuf("mu_b", [128, TSB])
    rs_b = sbuf("rs_b", [128, TSB])
    xfm_t = sbuf("xfm_t", [128, 4 * TSB])
    t1 = sbuf("t1", [128, TSB])
    xnhi = sbuf("xnhi", [128, 8 * TSB], BF16)
    xnlo = sbuf("xnlo", [128, 8 * TSB], BF16)
    pout = sbuf("pout", [128, 16 * TSB])
    v_out = sbuf("v_out", [128, 8 * TSB])
    raw = sbuf("raw", [128, TSB + 4])
    hist = sbuf("hist", [128, 96])
    sq2 = sbuf("sq2", [128, 2 * TSB])
    beta_r = sbuf("beta_r", [4, TSB])
    beta_tm = sbuf("beta_tm", [128, 4 * NCPS])
    big3 = sbuf("big3", [128, 4096])
    a4 = sbuf("a4", [128, 4 * 128])
    at4 = sbuf("at4", [128, 4 * 128])
    tta = sbuf("tta", [128, 4 * 128])
    ttb = sbuf("ttb", [128, 4 * 128])
    attn4 = sbuf("attn4", [128, 4 * 128])
    kntm4 = sbuf("kntm4", [128, 4 * DH])
    vbtm4 = sbuf("vbtm4", [128, 4 * DH])
    kbtm4 = sbuf("kbtm4", [128, 4 * DH])
    utm4 = sbuf("utm4", [128, 4 * DH])
    wfm4 = sbuf("wfm4", [128, 4 * DH])
    uadj = sbuf("uadj", [128, DH])
    doutb = uadj
    ssb = sq2[:, 0:512]
    lgs = beta_r[:, 0:TSB]
    lgtm = sbuf("lgtm", [128, 64])
    fwc = sbuf("fwc", [128, 16])
    rms_c = sbuf("rms_c", [128, 16])
    mx4 = sbuf("mx4", [128, 8])
    whh = wkl
    ttoep = [v_out[:, 0:128], v_out[:, 128:256]]
    rbuf = pout[:, 0:8192]
    scat = w_hiF[:, 0:8192]

    ps_s = [psumt(f"ps_s{h}", [128, 512]) for h in range(H)]
    ps_w = [psumt(f"ps_w{i}", [128, 512]) for i in range(4)]
    ps_all = ps_s + ps_w

    ld(consts_s[:], cvec[:])
    ld(ident_s[:], ident[:])
    ldq.wait_all("vector")
    op("vector", "tensor_copy", out=identb[:], in_=ident_s[:])
    ld(ones_s[:], ones[:])
    ld(mlow_s[:], mask_low[:])
    ld(mup0_s[:], mask_up0[:])
    ld(cw_s[:].rearrange("b (a c) -> b a c", a=24),
       convw.rearrange("(a b) c -> b a c", b=128))
    ld(w1s_s[:], w1s[:])
    ld(b1_s[:], b1[:])
    ld(w2_s[:].rearrange("b (a c) -> b a c", a=8),
       w2.rearrange("(a b) c -> b a c", b=128))
    ld(b2_s[:], b2[:, None])
    ld(scal_s[:], scal[:, None])
    ldq.wait_all("scalar")
    ldq.wait_all("vector")
    ldq.wait_all("gpsimd")
    ldq.wait_all("tensor")

    for k in range(8):
        ld(big3[:, 0:NW] if NW <= 4096 else None, wqkvb[ts(k, 128), :])
        ldq.wait_all("vector")
        op("vector", "tensor_copy", out=w_hi[:, k * NW:(k + 1) * NW],
           in_=big3[:, 0:NW])
        op("vector", "tensor_tensor", out=big3[:, 0:NW], in0=big3[:, 0:NW],
           in1=w_hi[:, k * NW:(k + 1) * NW], op=ALU.subtract)
        op("vector", "tensor_copy", out=xnhi[:, 0:NW], in_=big3[:, 0:NW],
           inc=s_ve)
        p.wait("scalar", s_ve)
        for oc in range(24):
            st(wlo_slab[oc, k], xnhi[:, ts(oc, 128)])
        stq.wait_all("sync")
        stq.wait_all("vector")
    ld(scr[:, 0:256].rearrange("p (k c) -> p k c", k=8),
       wbeta.rearrange("(k p) c -> p k c", p=128))
    ldq.wait_all("vector")
    op("vector", "tensor_copy", out=wbeta_s[:], in_=scr[:, 0:256])
    op("vector", "tensor_tensor", out=scr[:, 0:256], in0=scr[:, 0:256],
       in1=wbeta_s[:], op=ALU.subtract)
    op("vector", "tensor_copy", out=xnhi[:, 0:256], in_=scr[:, 0:256],
       inc=s_ve)
    p.wait("scalar", s_ve)
    for k in range(8):
        st(wlo_slab[24, k][:, 0:32], xnhi[:, k * 32:(k + 1) * 32])
    stq.wait_all("sync")
    stq.wait_all("vector")
    for src, dhi, dlo in ((w1h, w1h_hi_d, w1h_lo_d), (wo, wo_hi_d, wo_lo_d)):
        for k in range(8):
            ld(big3[:, 0:D], src[ts(k, 128), :])
            ldq.wait_all("vector")
            op("vector", "tensor_copy", out=xnhi[:, 0:D], in_=big3[:, 0:D])
            op("vector", "tensor_tensor", out=big3[:, 0:D],
               in0=big3[:, 0:D], in1=xnhi[:, 0:D], op=ALU.subtract)
            op("vector", "tensor_copy", out=xnlo[:, 0:D], in_=big3[:, 0:D],
               inc=s_ve)
            p.wait("scalar", s_ve)
            st(dhi[k], xnhi[:, 0:D])
            st(dlo[k], xnlo[:, 0:D])
            stq.wait_all("sync")
            stq.wait_all("vector")

    ld(scr[0:1, 0:DH], o_norm[None, :])
    ldq.wait_all("tensor")
    op("tensor", "matmul", ps_w[0][:, 0:DH], ones_s[0:1, :], scr[0:1, 0:DH],
       start=True, stop=True)
    op("tensor", "transpose", ps_w[1][0:1, 0:4], scal_s[:],
       ident_s[0:4, 0:4], inc=s_mm)
    p.wait("scalar", s_mm)
    op("scalar", "copy", out=onorm_b[:], in_=ps_w[0][:, 0:DH])
    op("scalar", "copy", out=l2row[0:1, 0:4], in_=ps_w[1][0:1, 0:4],
       inc=s_ac)
    p.wait("tensor", s_ac)
    op("tensor", "matmul", ps_w[2][:, 0:4], ones_s[0:1, :], l2row[0:1, 0:4],
       start=True, stop=True, inc=s_mm)
    p.wait("scalar", s_mm)
    op("scalar", "copy", out=sc_rep[:], in_=ps_w[2][:, 0:4], inc=s_ac)

    ldq.wait_all("vector")
    p.wait("vector", s_mm)
    p.wait("vector", s_ac)
    op("vector", "memset", hist[:], 0.0)
    op("vector", "memset", scr[:, 0:D], 0.0, inc=s_ve)
    p.wait("scalar", s_ve)
    st(v_pad[0:64, :], scr[0:64, 0:D])

    ffw = pout
    ffa = v_out
    for k in range(8):
        ld(ffw[:, ts(k, D)], ffn_rhs[ts(k, 128), :])
    ldq.wait_all("vector")
    for j in range(4):
        op("vector", "tensor_tensor", out=ffa[:, ts(j, D)],
           in0=ffw[:, ts(2 * j, D)], in1=ffw[:, ts(2 * j + 1, D)], op=ALU.max)
        op("vector", "scalar_tensor_tensor", out=ffa[:, ts(j, D)],
           in0=ffw[:, ts(2 * j, D)], scalar=-1.0, in1=ffa[:, ts(j, D)],
           op0=ALU.mult, op1=ALU.max)
        op("vector", "scalar_tensor_tensor", out=ffa[:, ts(j, D)],
           in0=ffw[:, ts(2 * j + 1, D)], scalar=-1.0, in1=ffa[:, ts(j, D)],
           op0=ALU.mult, op1=ALU.max)
    for j in range(2):
        op("vector", "tensor_tensor", out=ffa[:, ts(j, D)],
           in0=ffa[:, ts(2 * j, D)], in1=ffa[:, ts(2 * j + 1, D)], op=ALU.max)
    op("vector", "tensor_tensor", out=ffa[:, 0:D], in0=ffa[:, 0:D],
       in1=ffa[:, ts(1, D)], op=ALU.max, inc=s_ve)
    for j in range(8):
        p.wait("tensor", s_ve)
        p.wait("tensor", s_ac)
        op("tensor", "transpose", ps_w[0][:, 0:128], ffa[:, ts(j, 128)],
           ident_s[:], inc=s_mm)
        p.wait("vector", s_mm)
        op("vector", "reduce_max", out=rsc_c[:, j:j + 1],
           in_=ps_w[0][:, 0:128], axis=AX.X, inc=s_ve)
    op("vector", "tensor_scalar", out=rsc_c[:, 0:8], in0=rsc_c[:, 0:8],
       scalar1=1e-12, scalar2=1.0 / 127.0, op0=ALU.max, op1=ALU.mult,
       inc=s_ve)
    p.wait("scalar", s_ve)
    op("scalar", "activation", out=stat_c[:, 0:8], in_=rsc_c[:, 0:8],
       func=AF.Ln)
    op("scalar", "activation", out=stat_c[:, 0:8], in_=stat_c[:, 0:8],
       func=AF.Exp, scale=-1.0, inc=s_ac)
    p.wait("tensor", s_ac)
    for j in range(8):
        op("tensor", "transpose", ps_w[j // 4][0:1, ts(j % 4, 128)],
           stat_c[:, j:j + 1], ident_s[:], inc=(s_mm if j == 7 else None))
    p.wait("scalar", s_mm)
    op("scalar", "copy", out=l2row[:, 0:512], in_=ps_w[0][0:1, 0:512])
    op("scalar", "copy", out=l2row[:, 512:1024], in_=ps_w[1][0:1, 0:512],
       inc=s_ac)
    for j in range(8):
        p.wait("tensor", s_ac)
        p.wait("tensor", s_ve)
        op("tensor", "matmul", ps_w[1][:, 0:128], ones_s[0:1, :],
           l2row[0:1, ts(j, 128)], start=True, stop=True, inc=s_mm)
        p.wait("vector", s_mm)
        op("vector", "tensor_copy", out=sq2[:, ts(j, 128)],
           in_=ps_w[1][:, 0:128], inc=s_ve)
    srep = sq2
    p.wait("vector", s_ac)

    def round_halfup(engine, dst_bf16, src_f32, tmp, tmp2):
        op(engine, "tensor_scalar", out=tmp, in0=src_f32, scalar1=128.5,
           scalar2=None, op0=ALU.add)
        for c in (256.0, 128.0, 64.0, 32.0, 16.0, 8.0, 4.0, 2.0, 1.0):
            op(engine, "tensor_scalar", out=tmp2, in0=tmp, scalar1=c,
               scalar2=c, op0=ALU.is_ge, op1=ALU.mult)
            op(engine, "tensor_tensor", out=tmp, in0=tmp, in1=tmp2,
               op=ALU.subtract)
        op(engine, "scalar_tensor_tensor", out=tmp2, in0=src_f32, scalar=0.5,
           in1=tmp, op0=ALU.add, op1=ALU.subtract)
        op(engine, "tensor_copy", out=dst_bf16, in_=tmp2)

    for k in range(8):
        op("vector", "tensor_tensor", out=ffw[:, ts(k, D)],
           in0=ffw[:, ts(k, D)], in1=srep[:, 0:D], op=ALU.mult)
        round_halfup("vector", xnhi[:, 0:D], ffw[:, ts(k, D)],
                     big3[:, 0:D], big3[:, D:2 * D])
        op("vector", "engine_nop", inc=s_ve)
        p.wait("scalar", s_ve)
        st(qrhs_d[k], xnhi[:, 0:D])
        stq.wait_all("vector")
    p.barrier(bar)
    if stop_after == "PH0":
        return _finish()

    def ln_and_xn(sbi):
        t0 = sbi * TSB
        for j in range(4):
            ld(xtm_t[:], x_tm[t0 + j * 128:t0 + (j + 1) * 128, :])
            ldq.wait_all("vector")
            ldq.wait_all("scalar")
            op("vector", "tensor_scalar", out=scr[:, 0:D], in0=xtm_t[:],
               scalar1=1.0 / D, scalar2=None, op0=ALU.mult, op1=ALU.add,
               accum_out=stat_c[:, j:j + 1], inc=s_ve)
            op("scalar", "activation", out=sq2[:, 0:D], in_=xtm_t[:],
               func=AF.Square, accum_out=stat_c[:, 4 + j:5 + j], inc=s_ac)
            p.wait("sync", s_ve)
            p.wait("sync", s_ac)
        p.wait("vector", s_ac)
        op("vector", "tensor_tensor", out=stat_c[:, 8:12],
           in0=stat_c[:, 0:4], in1=stat_c[:, 0:4], op=ALU.mult)
        op("vector", "tensor_scalar", out=stat_c[:, 4:8], in0=stat_c[:, 4:8],
           scalar1=1.0 / D, scalar2=None, op0=ALU.mult)
        op("vector", "tensor_tensor", out=stat_c[:, 4:8], in0=stat_c[:, 4:8],
           in1=stat_c[:, 8:12], op=ALU.subtract, inc=s_ve)
        p.wait("scalar", s_ve)
        op("scalar", "activation", out=stat_c[:, 4:8], in_=stat_c[:, 4:8],
           func=AF.Ln, bias=EPS_LN)
        op("scalar", "activation", out=stat_c[:, 4:8], in_=stat_c[:, 4:8],
           func=AF.Exp, scale=-0.5, inc=s_ac)
        p.wait("tensor", s_ac)
        for j in range(8):
            op("tensor", "transpose", ps_w[j // 4][0:1, ts(j % 4, 128)],
               stat_c[:, j:j + 1], ident_s[:],
               inc=(s_mm if j == 7 else None))
        p.wait("scalar", s_mm)
        op("scalar", "copy", out=l2row[:, 0:512], in_=ps_w[0][0:1, 0:512])
        op("scalar", "copy", out=l2row[:, 512:1024],
           in_=ps_w[1][0:1, 0:512], inc=s_ac)
        p.wait("tensor", s_ac)
        for j in range(4):
            op("tensor", "matmul", ps_w[1][:, 0:128], ones_s[0:1, :],
               l2row[0:1, ts(j, 128)], start=True, stop=True)
            op("tensor", "matmul", ps_w[2][:, 0:128], ones_s[0:1, :],
               l2row[0:1, ts(4 + j, 128)], start=True, stop=True, inc=s_mm)
            p.wait("scalar", s_mm)
            op("scalar", "copy", out=mu_b[:, ts(j, 128)],
               in_=ps_w[1][:, 0:128])
            op("scalar", "copy", out=rs_b[:, ts(j, 128)],
               in_=ps_w[2][:, 0:128], inc=s_ac)
            p.wait("tensor", s_ac)
        x3 = x_fm.rearrange("(a b) c -> b a c", b=128)
        for hf in range(2):
            ld(xfm_t[:].rearrange("p (a c) -> p a c", a=4),
               x3[:, hf * 4:(hf + 1) * 4, t0:t0 + TSB])
            ldq.wait_all("vector")
            p.wait("vector", s_ac)
            for kk in range(4):
                k = hf * 4 + kk
                op("vector", "tensor_tensor", out=t1[:],
                   in0=xfm_t[:, ts(kk, TSB)], in1=mu_b[:], op=ALU.subtract)
                op("vector", "tensor_tensor", out=t1[:], in0=t1[:],
                   in1=rs_b[:], op=ALU.mult, inc=s_ve)
                p.wait("scalar", s_ve)
                op("scalar", "activation", out=xnhi[:, ts(k, TSB)],
                   in_=t1[:], func=AF.Copy, inc=s_ac)
                p.wait("vector", s_ac)
                op("vector", "tensor_tensor", out=xnlo[:, ts(k, TSB)],
                   in0=t1[:], in1=xnhi[:, ts(k, TSB)], op=ALU.subtract,
                   inc=s_ve)
            p.wait("sync", s_ve)
        if t0 >= TAIL:
            p.wait("scalar", s_ve)
            for k in range(8):
                st(xn_d[0, k][:, t0 - TAIL:t0 - TAIL + TSB],
                   xnhi[:, ts(k, TSB)])
                st(xn_d[1, k][:, t0 - TAIL:t0 - TAIL + TSB],
                   xnlo[:, ts(k, TSB)])

    def proj_conv(sbi, oc_limit=25, do_tail=True, tail_lim=99,
                  bank_override=None):
        mm_marks = []
        for oc in range(oc_limit):
            bank = bank_override or ps_w[oc % 2]
            ncols = 128 if oc < 24 else 32
            buf = wkl[oc % 2]
            if len(mm_marks) >= 2:
                p.wait("sync", s_mm, mm_marks[-2])
            if ncols == 128:
                ld(buf[:, 0:8 * 128].rearrange("p (k c) -> p k c", k=8),
                   wlo_slab[oc].rearrange("k p c -> p k c"))
            else:
                ld(buf[:, 0:256].rearrange("p (k c) -> p k c", k=8),
                   wlo_slab[oc][:, :, 0:32].rearrange("k p c -> p k c"))
            ldq.wait_all("tensor")
            p.wait("tensor", s_ve)
            p.wait("tensor", s_ac)
            for k in range(8):
                hi = (w_hi[:, k * NW + oc * 128:k * NW + oc * 128 + 128]
                      if oc < 24 else wbeta_s[:, k * 32:(k + 1) * 32])
                op("tensor", "matmul", bank[0:ncols, 0:TSB], hi,
                   xnhi[:, ts(k, TSB)], start=(k == 0), stop=False)
                op("tensor", "matmul", bank[0:ncols, 0:TSB], hi,
                   xnlo[:, ts(k, TSB)], start=False, stop=False)
                op("tensor", "matmul", bank[0:ncols, 0:TSB],
                   buf[:, k * ncols:(k + 1) * ncols], xnhi[:, ts(k, TSB)],
                   start=False, stop=(k == 7), inc=(s_mm if k == 7 else None))
            mm_marks.append(s_mm.n)
            if not do_tail:
                continue
            if oc < 24:
                if tail_lim > 0 and tail_lim != 115:
                    p.wait("scalar", s_mm)
                    op("scalar", "copy", out=raw[:, 4:4 + TSB],
                       in_=bank[0:128, 0:TSB], inc=s_ac)
                dst = (pout[:, ts(oc, TSB)] if oc < 16
                       else v_out[:, ts(oc - 16, TSB)])
                if tail_lim == 101:
                    p.wait("vector", s_mm)
                    op("vector", "tensor_copy", out=dst,
                       in_=bank[0:128, 0:TSB], inc=s_ve)
                elif tail_lim == 102:
                    p.wait("vector", s_mm)
                    op("vector", "tensor_scalar", out=dst,
                       in0=xnhi[:, 0:TSB],
                       scalar1=cw_s[:, oc * 4 + 3:oc * 4 + 4], scalar2=None,
                       op0=ALU.mult, inc=s_ve)
                elif tail_lim == 103:
                    p.wait("vector", s_mm)
                    op("vector", "tensor_scalar", out=sq2[:, 0:TSB],
                       in0=bank[0:128, 0:TSB],
                       scalar1=cw_s[:, oc * 4 + 3:oc * 4 + 4], scalar2=None,
                       op0=ALU.mult, inc=s_ve)
                elif tail_lim == 104:
                    p.wait("vector", s_mm)
                elif tail_lim == 105:
                    op("vector", "tensor_copy", out=sq2[:, 0:TSB],
                       in_=xnhi[:, 0:TSB], inc=s_ve)
                elif tail_lim == 115:
                    p.wait("vector", s_mm)
                    op("vector", "tensor_scalar", out=dst,
                       in0=bank[0:128, 0:TSB],
                       scalar1=cw_s[:, oc * 4 + 3:oc * 4 + 4], scalar2=None,
                       op0=ALU.mult, inc=s_ve)
                elif tail_lim == 110:
                    p.wait("vector", s_ac)
                    op("vector", "tensor_scalar", out=sq2[:, 0:TSB],
                       in0=bank[0:128, 0:TSB],
                       scalar1=cw_s[:, oc * 4 + 3:oc * 4 + 4], scalar2=None,
                       op0=ALU.mult, inc=s_ve)
                elif tail_lim == 111:
                    p.wait("gpsimd", s_mm)
                    op("gpsimd", "tensor_scalar", out=sq2[:, 0:TSB],
                       in0=bank[0:128, 0:TSB],
                       scalar1=cw_s[:, oc * 4 + 3:oc * 4 + 4], scalar2=None,
                       op0=ALU.mult, inc=s_gp)
                elif tail_lim == 106:
                    p.wait("vector", s_mm)
                    op("vector", "tensor_copy", out=sq2[:, 0:128],
                       in_=bank[0:128, 0:128], inc=s_ve)
                elif tail_lim == 107:
                    p.wait("vector", s_mm)
                    for q4 in range(4):
                        op("vector", "tensor_copy",
                           out=sq2[:, ts(q4, 128)],
                           in_=bank[0:128, ts(q4, 128)],
                           inc=(s_ve if q4 == 3 else None))
                elif tail_lim > 1:
                    p.wait("vector", s_ac)
                    op("vector", "tensor_scalar", out=dst,
                       in0=raw[:, 4:4 + TSB],
                       scalar1=cw_s[:, oc * 4 + 3:oc * 4 + 4], scalar2=None,
                       op0=ALU.mult, inc=s_ve)
                if 2 < tail_lim <= 99:
                    op("vector", "tensor_copy", out=raw[:, 0:4],
                       in_=hist[:, oc * 4:oc * 4 + 4])
                for jt in range(3):
                    if 3 + jt < tail_lim <= 99:
                        op("vector", "scalar_tensor_tensor", out=dst,
                           in0=raw[:, 1 + jt:1 + jt + TSB],
                           scalar=cw_s[:, oc * 4 + jt:oc * 4 + jt + 1],
                           in1=dst, op0=ALU.mult, op1=ALU.add)
                if 6 < tail_lim <= 99:
                    op("vector", "tensor_copy",
                       out=hist[:, oc * 4:oc * 4 + 4],
                       in_=raw[:, TSB:TSB + 4], inc=s_ve)
                if 7 < tail_lim <= 99:
                    p.wait("scalar", s_ve)
                    op("scalar", "activation", out=dst, in_=dst, func=AF.Silu,
                       inc=s_ac)
            else:
                p.wait("scalar", s_mm)
                op("scalar", "activation", out=beta_r[:],
                   in_=bank[0:4, 0:TSB], func=AF.Sigmoid, inc=s_ac)
        p.wait("vector", s_ac)
        op("vector", "engine_nop", inc=s_ve)

    def l2norm(sbi):
        for tnsr in range(2):
            for h in range(H):
                oc0 = tnsr * 8 + 2 * h
                bank = ps_w[(tnsr * H + h) % 2]
                p.wait("scalar", s_ve)
                op("scalar", "activation", out=sq2[:, 0:TSB],
                   in_=pout[:, ts(oc0, TSB)], func=AF.Square)
                op("scalar", "activation", out=sq2[:, TSB:2 * TSB],
                   in_=pout[:, ts(oc0 + 1, TSB)], func=AF.Square, inc=s_ac)
                p.wait("tensor", s_ac)
                p.wait("tensor", s_ve)
                op("tensor", "matmul", bank[:, 0:TSB], ones_s[:],
                   sq2[:, 0:TSB], start=True, stop=False)
                op("tensor", "matmul", bank[:, 0:TSB], ones_s[:],
                   sq2[:, TSB:2 * TSB], start=False, stop=True, inc=s_mm)
                p.wait("scalar", s_mm)
                op("scalar", "activation", out=t1[:], in_=bank[:, 0:TSB],
                   func=AF.Ln, bias=EPS_L2)
                op("scalar", "activation", out=t1[:], in_=t1[:], func=AF.Exp,
                   scale=-0.5, inc=s_ac)
                p.wait("vector", s_ac)
                op("vector", "tensor_tensor", out=pout[:, ts(oc0, TSB)],
                   in0=pout[:, ts(oc0, TSB)], in1=t1[:], op=ALU.mult)
                op("vector", "tensor_tensor", out=pout[:, ts(oc0 + 1, TSB)],
                   in0=pout[:, ts(oc0 + 1, TSB)], in1=t1[:], op=ALU.mult,
                   inc=s_ve)

    def delta(sbi):
        t0 = sbi * TSB
        p.wait("tensor", s_ve)
        p.wait("tensor", s_ac)
        for ci in range(NCPS):
            op("tensor", "transpose", ps_w[3][:, ts(ci, 4)],
               beta_r[:, ts(ci, 128)], ident_s[0:4, 0:4],
               inc=(s_mm if ci == NCPS - 1 else None))
        p.wait("scalar", s_mm)
        op("scalar", "copy", out=beta_tm[:], in_=ps_w[3][:, 0:16], inc=s_ac)

        for h in range(H):
            gc0 = sbi * NCPS
            tail0 = gc0 * CK >= TAIL

            def qsl(c, ci):
                base = (2 * h + c) * TSB + ci * 128
                return pout[:, base:base + 128]

            def ksl(c, ci):
                base = (8 + 2 * h + c) * TSB + ci * 128
                return pout[:, base:base + 128]

            def vsl(c, ci):
                base = (2 * h + c) * TSB + ci * 128
                return v_out[:, base:base + 128]

            p.wait("tensor", s_ac)
            p.wait("tensor", s_ve)
            for ci in range(NCPS):
                op("tensor", "transpose", ps_w[ci][:, 0:128], ksl(0, ci),
                   ident_s[:])
                op("tensor", "transpose", ps_w[ci][:, 128:256], ksl(1, ci),
                   ident_s[:])
                op("tensor", "transpose", ps_w[ci][:, 256:384], vsl(0, ci),
                   ident_s[:])
                op("tensor", "transpose", ps_w[ci][:, 384:512], vsl(1, ci),
                   ident_s[:], inc=(s_mm if ci == NCPS - 1 else None))
            p.wait("scalar", s_mm)
            p.wait("vector", s_mm)
            for ci in range(NCPS):
                eng, cp, sm = (("scalar", "copy", s_ac) if ci < 2
                               else ("vector", "tensor_copy", s_ve))
                op(eng, cp, out=kntm4[:, ts(ci, DH)],
                   in_=ps_w[ci][:, 0:256])
                op(eng, cp, out=vbtm4[:, ts(ci, DH)],
                   in_=ps_w[ci][:, 256:512],
                   inc=(sm if ci % 2 == 1 else None))
            p.wait("scalar", s_ve)
            for ci in range(NCPS):
                st(v_pad[64 + t0 + ci * 128:64 + t0 + (ci + 1) * 128,
                         h * DH:(h + 1) * DH], vbtm4[:, ts(ci, DH)])
            stq.wait_all("vector")
            p.wait("vector", s_ac)
            for ci in range(NCPS):
                bcol = beta_tm[:, ci * 4 + h:ci * 4 + h + 1]
                op("vector", "tensor_scalar", out=kbtm4[:, ts(ci, DH)],
                   in0=kntm4[:, ts(ci, DH)], scalar1=bcol, scalar2=None,
                   op0=ALU.mult)
                op("vector", "tensor_scalar", out=vbtm4[:, ts(ci, DH)],
                   in0=vbtm4[:, ts(ci, DH)], scalar1=bcol, scalar2=None,
                   op0=ALU.mult)
            op("vector", "engine_nop", inc=s_ve)
            p.wait("tensor", s_ve)
            for ci in range(NCPS):
                last = (ci == NCPS - 1)
                op("tensor", "matmul", ps_w[ci][:, 0:128], ksl(0, ci),
                   ksl(0, ci), start=True, stop=False)
                op("tensor", "matmul", ps_w[ci][:, 0:128], ksl(1, ci),
                   ksl(1, ci), start=False, stop=True,
                   inc=(s_mm if last and not tail0 else None))
                if tail0:
                    op("tensor", "matmul", ps_w[ci][:, 128:256], ksl(0, ci),
                       qsl(0, ci), start=True, stop=False)
                    op("tensor", "matmul", ps_w[ci][:, 128:256], ksl(1, ci),
                       qsl(1, ci), start=False, stop=True,
                       inc=(s_mm if last else None))
            p.wait("vector", s_mm)
            for ci in range(NCPS):
                bcol = beta_tm[:, ci * 4 + h:ci * 4 + h + 1]
                op("vector", "tensor_scalar", out=a4[:, ts(ci, 128)],
                   in0=ps_w[ci][:, 0:128], scalar1=bcol, scalar2=None,
                   op0=ALU.mult)
                op("vector", "tensor_tensor", out=a4[:, ts(ci, 128)],
                   in0=a4[:, ts(ci, 128)], in1=mlow_s[:], op=ALU.mult)
                if tail0:
                    op("vector", "tensor_tensor", out=attn4[:, ts(ci, 128)],
                       in0=ps_w[ci][:, 128:256], in1=mup0_s[:], op=ALU.mult)
            op("vector", "engine_nop", inc=s_ve)
            p.wait("tensor", s_ve)
            for ci in range(NCPS):
                op("tensor", "transpose", ps_w[ci][:, 256:384],
                   a4[:, ts(ci, 128)], ident_s[:],
                   inc=(s_mm if ci == NCPS - 1 else None))
            p.wait("scalar", s_mm)
            for ci in range(NCPS):
                op("scalar", "copy", out=at4[:, ts(ci, 128)],
                   in_=ps_w[ci][:, 256:384],
                   inc=(s_ac if ci == NCPS - 1 else None))
            p.wait("vector", s_ac)
            for ci in range(NCPS):
                op("vector", "tensor_tensor", out=tta[:, ts(ci, 128)],
                   in0=at4[:, ts(ci, 128)], in1=ident_s[:], op=ALU.add)
            op("vector", "engine_nop", inc=s_ve)
            tt_cur, tt_nxt = tta, ttb
            for j in range(1, 7):
                p.wait("tensor", s_ve)
                p.wait("tensor", s_ac)
                for ci in range(NCPS):
                    last = (ci == NCPS - 1)
                    op("tensor", "matmul", ps_w[ci][:, 0:128],
                       at4[:, ts(ci, 128)], a4[:, ts(ci, 128)],
                       start=True, stop=True,
                       inc=(s_mm if last and j == 6 else None))
                    if j < 6:
                        op("tensor", "matmul", ps_w[ci][:, 128:256],
                           a4[:, ts(ci, 128)], at4[:, ts(ci, 128)],
                           start=True, stop=True,
                           inc=(s_mm if last else None))
                p.wait("scalar", s_mm)
                p.wait("vector", s_mm)
                for ci in range(NCPS):
                    last = (ci == NCPS - 1)
                    eng, cp, sm = (("scalar", "copy", s_ac) if ci < 2
                                   else ("vector", "tensor_copy", s_ve))
                    op(eng, cp, out=a4[:, ts(ci, 128)],
                       in_=ps_w[ci][:, 0:128],
                       inc=(sm if ci % 2 == 1 and j == 6 else None))
                    if j < 6:
                        op(eng, cp, out=at4[:, ts(ci, 128)],
                           in_=ps_w[ci][:, 128:256],
                           inc=(sm if ci % 2 == 1 else None))
                p.wait("tensor", s_ac)
                p.wait("tensor", s_ve)
                for ci in range(NCPS):
                    op("tensor", "matmul", ps_w[ci][:, 256:384],
                       a4[:, ts(ci, 128)], tt_cur[:, ts(ci, 128)],
                       start=True, stop=True,
                       inc=(s_mm if ci == NCPS - 1 else None))
                p.wait("vector", s_mm)
                for ci in range(NCPS):
                    op("vector", "scalar_tensor_tensor",
                       out=tt_nxt[:, ts(ci, 128)], in0=ps_w[ci][:, 256:384],
                       scalar=1.0, in1=tt_cur[:, ts(ci, 128)],
                       op0=ALU.mult, op1=ALU.add)
                op("vector", "engine_nop", inc=s_ve)
                tt_cur, tt_nxt = tt_nxt, tt_cur
            p.wait("tensor", s_ve)
            for ci in range(NCPS):
                op("tensor", "matmul", ps_w[ci][:, 0:256],
                   tt_cur[:, ts(ci, 128)], vbtm4[:, ts(ci, DH)],
                   start=True, stop=True)
                op("tensor", "matmul", ps_w[ci][:, 256:384],
                   kbtm4[:, ci * DH:ci * DH + 128], tt_cur[:, ts(ci, 128)],
                   start=True, stop=True)
                op("tensor", "matmul", ps_w[ci][:, 384:512],
                   kbtm4[:, ci * DH + 128:ci * DH + 256],
                   tt_cur[:, ts(ci, 128)], start=True, stop=True,
                   inc=(s_mm if ci == NCPS - 1 else None))
            p.wait("scalar", s_mm)
            p.wait("vector", s_mm)
            for ci in range(NCPS):
                eng, cp, sm = (("scalar", "copy", s_ac) if ci < 2
                               else ("vector", "tensor_copy", s_ve))
                op(eng, cp, out=utm4[:, ts(ci, DH)],
                   in_=ps_w[ci][:, 0:256])
                op(eng, cp, out=wfm4[:, ts(ci, DH)],
                   in_=ps_w[ci][:, 256:512],
                   inc=(sm if ci % 2 == 1 else None))
            for ci in range(NCPS):
                gc = gc0 + ci
                tail = gc * CK >= TAIL
                p.wait("tensor", s_ac)
                p.wait("tensor", s_ve)
                if gc > 0:
                    p.wait("scalar", s_mm)
                    p.wait("scalar", s_ve)
                    p.wait("vector", s_mm)
                    p.wait("vector", s_ac)
                    op("scalar", "copy", out=ssb[:, 0:512],
                       in_=ps_s[h][:, 0:512], inc=s_ac)
                    op("vector", "engine_nop", inc=s_ve)
                    p.wait("tensor", s_ac)
                    p.wait("tensor", s_ve)
                    op("tensor", "matmul", ps_w[0][:, 0:256],
                       wfm4[:, ci * DH:ci * DH + 128], ssb[:, 0:256],
                       start=True, stop=False, skip_group_check=True)
                    op("tensor", "matmul", ps_w[0][:, 0:256],
                       wfm4[:, ci * DH + 128:ci * DH + 256],
                       ssb[:, 256:512], start=False, stop=True,
                       inc=(None if tail else s_mm), skip_group_check=True)
                    if tail:
                        op("tensor", "matmul", ps_w[0][:, 256:512],
                           qsl(0, ci), ssb[:, 0:256], start=True, stop=False,
                           skip_group_check=True)
                        op("tensor", "matmul", ps_w[0][:, 256:512],
                           qsl(1, ci), ssb[:, 256:512], start=False,
                           stop=False, inc=s_mm, skip_group_check=True)
                    p.wait("vector", s_mm)
                    stq.wait_all("vector")
                    op("vector", "tensor_tensor", out=uadj[:],
                       in0=utm4[:, ts(ci, DH)], in1=ps_w[0][:, 0:256],
                       op=ALU.subtract, inc=s_ve)
                    ua = uadj[:]
                    p.wait("tensor", s_ve)
                else:
                    ua = utm4[:, ts(ci, DH)]
                if tail:
                    op("tensor", "matmul", ps_w[0][:, 256:512],
                       attn4[:, ts(ci, 128)], ua, start=False, stop=True,
                       skip_group_check=True)
                op("tensor", "matmul", ps_s[h][:, 0:256],
                   kntm4[:, ci * DH:ci * DH + 128], ua,
                   start=(gc == 0), stop=False, skip_group_check=True)
                op("tensor", "matmul", ps_s[h][:, 256:512],
                   kntm4[:, ci * DH + 128:ci * DH + 256], ua,
                   start=False, stop=(gc == NCK - 1), inc=s_mm,
                   skip_group_check=True)
                if tail:
                    p.wait("vector", s_mm)
                    op("vector", "tensor_copy", out=doutb[:],
                       in_=ps_w[0][:, 256:512], inc=s_ve)
                    p.wait("scalar", s_ve)
                    trow = gc * CK - TAIL
                    st(delta_d[trow:trow + 128, h * DH:(h + 1) * DH],
                       doutb[:])
                    p.wait("tensor", s_ve)

    for sbi in range(NSB):
        ln_and_xn(sbi)
        p.barrier(bar)
        if stop_after == f"PH1:ln:{sbi}":
            return _finish()
        if sbi == 0 and stop_after and stop_after.startswith("PH1:pj"):
            kind, n = stop_after.split(":")[1], int(stop_after.split(":")[2])
            if kind == "pjtl" and n == 108:
                p.wait("vector", s_mm)
                op("vector", "tensor_copy", out=sq2[:, 0:128],
                   in_=ps_w[0][0:128, 0:128], inc=s_ve)
            elif kind == "pjtl" and n in (112, 114):
                p.wait("tensor", s_ve)
                p.wait("tensor", s_ac)
                bank = ps_w[0]
                for k in range(8):
                    for r in range(3):
                        if n == 112:
                            lh = w_hi[:, k * 3072:k * 3072 + 128]
                        else:
                            lh = xnhi[:, ts(k, TSB)][:, 0:128]
                        rhs = xnlo[:, ts(k, TSB)] if r == 1 \
                            else xnhi[:, ts(k, TSB)]
                        op("tensor", "matmul", bank[0:128, 0:TSB], lh, rhs,
                           start=(k == 0 and r == 0),
                           stop=(k == 7 and r == 2),
                           inc=(s_mm if (k == 7 and r == 2) else None))
                p.wait("vector", s_mm)
                op("vector", "tensor_copy", out=sq2[:, 0:TSB],
                   in_=bank[0:128, 0:TSB], inc=s_ve)
            elif kind == "pjtl" and n == 109:
                proj_conv(0, oc_limit=1, do_tail=True, tail_lim=103,
                          bank_override=ps_s[0])
            elif kind == "pjtl":
                proj_conv(0, oc_limit=1, do_tail=True, tail_lim=n)
            else:
                proj_conv(0, oc_limit=n + 1, do_tail=(kind == "pjoc"))
            p.barrier(bar)
            return _finish()
        proj_conv(sbi)
        p.barrier(bar)
        if stop_after == f"PH1:proj:{sbi}":
            return _finish()
        l2norm(sbi)
        p.barrier(bar)
        if stop_after == f"PH1:l2:{sbi}":
            return _finish()
        delta(sbi)
        p.barrier(bar)
        if stop_after == f"PH1:sb:{sbi}":
            return _finish()
        if "d_qn" in dbg:
            for k in range(16):
                st(dbg["d_qn"][sbi][:, ts(k, TSB)], pout[:, ts(k, TSB)])
        if "d_v" in dbg:
            for k in range(8):
                st(dbg["d_v"][sbi][:, ts(k, TSB)], v_out[:, ts(k, TSB)])
        if "d_beta" in dbg:
            st(dbg["d_beta"][sbi], beta_r[:])
        if dbg:
            stq.wait_all("scalar")
            p.barrier(bar)

    if "d_delta" in dbg:
        for r in range(8):
            ld(pout[:, 0:4096].rearrange("p (a c) -> p a c", a=4),
               delta_d[r * 512:(r + 1) * 512, :].rearrange(
                   "(a b) c -> b a c", b=128))
            ldq.wait_all("scalar")
            for j in range(4):
                st(dbg["d_delta"][r * 512 + j * 128:
                                  r * 512 + (j + 1) * 128, :],
                   pout[:, ts(j, D)])
            stq.wait_all("sync")
        p.barrier(bar)
    if stop_after == "PH1":
        return _finish()

    vp3 = v_pad.rearrange("(u p) c -> p u c", p=64)
    mm_mark = [0]
    for cs in range(8):
        p.wait("gpsimd", s_mm)
        g2(rbuf[0:64, 0:8192].rearrange("p (u c) -> p u c", u=64),
           vp3[:, 64:128, cs * 128:(cs + 1) * 128])
        g2(rbuf[64:128, 0:8192].rearrange("p (u c) -> p u c", u=64),
           vp3[:, 65:129, cs * 128:(cs + 1) * 128])
        gpq.wait_all("tensor")
        for ch in range(128):
            gch = cs * 128 + ch
            tb = ttoep[ch % 2]
            if mm_mark[0] > 0:
                p.wait("sync", s_mm, mm_mark[0])
            ld(tb[:], fir_toep[gch])
            mm_mark[0] = s_mm.n
            ldq.wait_all("tensor")
            rap = rbuf.rearrange("p (u c) -> p c u", u=64)[:, ch, :]
            op("tensor", "matmul", ps_w[ch % 4][:, 0:64], tb[:], rap,
               start=True, stop=True, inc=s_mm)
            if ch % 2 == 0:
                p.wait("vector", s_mm)
                op("vector", "tensor_copy",
                   out=scat.rearrange("p (u c) -> p c u", u=64)[:, ch, :],
                   in_=ps_w[ch % 4][:, 0:64], inc=s_ve)
                p.wait("tensor", s_ve)
            else:
                p.wait("scalar", s_mm)
                op("scalar", "copy",
                   out=scat.rearrange("p (u c) -> p c u", u=64)[:, ch, :],
                   in_=ps_w[ch % 4][:, 0:64], inc=s_ac)
                p.wait("tensor", s_ac)
        p.wait("gpsimd", s_ve)
        p.wait("gpsimd", s_ac)
        ld3 = l_d.rearrange("(u o) c -> o u c", o=64)
        g2(ld3[:, :, cs * 128:(cs + 1) * 128],
           scat[0:64, 0:8192].rearrange("p (u c) -> p u c", u=64))
        g2(ld3[:, :, D + cs * 128:D + (cs + 1) * 128],
           scat[64:128, 0:8192].rearrange("p (u c) -> p u c", u=64))
        gpq.wait_all("gpsimd")
    p.barrier(bar)
    if "d_L" in dbg:
        for r in range(8):
            ld(pout[:, 0:8192].rearrange("p (a c) -> p a c", a=4),
               l_d[r * 512:(r + 1) * 512, :].rearrange(
                   "(a b) c -> b a c", b=128))
            ldq.wait_all("scalar")
            for j in range(4):
                st(dbg["d_L"][r * 512 + j * 128:r * 512 + (j + 1) * 128, :],
                   pout[:, j * 2048:(j + 1) * 2048])
            stq.wait_all("sync")
        p.barrier(bar)
    if stop_after == "PH2":
        return _finish()

    d_buf = w_hiF[:, 0:4096]
    v_buf = w_hiF[:, 4096:8192]
    hdn = w_hiF[:, 8192:12288]
    L_buf = pout
    base_s = v_out
    o_buf = big3
    stt_tm = scr
    stt_fm = xfm_t[0:16, 0:4 * TSB]

    def softmax_ocombine(sl):
        t0g = sl * TSB
        for bk in range(4):
            p.wait("vector", s_ac)
            op("vector", "tensor_copy", out=fwc[:].rearrange(
                "p (h j) -> p h j", h=4),
               in_=lgtm[:].rearrange("p (h b j) -> p b h j", h=4, b=4)[:, bk],
               inc=s_ve)
            op("vector", "tensor_reduce", out=mx4[:, 0:4],
               in_=fwc[:].rearrange("p (h j) -> p h j", h=4), axis=AX.X,
               op=ALU.max)
            for h in range(H):
                op("vector", "tensor_scalar", out=fwc[:, ts(h, 4)],
                   in0=fwc[:, ts(h, 4)], scalar1=mx4[:, h:h + 1],
                   scalar2=None, op0=ALU.subtract)
            op("vector", "engine_nop", inc=s_ve)
            p.wait("scalar", s_ve)
            op("scalar", "activation", out=fwc[:], in_=fwc[:], func=AF.Exp,
               inc=s_ac)
            p.wait("vector", s_ac)
            op("vector", "tensor_reduce", out=mx4[:, 4:8],
               in_=fwc[:].rearrange("p (h j) -> p h j", h=4), axis=AX.X,
               op=ALU.add, inc=s_ve)
            p.wait("scalar", s_ve)
            op("scalar", "activation", out=mx4[:, 4:8], in_=mx4[:, 4:8],
               func=AF.Ln)
            op("scalar", "activation", out=mx4[:, 4:8], in_=mx4[:, 4:8],
               func=AF.Exp, scale=-1.0, inc=s_ac)
            p.wait("vector", s_ac)
            for h in range(H):
                op("vector", "tensor_scalar", out=fwc[:, ts(h, 4)],
                   in0=fwc[:, ts(h, 4)], scalar1=mx4[:, 4 + h:5 + h],
                   scalar2=None, op0=ALU.mult)
            for h in range(H):
                op("vector", "scalar_tensor_tensor",
                   out=fwc[:, h * 4:h * 4 + 1], in0=sc_rep[:, 1:2],
                   scalar=1.0, in1=fwc[:, h * 4:h * 4 + 1],
                   op0=ALU.mult, op1=ALU.add)
                op("vector", "scalar_tensor_tensor",
                   out=fwc[:, h * 4 + 1:h * 4 + 2], in0=sc_rep[:, 2:3],
                   scalar=1.0, in1=fwc[:, h * 4 + 1:h * 4 + 2],
                   op0=ALU.mult, op1=ALU.add)
            op("vector", "engine_nop", inc=s_ve)
            if "d_fw" in dbg:
                p.wait("scalar", s_ve)
                st(dbg["d_fw"][sl * 4 + bk], fwc[:])
                stq.wait_all("scalar")
                stq.wait_all("vector")
            j = bk
            for h in range(H):
                srt = L_buf[:, j * 2048 + 1024 + h * DH:
                            j * 2048 + 1024 + (h + 1) * DH]
                lng = L_buf[:, j * 2048 + h * DH:j * 2048 + (h + 1) * DH]
                dlt = d_buf[:, j * 1024 + h * DH:j * 1024 + (h + 1) * DH]
                vdr = v_buf[:, j * 1024 + h * DH:j * 1024 + (h + 1) * DH]
                odst = o_buf[:, j * 1024 + h * DH:j * 1024 + (h + 1) * DH]
                op("vector", "tensor_scalar", out=odst, in0=srt,
                   scalar1=fwc[:, h * 4:h * 4 + 1], scalar2=None,
                   op0=ALU.mult)
                op("vector", "scalar_tensor_tensor", out=odst, in0=lng,
                   scalar=fwc[:, h * 4 + 1:h * 4 + 2], in1=odst,
                   op0=ALU.mult, op1=ALU.add)
                op("vector", "scalar_tensor_tensor", out=odst, in0=dlt,
                   scalar=fwc[:, h * 4 + 2:h * 4 + 3], in1=odst,
                   op0=ALU.mult, op1=ALU.add)
                op("vector", "scalar_tensor_tensor", out=odst, in0=vdr,
                   scalar=fwc[:, h * 4 + 3:h * 4 + 4], in1=odst,
                   op0=ALU.mult, op1=ALU.add, inc=s_ve)
                p.wait("scalar", s_ve)
                op("scalar", "activation", out=sq2[:, 0:DH], in_=odst,
                   func=AF.Square,
                   accum_out=rms_c[:, j * 4 + h:j * 4 + h + 1], inc=s_ac)
        p.wait("vector", s_ac)
        op("vector", "tensor_scalar", out=rms_c[:], in0=rms_c[:],
           scalar1=1.0 / DH, scalar2=None, op0=ALU.mult, inc=s_ve)
        p.wait("scalar", s_ve)
        op("scalar", "activation", out=rms_c[:], in_=rms_c[:], func=AF.Ln,
           bias=EPS_LN)
        op("scalar", "activation", out=rms_c[:], in_=rms_c[:], func=AF.Exp,
           scale=-0.5, inc=s_ac)
        p.wait("vector", s_ac)
        for j in range(4):
            for h in range(H):
                odst = o_buf[:, j * 1024 + h * DH:j * 1024 + (h + 1) * DH]
                op("vector", "tensor_scalar", out=odst, in0=odst,
                   scalar1=rms_c[:, j * 4 + h:j * 4 + h + 1], scalar2=None,
                   op0=ALU.mult)
                op("vector", "tensor_tensor", out=odst, in0=odst,
                   in1=onorm_b[:], op=ALU.mult)
        op("vector", "engine_nop", inc=s_ve)
        if "d_o" in dbg:
            p.wait("scalar", s_ve)
            for j in range(4):
                st(dbg["d_o"][t0g + j * 128:t0g + (j + 1) * 128, :],
                   o_buf[:, ts(j, D)])
            stq.wait_all("scalar")
            stq.wait_all("vector")
        p.wait("scalar", s_ve)
        for j in range(4):
            op("scalar", "activation", out=xnhi[:, j * 1024:(j + 1) * 1024]
               .bitcast(BF16)[:, 0:1024] if False else
               xnhi[:, ts(2 * j, TSB)], in_=o_buf[:, j * 1024:j * 1024 + 512],
               func=AF.Copy)
            op("scalar", "activation", out=xnhi[:, ts(2 * j + 1, TSB)],
               in_=o_buf[:, j * 1024 + 512:(j + 1) * 1024], func=AF.Copy,
               inc=s_ac)
        p.wait("vector", s_ac)
        for j in range(4):
            op("vector", "tensor_tensor", out=xnlo[:, ts(2 * j, TSB)],
               in0=o_buf[:, j * 1024:j * 1024 + 512],
               in1=xnhi[:, ts(2 * j, TSB)], op=ALU.subtract)
            op("vector", "tensor_tensor", out=xnlo[:, ts(2 * j + 1, TSB)],
               in0=o_buf[:, j * 1024 + 512:(j + 1) * 1024],
               in1=xnhi[:, ts(2 * j + 1, TSB)], op=ALU.subtract,
               inc=(s_ve if j == 3 else None))
        p.wait("scalar", s_ve)
        for j in range(4):
            st(o_d[0, t0g + j * 128:t0g + (j + 1) * 128, :],
               xnhi[:, ts(2 * j, TSB)].bitcast(BF16)[:, 0:512]
               if False else xnhi[:, 2 * j * TSB:2 * j * TSB + 1024])
            st(o_d[1, t0g + j * 128:t0g + (j + 1) * 128, :],
               xnlo[:, 2 * j * TSB:2 * j * TSB + 1024])
        stq.wait_all("sync")
        stq.wait_all("vector")
        stq.wait_all("gpsimd")

    for sl in range(8):
        t0 = sl * TSB
        ld(d_buf.rearrange("p (a c) -> p a c", a=4),
           delta_d[t0:t0 + TSB, :].rearrange("(a b) c -> b a c", b=128))
        ld(v_buf.rearrange("p (a c) -> p a c", a=4),
           v_pad[64 + TAIL + t0:64 + TAIL + t0 + TSB, :].rearrange(
               "(a b) c -> b a c", b=128))
        ld(L_buf[:, 0:8192].rearrange("p (a c) -> p a c", a=4),
           l_d[t0:t0 + TSB, :].rearrange("(a b) c -> b a c", b=128))
        ld(xnhi[:].rearrange("p (a c) -> p a c", a=8),
           xn_d[0][:, :, t0:t0 + TSB].rearrange("a b c -> b a c"))
        ld(xnlo[:].rearrange("p (a c) -> p a c", a=8),
           xn_d[1][:, :, t0:t0 + TSB].rearrange("a b c -> b a c"))
        ldq.wait_all("vector")
        ldq.wait_all("scalar")
        ldq.wait_all("tensor")
        op("scalar", "activation", out=stat_c[0:1, 0:1],
           in_=scal_s[0:1, :], func=AF.Exp)
        op("scalar", "activation", out=stat_c[0:1, 0:1],
           in_=stat_c[0:1, 0:1], func=AF.Ln, bias=1.0)
        op("scalar", "activation", out=stat_c[0:1, 1:2],
           in_=stat_c[0:1, 0:1], func=AF.Ln, bias=1e-4, inc=s_ac)
        p.wait("tensor", s_ac)
        op("tensor", "matmul", ps_w[2][0:4, 0:1], ones_s[0:1, 0:4],
           stat_c[0:1, 1:2], start=True, stop=True, inc=s_mm)
        p.wait("scalar", s_mm)
        op("scalar", "activation", out=stat_c[0:4, 2:3],
           in_=ps_w[2][0:4, 0:1], func=AF.Exp, scale=-1.0, inc=s_ac)
        p.wait("vector", s_ac)
        op("vector", "tensor_tensor", out=stat_c[0:4, 3:4], in0=b2_s[:],
           in1=stat_c[0:4, 2:3], op=ALU.mult, inc=s_ve)
        p.wait("scalar", s_ve)
        for j in range(4):
            p.wait("scalar", s_mm)
            p.wait("vector", s_mm)
            for h in range(H):
                srcs = [L_buf[:, j * 2048 + 1024 + h * DH:
                              j * 2048 + 1024 + (h + 1) * DH],
                        L_buf[:, j * 2048 + h * DH:j * 2048 + (h + 1) * DH],
                        d_buf[:, j * 1024 + h * DH:j * 1024 + (h + 1) * DH],
                        v_buf[:, j * 1024 + h * DH:j * 1024 + (h + 1) * DH]]
                for si, src in enumerate(srcs):
                    c0 = h * 16 + si * 4
                    op("vector", "tensor_scalar", out=t1[:, 0:DH], in0=src,
                       scalar1=1.0 / DH, scalar2=None, op0=ALU.mult,
                       op1=ALU.add, accum_out=stt_tm[:, c0:c0 + 1])
                    op("scalar", "activation", out=sq2[:, 0:DH], in_=src,
                       func=AF.Square, accum_out=stt_tm[:, c0 + 1:c0 + 2])
                    op("scalar", "activation", out=sq2[:, DH:2 * DH],
                       in_=src, func=AF.Abs,
                       accum_out=stt_tm[:, c0 + 2:c0 + 3])
            op("vector", "engine_nop", inc=s_ve)
            p.wait("scalar", s_ve)
            p.wait("scalar", s_ac)
            sv = stt_tm[:, 0:64].rearrange("p (a c) -> p a c", a=16)
            op("scalar", "activation", out=sv[:, :, 3], in_=sv[:, :, 1],
               func=AF.Sqrt, inc=s_ac)
            p.wait("vector", s_ac)
            op("vector", "tensor_tensor", out=t1[:, 0:16], in0=sv[:, :, 0],
               in1=sv[:, :, 0], op=ALU.mult)
            op("vector", "tensor_scalar", out=sv[:, :, 1], in0=sv[:, :, 1],
               scalar1=1.0 / DH, scalar2=None, op0=ALU.mult)
            op("vector", "tensor_tensor", out=sv[:, :, 1], in0=sv[:, :, 1],
               in1=t1[:, 0:16], op=ALU.subtract)
            op("vector", "tensor_scalar", out=sv[:, :, 2], in0=sv[:, :, 2],
               scalar1=1.0 / DH, scalar2=None, op0=ALU.mult, inc=s_ve)
            if "d_stats" in dbg:
                p.wait("scalar", s_ve)
                st(dbg["d_stats"][sl * 4 + j], stt_tm[:, 0:64])
                stq.wait_all("scalar")
                stq.wait_all("vector")
            p.wait("tensor", s_ve)
            p.wait("tensor", s_ac)
            for h in range(H):
                op("tensor", "transpose", ps_w[3][0:16, ts(h, 128)],
                   stt_tm[:, ts(h, 16)], ident_s[:],
                   inc=(s_mm if h == H - 1 else None))
            p.wait("vector", s_mm)
            for h in range(H):
                op("vector", "tensor_copy",
                   out=stt_fm[:, h * TSB + j * 128:h * TSB + (j + 1) * 128],
                   in_=ps_w[3][0:16, ts(h, 128)],
                   inc=(s_ve if h == H - 1 else None))
            p.wait("tensor", s_ve)
        for k in range(8):
            p.wait("sync", s_mm)
            ld(whh[0][:], w1h_hi_d[k])
            ld(whh[1][:], w1h_lo_d[k])
            ldq.wait_all("tensor")
            for oc in range(8):
                op("tensor", "matmul", ps_all[oc][:, 0:TSB],
                   whh[0][:, ts(oc, 128)], xnhi[:, ts(k, TSB)],
                   start=(k == 0), stop=False)
                op("tensor", "matmul", ps_all[oc][:, 0:TSB],
                   whh[0][:, ts(oc, 128)], xnlo[:, ts(k, TSB)],
                   start=False, stop=False)
                op("tensor", "matmul", ps_all[oc][:, 0:TSB],
                   whh[1][:, ts(oc, 128)], xnhi[:, ts(k, TSB)],
                   start=False, stop=(k == 7),
                   inc=(s_mm if oc == 7 else None))
        p.wait("scalar", s_mm)
        for oc in range(8):
            op("scalar", "copy", out=base_s[:, ts(oc, TSB)],
               in_=ps_all[oc][:, 0:TSB], inc=(s_ac if oc == 7 else None))
        for h in range(H):
            p.wait("tensor", s_ac)
            p.wait("tensor", s_ve)
            for oc in range(8):
                for j in range(4):
                    op("tensor", "matmul",
                       ps_all[oc][:, j * 128:(j + 1) * 128],
                       w1s_s[:, ts(oc, 128)],
                       stt_fm[:, h * TSB + j * 128:h * TSB + (j + 1) * 128],
                       start=True, stop=True,
                       inc=(s_mm if oc == 7 and j == 3 else None))
            p.wait("vector", s_mm)
            for oc in range(8):
                op("vector", "tensor_tensor", out=hdn[:, ts(oc, TSB)],
                   in0=base_s[:, ts(oc, TSB)], in1=ps_all[oc][:, 0:TSB],
                   op=ALU.add, inc=(s_ve if oc == 7 else None))
            p.wait("scalar", s_ve)
            for oc in range(8):
                op("scalar", "activation", out=hdn[:, ts(oc, TSB)],
                   in_=hdn[:, ts(oc, TSB)], func=AF.Gelu,
                   bias=b1_s[:, oc:oc + 1], inc=(s_ac if oc == 7 else None))
            p.wait("tensor", s_ac)
            for oc in range(8):
                op("tensor", "matmul", ps_w[3][0:4, 0:TSB],
                   w2_s[:, ts(oc, 4)], hdn[:, ts(oc, TSB)],
                   start=(oc == 0), stop=(oc == 7),
                   inc=(s_mm if oc == 7 else None))
            p.wait("vector", s_mm)
            op("vector", "tensor_scalar", out=lgs[:],
               in0=ps_w[3][0:4, 0:TSB], scalar1=stat_c[0:4, 2:3],
               scalar2=stat_c[0:4, 3:4], op0=ALU.mult, op1=ALU.add,
               inc=s_ve)
            p.wait("tensor", s_ve)
            for bk in range(4):
                op("tensor", "transpose", ps_w[2][:, ts(bk, 4)],
                   lgs[:, ts(bk, 128)], ident_s[0:4, 0:4],
                   inc=(s_mm if bk == 3 else None))
            p.wait("vector", s_mm)
            op("vector", "tensor_copy", out=lgtm[:, ts(h, 16)],
               in_=ps_w[2][:, 0:16], inc=s_ve)
            p.wait("tensor", s_ve)
        softmax_ocombine(sl)
        p.barrier(bar)
    if stop_after == "PH3":
        return _finish()

    qrhs5 = w_hi[:, 16384:24576]
    otm_hi = w_hi[:, 0:4096]
    otm_lo = w_hi[:, 4096:8192]
    ofm_hi = xnhi
    ofm_lo = xnlo
    y1 = pout
    y1sq = big3
    cent = w_hiF[:, 4096:8192]
    aa = v_out
    ld(qrhs5.rearrange("p (a c) -> p a c", a=8),
       qrhs_d.rearrange("a b c -> b a c"))
    for tl in range(8):
        t0 = tl * TSB
        ld(otm_hi.rearrange("p (a c) -> p a c", a=4),
           o_d[0][t0:t0 + TSB, :].rearrange("(a b) c -> b a c", b=128))
        ld(otm_lo.rearrange("p (a c) -> p a c", a=4),
           o_d[1][t0:t0 + TSB, :].rearrange("(a b) c -> b a c", b=128))
        ldq.wait_all("tensor")
        for pair in range(2):
            src = otm_hi if pair == 0 else otm_lo
            dst = ofm_hi if pair == 0 else ofm_lo
            for j in range(4):
                for oc in range(8):
                    op("tensor", "transpose",
                       ps_w[(j * 8 + oc) % 4][:, 0:64].bitcast(BF16)[:, 0:128],
                       src[:, j * 1024 + oc * 128:j * 1024 + (oc + 1) * 128],
                       identb[:], inc=s_mm)
                    eng = "vector" if (j * 8 + oc) % 2 == 0 else "scalar"
                    esm = s_ve if (j * 8 + oc) % 2 == 0 else s_ac
                    p.wait(eng, s_mm)
                    meth = "tensor_copy" if eng == "vector" else "copy"
                    op(eng, meth,
                       out=dst[:, oc * TSB + j * 128:oc * TSB + (j + 1) * 128],
                       in_=ps_w[(j * 8 + oc) % 4][:, 0:64]
                       .bitcast(BF16)[:, 0:128], inc=esm)
                    p.wait("tensor", esm)
        for k in range(8):
            p.wait("sync", s_mm)
            ld(whh[0][:], wo_hi_d[k])
            ld(whh[1][:], wo_lo_d[k])
            ldq.wait_all("tensor")
            for oc in range(8):
                op("tensor", "matmul", ps_all[oc][:, 0:TSB],
                   whh[0][:, ts(oc, 128)], ofm_hi[:, ts(k, TSB)],
                   start=(k == 0), stop=False)
                op("tensor", "matmul", ps_all[oc][:, 0:TSB],
                   whh[0][:, ts(oc, 128)], ofm_lo[:, ts(k, TSB)],
                   start=False, stop=False)
                op("tensor", "matmul", ps_all[oc][:, 0:TSB],
                   whh[1][:, ts(oc, 128)], ofm_hi[:, ts(k, TSB)],
                   start=False, stop=(k == 7),
                   inc=(s_mm if oc == 7 else None))
        x3 = x_fm.rearrange("(a b) c -> b a c", b=128)
        ld(xfm_t[:].rearrange("p (a c) -> p a c", a=4),
           x3[:, 0:4, TAIL + t0:TAIL + t0 + TSB])
        ldq.wait_all("vector")
        p.wait("vector", s_mm)
        for oc in range(4):
            op("vector", "scalar_tensor_tensor", out=y1[:, ts(oc, TSB)],
               in0=ps_all[oc][:, 0:TSB], scalar=1.0,
               in1=xfm_t[:, ts(oc, TSB)], op0=ALU.mult, op1=ALU.add,
               inc=(s_ve if oc == 3 else None))
        p.wait("sync", s_ve)
        ld(xfm_t[:].rearrange("p (a c) -> p a c", a=4),
           x3[:, 4:8, TAIL + t0:TAIL + t0 + TSB])
        ldq.wait_all("vector")
        for oc in range(4, 8):
            op("vector", "scalar_tensor_tensor", out=y1[:, ts(oc, TSB)],
               in0=ps_all[oc][:, 0:TSB], scalar=1.0,
               in1=xfm_t[:, ts(oc - 4, TSB)], op0=ALU.mult, op1=ALU.add,
               inc=(s_ve if oc == 7 else None))
        if "d_y1" in dbg:
            p.wait("scalar", s_ve)
            st(dbg["d_y1"][tl], y1[:, 0:4096])
            stq.wait_all("scalar")
            stq.wait_all("vector")
        p.wait("scalar", s_ve)
        for oc in range(8):
            op("scalar", "activation", out=y1sq[:, ts(oc, TSB)],
               in_=y1[:, ts(oc, TSB)], func=AF.Square,
               inc=(s_ac if oc == 7 else None))
        p.wait("tensor", s_ve)
        p.wait("tensor", s_ac)
        for oc in range(8):
            op("tensor", "matmul", ps_w[0][:, 0:TSB], ones_s[:],
               y1[:, ts(oc, TSB)], start=(oc == 0), stop=(oc == 7))
            op("tensor", "matmul", ps_w[1][:, 0:TSB], ones_s[:],
               y1sq[:, ts(oc, TSB)], start=(oc == 0), stop=(oc == 7),
               inc=(s_mm if oc == 7 else None))
        p.wait("scalar", s_mm)
        op("scalar", "activation", out=mu_b[:], in_=ps_w[0][:, 0:TSB],
           func=AF.Copy, scale=1.0 / D)
        op("scalar", "activation", out=rs_b[:], in_=ps_w[1][:, 0:TSB],
           func=AF.Copy, scale=1.0 / D, inc=s_ac)
        p.wait("vector", s_ac)
        op("vector", "tensor_tensor", out=t1[:], in0=mu_b[:], in1=mu_b[:],
           op=ALU.mult)
        op("vector", "tensor_tensor", out=rs_b[:], in0=rs_b[:], in1=t1[:],
           op=ALU.subtract, inc=s_ve)
        p.wait("scalar", s_ve)
        op("scalar", "activation", out=rs_b[:], in_=rs_b[:], func=AF.Ln,
           bias=EPS_LN)
        op("scalar", "activation", out=rs_b[:], in_=rs_b[:], func=AF.Exp,
           scale=-0.5, inc=s_ac)
        p.wait("vector", s_ac)
        for oc in range(8):
            op("vector", "tensor_tensor", out=cent[:, ts(oc, TSB)],
               in0=y1[:, ts(oc, TSB)], in1=mu_b[:], op=ALU.subtract,
               inc=(s_ve if oc == 7 else None))
        p.wait("scalar", s_ve)
        for oc in range(8):
            op("scalar", "activation", out=aa[:, ts(oc, TSB)],
               in_=cent[:, ts(oc, TSB)], func=AF.Abs,
               inc=(s_ac if oc == 7 else None))
        p.wait("vector", s_ac)
        for j in range(4):
            op("vector", "tensor_tensor", out=aa[:, ts(j, TSB)],
               in0=aa[:, ts(2 * j, TSB)], in1=aa[:, ts(2 * j + 1, TSB)],
               op=ALU.max)
        for j in range(2):
            op("vector", "tensor_tensor", out=aa[:, ts(j, TSB)],
               in0=aa[:, ts(2 * j, TSB)], in1=aa[:, ts(2 * j + 1, TSB)],
               op=ALU.max)
        op("vector", "tensor_tensor", out=aa[:, 0:TSB], in0=aa[:, 0:TSB],
           in1=aa[:, TSB:2 * TSB], op=ALU.max, inc=s_ve)
        for bk in range(4):
            p.wait("tensor", s_ve)
            p.wait("tensor", s_ac)
            op("tensor", "transpose", ps_w[2][:, 0:128],
               aa[:, bk * 128:(bk + 1) * 128], ident_s[:], inc=s_mm)
            p.wait("vector", s_mm)
            op("vector", "reduce_max", out=stat_c[:, bk:bk + 1],
               in_=ps_w[2][:, 0:128], axis=AX.X, inc=s_ve)
        p.wait("tensor", s_ac)
        p.wait("tensor", s_ve)
        for bk in range(4):
            op("tensor", "transpose", ps_w[2][0:1, ts(bk, 128)],
               stat_c[:, bk:bk + 1], ident_s[:],
               inc=(s_mm if bk == 3 else None))
        p.wait("scalar", s_mm)
        op("scalar", "copy", out=l2row[:, 0:512], in_=ps_w[2][0:1, 0:512],
           inc=s_ac)
        p.wait("tensor", s_ac)
        for bk in range(4):
            op("tensor", "matmul", ps_w[3][:, ts(bk, 128)], ones_s[0:1, :],
               l2row[0:1, ts(bk, 128)], start=True, stop=True,
               inc=(s_mm if bk == 3 else None))
        p.wait("vector", s_mm)
        op("vector", "tensor_copy", out=t1[:], in_=ps_w[3][:, 0:TSB],
           inc=s_ve)
        op("vector", "tensor_tensor", out=t1[:], in0=t1[:], in1=rs_b[:],
           op=ALU.mult)
        op("vector", "tensor_scalar", out=t1[:], in0=t1[:], scalar1=1e-12,
           scalar2=None, op0=ALU.max, inc=s_ve)
        p.wait("scalar", s_ve)
        op("scalar", "activation", out=t1[:], in_=t1[:], func=AF.Ln,
           inc=s_ac)
        p.wait("scalar", s_ac)
        op("scalar", "activation", out=mu_b[:], in_=t1[:], func=AF.Exp,
           scale=-1.0, bias=LN127)
        op("scalar", "activation", out=t1[:], in_=t1[:], func=AF.Exp,
           scale=1.0, bias=-LN127, inc=s_ac)
        p.wait("vector", s_ac)
        op("vector", "tensor_tensor", out=mu_b[:], in0=mu_b[:], in1=rs_b[:],
           op=ALU.mult, inc=s_ve)
        p.wait("vector", s_ve)
        for oc in range(8):
            op("vector", "tensor_tensor", out=cent[:, ts(oc, TSB)],
               in0=cent[:, ts(oc, TSB)], in1=mu_b[:], op=ALU.mult)
            round_halfup("vector", ofm_hi[:, ts(oc, TSB)],
                         cent[:, ts(oc, TSB)], sq2[:, 0:TSB],
                         sq2[:, TSB:2 * TSB])
        op("vector", "engine_nop", inc=s_ve)
        if "d_qlhs" in dbg:
            p.wait("scalar", s_ve)
            for oc in range(8):
                op("scalar", "activation", out=y1sq[:, ts(oc, TSB)][:, 0:TSB],
                   in_=ofm_hi[:, ts(oc, TSB)], func=AF.Copy,
                   inc=(s_ac if oc == 7 else None))
            p.wait("scalar", s_ac)
            st(dbg["d_qlhs"][tl], y1sq[:, 0:4096])
            stq.wait_all("scalar")
            stq.wait_all("vector")
        p.wait("tensor", s_ve)
        for oc in range(8):
            for k in range(8):
                op("tensor", "matmul", ps_all[oc][:, 0:TSB],
                   qrhs5[:, k * D + oc * 128:k * D + (oc + 1) * 128],
                   ofm_hi[:, ts(k, TSB)], start=(k == 0), stop=(k == 7),
                   inc=(s_mm if k == 7 else None))
        p.wait("vector", s_mm)
        for oc in range(8):
            op("vector", "tensor_scalar", out=y1sq[:, ts(oc, TSB)],
               in0=ps_all[oc][:, 0:TSB], scalar1=rsc_c[:, oc:oc + 1],
               scalar2=None, op0=ALU.mult)
            op("vector", "tensor_tensor", out=y1sq[:, ts(oc, TSB)],
               in0=y1sq[:, ts(oc, TSB)], in1=t1[:], op=ALU.mult)
            op("vector", "tensor_tensor", out=y1sq[:, ts(oc, TSB)],
               in0=y1sq[:, ts(oc, TSB)], in1=y1[:, ts(oc, TSB)], op=ALU.add,
               inc=(s_ve if oc == 7 else None))
        p.wait("scalar", s_ve)
        for oc in range(8):
            st(out_fm[ts(oc, 128), t0:t0 + TSB], y1sq[:, ts(oc, TSB)])
        stq.wait_all("sync")
        p.barrier(bar)

    return _finish()



def make_fir_toeplitz(fir_long, fir_short):
    T = np.zeros((D, 128, 128), dtype=np.float32)
    wl = fir_long.reshape(D, 64)
    ws = fir_short.reshape(D, 5)
    for po in range(64):
        for k in range(64):
            pp = po + 1 + k
            if pp < 128:
                T[:, pp, po] = wl[:, k]
        for k in range(5):
            pp = po + 60 + k
            if pp < 128:
                T[:, pp, 64 + po] = ws[:, k]
    return T


def stage_inputs(inputs):
    x = np.ascontiguousarray(inputs["L0_attn_hidden_states"], np.float32)
    wq = np.asarray(inputs["L0_attn_q_proj_w"], np.float32)
    wk = np.asarray(inputs["L0_attn_k_proj_w"], np.float32)
    wv = np.asarray(inputs["L0_attn_v_proj_w"], np.float32)
    bp = np.asarray(inputs["L0_attn_b_proj_w"], np.float32)
    wqkvb = np.ascontiguousarray(
        np.concatenate([wq.T, wk.T, wv.T], axis=1))
    wbeta = np.zeros((D, 32), np.float32)
    wbeta[:, 0:4] = bp.T
    convw = np.ascontiguousarray(np.concatenate(
        [np.asarray(inputs["L0_attn_q_conv_w"], np.float32),
         np.asarray(inputs["L0_attn_k_conv_w"], np.float32),
         np.asarray(inputs["L0_attn_v_conv_w"], np.float32)], axis=0))
    fir_toep = make_fir_toeplitz(
        np.asarray(inputs["L0_attn_fir_long_w"], np.float32),
        np.asarray(inputs["L0_attn_fir_short_w"], np.float32))
    fw1 = np.asarray(inputs["L0_attn_fuse_w1"], np.float32)
    shared = dict(
        wqkvb=wqkvb, wbeta=wbeta, convw=convw, fir_toep=fir_toep,
        w1h=np.ascontiguousarray(fw1[:, :D].T),
        w1s=np.ascontiguousarray(fw1[:, D:].T),
        b1=np.ascontiguousarray(np.asarray(
            inputs["L0_attn_fuse_b1"], np.float32).reshape(8, 128).T),
        w2=np.ascontiguousarray(
            np.asarray(inputs["L0_attn_fuse_w2"], np.float32).T),
        b2=np.asarray(inputs["L0_attn_fuse_b2"], np.float32),
        wo=np.ascontiguousarray(
            np.asarray(inputs["L0_attn_o_proj_w"], np.float32).T),
        ffn_rhs=np.ascontiguousarray(inputs["L0_ffn_rhs"], np.float32),
        o_norm=np.asarray(inputs["L0_attn_o_norm_w"], np.float32),
        scal=np.array([inputs["L0_attn_logit_temp"][0],
                       inputs["L0_attn_res_scale_short"][0],
                       inputs["L0_attn_res_scale_long"][0], 0.0], np.float32),
        ident=np.eye(128, dtype=np.float32),
        cvec=np.tile(np.array([1e-5, 1e-12, 1e-4, 1.0, -0.5, -1.0,
                               1.0 / 1024, LN127, -LN127, 0.5, -1e-12, 2.0,
                               0.0, 3.0, 4.0, 5.0], np.float32), (128, 1)),
        ones=np.ones((128, 128), np.float32),
        mask_low=(-np.tril(np.ones((128, 128), np.float32), -1)),
        mask_up0=np.triu(np.ones((128, 128), np.float32), 0),
    )
    in_maps = []
    for c in range(8):
        b, half = c // 2, c % 2
        buf = np.zeros((TBUF, D), np.float32)
        if half == 0:
            buf[TAIL:] = x[b, :TAIL]
        else:
            buf[:] = x[b]
        m = dict(shared)
        m["x_tm"] = buf
        m["x_fm"] = np.ascontiguousarray(buf.T)
        in_maps.append(m)
    return in_maps


_NC_CACHE = {}


def kernel(**inputs):
    in_maps = stage_inputs(inputs)
    if "nc" not in _NC_CACHE:
        _NC_CACHE["nc"] = build_program()[0]
    nc = _NC_CACHE["nc"]
    res = run_bass_kernel_spmd(nc, in_maps, core_ids=list(range(8)))
    out = np.empty((B, L, D), np.float32)
    for c in range(8):
        b, half = c // 2, c % 2
        out[b, half * TAIL:(half + 1) * TAIL] = res.results[c]["out_fm"].T
    return out


# version 06
# speedup vs baseline: 1.0383x; 1.0383x over previous
import numpy as np
import concourse.bass as bass
import concourse.mybir as mybir
from concourse.bass_utils import run_bass_kernel_spmd

F32 = mybir.dt.float32
BF16 = mybir.dt.bfloat16
AF = mybir.ActivationFunctionType
ALU = mybir.AluOpType
AX = mybir.AxisListType

B, L, D, H, DH = 4, 8192, 1024, 4, 256
TBUF, TAIL = 8192, 4096
TSB = 512
NSB = TBUF // TSB
CK = 128
NCPS = TSB // CK
NCK = TBUF // CK
NW = 3 * D
EPS_LN = 1e-5
EPS_L2 = 1e-12
LN127 = float(np.log(np.float64(127.0)))

DBG = ()
CONSTVALS = [1e-5, 1e-12, 1e-4, 1.0, -0.5, -1.0, 1.0 / 1024, LN127, -LN127,
             0.5, -1e-12, 2.0, 0.0, 3.0, 4.0, 5.0]


def ts(i, n):
    return slice(i * n, (i + 1) * n)


class Sem:
    def __init__(self, cm):
        self._cm = cm
        self.h = cm.__enter__()
        self.n = 0

    def inc(self, k=1):
        self.n += k
        return self.h


class P:
    ENGINES = ("tensor", "vector", "scalar", "gpsimd", "sync")

    def __init__(self, nc):
        self.nc = nc
        self.q = {e: [] for e in self.ENGINES}
        self.sems = []

    def sem(self, name):
        s = Sem(self.nc.semaphore(name))
        self.sems.append(s)
        return s

    def on(self, engine, fn):
        self.q[engine].append(fn)

    def wait(self, engine, sem, val=None):
        v = sem.n if val is None else val
        if v <= 0:
            return
        self.q[engine].append(lambda eng, s=sem.h, vv=v: eng.wait_ge(s, vv))

    def barrier(self, bar):
        target = bar.n + len(self.ENGINES)
        for e in self.ENGINES:
            self.q[e].append(lambda eng, s=bar.h: eng.sem_inc(s, 1))
        bar.n = target
        for e in self.ENGINES:
            self.q[e].append(lambda eng, s=bar.h, v=target: eng.wait_ge(s, v))

    def emit(self):
        def mk(e):
            def run(eng):
                for f in self.q[e]:
                    f(eng)
            return run
        with self.nc.Block() as block:
            block.tensor(mk("tensor"))
            block.vector(mk("vector"))
            block.scalar(mk("scalar"))
            block.gpsimd(mk("gpsimd"))
            block.sync(mk("sync"))


def build_program(debug=DBG, stop_after=None):
    nc = bass.Bass("TRN2", target_bir_lowering=False, debug=False)
    p = P(nc)
    alloc = []

    def sbuf(name, shape, dt=F32):
        cm = nc.sbuf_tensor(name, shape, dt)
        alloc.append(cm)
        return cm.__enter__()

    def psumt(name, shape, dt=F32):
        cm = nc.psum_tensor(name, shape, dt)
        alloc.append(cm)
        return cm.__enter__()

    def I(name, shape, dt=F32):
        return nc.dram_tensor(name, shape, dt, kind="ExternalInput").ap()

    x_tm = I("x_tm", [TBUF, D])
    x_fm = I("x_fm", [D, TBUF])
    wqkvb = I("wqkvb", [D, NW])
    wbeta = I("wbeta", [D, 32])
    convw = I("convw", [3 * D, 4])
    fir_toep = I("fir_toep", [D, 128, 128])
    w1h = I("w1h", [D, D])
    w1s = I("w1s", [16, D])
    b1 = I("b1", [128, 8])
    w2 = I("w2", [D, 4])
    b2 = I("b2", [4])
    wo = I("wo", [D, D])
    ffn_rhs = I("ffn_rhs", [D, D])
    o_norm = I("o_norm", [DH])
    scal = I("scal", [4])
    ident = I("ident", [128, 128])
    ones = I("ones", [128, 128])
    mask_low = I("mask_low", [128, 128])
    mask_up0 = I("mask_up0", [128, 128])
    cvec = I("cvec", [128, 16])

    out_fm = nc.dram_tensor("out_fm", [D, TAIL], F32, kind="ExternalOutput").ap()
    dbg = {}
    for name, shape in [
        ("d_qn", [NSB, 128, 16 * TSB]), ("d_v", [NSB, 128, 8 * TSB]),
        ("d_beta", [NSB, 4, TSB]), ("d_delta", [TAIL, D]),
        ("d_stats", [TAIL // 128, 128, 64]), ("d_fw", [TAIL // 128, 128, 16]),
        ("d_o", [TAIL, D]), ("d_y1", [8, 128, 8 * TSB]), ("d_L", [TAIL, 2 * D]),
        ("d_qlhs", [8, 128, 8 * TSB]),
    ]:
        if name in debug:
            dbg[name] = nc.dram_tensor(name, shape, F32,
                                       kind="ExternalOutput").ap()

    wlo_slab = nc.dram_tensor("wlo_slab", [25, 8, 128, 128], BF16).ap()
    w1h_hi_d = nc.dram_tensor("w1h_hi_d", [8, 128, D], BF16).ap()
    w1h_lo_d = nc.dram_tensor("w1h_lo_d", [8, 128, D], BF16).ap()
    wo_hi_d = nc.dram_tensor("wo_hi_d", [8, 128, D], BF16).ap()
    wo_lo_d = nc.dram_tensor("wo_lo_d", [8, 128, D], BF16).ap()
    v_pad = nc.dram_tensor("v_pad", [64 + TBUF, D], F32).ap()
    delta_d = nc.dram_tensor("delta_d", [TAIL, D], F32).ap()
    l_d = nc.dram_tensor("l_d", [TAIL, 2 * D], F32).ap()
    xn_d = nc.dram_tensor("xn_d", [2, 8, 128, TAIL], BF16).ap()
    o_d = nc.dram_tensor("o_d", [2, TAIL, D], BF16).ap()
    qrhs_d = nc.dram_tensor("qrhs_d", [8, 128, D], BF16).ap()

    bar = p.sem("bar")
    s_mm = p.sem("s_mm")
    s_ve = p.sem("s_ve")
    s_ac = p.sem("s_ac")
    s_gp = p.sem("s_gp")

    class DmaQ:
        def __init__(self, engine, name, depth=6, compute_guard=True):
            self.engine = engine
            self.depth = depth
            self.guard = compute_guard
            self.sems = [p.sem(f"{name}{i}") for i in range(depth)]
            self.cnt = 0

        def dma(self, dst, src):
            s = self.sems[self.cnt % self.depth]
            if self.cnt >= self.depth:
                p.wait(self.engine, s)
            if self.engine == "scalar" and self.guard:
                p.wait("scalar", s_ac)
                p.wait("scalar", s_ve)
            self.cnt += 1
            tgt = s.inc(16)
            p.on(self.engine, lambda eng, d=dst, ss=src, h=tgt:
                 eng.dma_start(out=d, in_=ss).then_inc(h, 16))
            return s, s.n

        def wait_all(self, engine):
            for s in self.sems:
                p.wait(engine, s)

    ldq = DmaQ("sync", "qld", 6)
    ld2q = DmaQ("scalar", "ql2", 4, compute_guard=False)
    stq = DmaQ("scalar", "qst", 6)
    gpq = DmaQ("scalar", "qgp", 4)

    def ld(dst, src):
        ldq.dma(dst, src)

    def st(dst, src):
        stq.dma(dst, src)

    def g2(dst, src):
        gpq.dma(dst, src)

    osems = {"vector": s_ve, "scalar": s_ac, "gpsimd": s_gp}

    def op(engine, meth, *a, inc=None, **kw):
        if engine in osems:
            osem = osems[engine]
            prior = osem.n
            h = osem.inc()

            def f(eng, meth=meth, a=a, kw=kw, h=h, prior=prior):
                if prior > 0:
                    eng.wait_ge(h, prior)
                getattr(eng, meth)(*a, **kw).then_inc(h, 1)
            p.on(engine, f)
        else:
            h = inc.inc() if inc is not None else None

            def f(eng, meth=meth, a=a, kw=kw, h=h):
                inst = getattr(eng, meth)(*a, **kw)
                if h is not None:
                    inst.then_inc(h, 1)
            p.on(engine, f)

    def _finish():
        stq.wait_all("scalar")
        ldq.wait_all("sync")
        gpq.wait_all("gpsimd")
        p.barrier(bar)
        p.emit()
        for cm in reversed(alloc):
            cm.__exit__(None, None, None)
        for s in reversed(p.sems):
            s._cm.__exit__(None, None, None)
        return nc, dbg

    w_hi = sbuf("w_hi", [128, 8 * NW], BF16)
    wbeta_s = sbuf("wbeta_s", [128, 8 * 32], BF16)
    w_hiF = w_hi[:].bitcast(F32)
    wkl = [sbuf(f"wkl{i}", [128, 1024], BF16) for i in range(2)]
    ident_s = sbuf("ident_s", [128, 128])
    identb = sbuf("identb", [128, 128], BF16)
    consts_s = sbuf("consts_s", [128, 16])
    for _i, _v in enumerate(CONSTVALS):
        nc.const_aps.register(F32, _v, consts_s[:, _i:_i + 1])
    ones_s = sbuf("ones_s", [128, 128])
    mlow_s = sbuf("mlow_s", [128, 128])
    mup0_s = sbuf("mup0_s", [128, 128])
    cw_s = sbuf("cw_s", [128, 96])
    w1s_s = sbuf("w1s_s", [16, D])
    b1_s = sbuf("b1_s", [128, 8])
    w2_s = sbuf("w2_s", [128, 32])
    b2_s = sbuf("b2_s", [4, 1])
    scal_s = sbuf("scal_s", [4, 1])
    sc_rep = sbuf("sc_rep", [128, 4])
    onorm_b = sbuf("onorm_b", [128, DH])
    rsc_c = sbuf("rsc_c", [128, 8])
    xtm_t = sbuf("xtm_t", [128, D])
    stat_c = sbuf("stat_c", [128, 16])
    scr = sbuf("scr", [128, D])
    l2row = sbuf("l2row", [1, 1024])
    mu_b = sbuf("mu_b", [128, TSB])
    rs_b = sbuf("rs_b", [128, TSB])
    xfm_t = sbuf("xfm_t", [128, 4 * TSB])
    t1 = sbuf("t1", [128, TSB])
    xnhi = sbuf("xnhi", [128, 8 * TSB], BF16)
    xnlo = sbuf("xnlo", [128, 8 * TSB], BF16)
    pout = sbuf("pout", [128, 16 * TSB])
    v_out = sbuf("v_out", [128, 8 * TSB])
    raw = sbuf("raw", [128, TSB + 4])
    hist = sbuf("hist", [128, 96])
    sq2 = sbuf("sq2", [128, 2 * TSB])
    beta_r = sbuf("beta_r", [4, TSB])
    beta_tm = sbuf("beta_tm", [128, 4 * NCPS])
    big3 = sbuf("big3", [128, 4096])
    a4 = sbuf("a4", [128, 4 * 128])
    at4 = sbuf("at4", [128, 4 * 128])
    tta = sbuf("tta", [128, 4 * 128])
    ttb = sbuf("ttb", [128, 4 * 128])
    attn4 = sbuf("attn4", [128, 4 * 128])
    kntm4 = sbuf("kntm4", [128, 4 * DH])
    vbtm4 = sbuf("vbtm4", [128, 4 * DH])
    kbtm4 = sbuf("kbtm4", [128, 4 * DH])
    utm4 = sbuf("utm4", [128, 4 * DH])
    wfm4 = sbuf("wfm4", [128, 4 * DH])
    uadj = sbuf("uadj", [128, DH])
    doutb = uadj
    ssb = sq2[:, 0:512]
    lgs = beta_r[:, 0:TSB]
    lgtm = sbuf("lgtm", [128, 64])
    fwc = sbuf("fwc", [128, 16])
    rms_c = sbuf("rms_c", [128, 16])
    mx4 = sbuf("mx4", [128, 8])
    whh = wkl
    ttoep = [v_out[:, 0:128], v_out[:, 128:256]]
    rbuf = pout[:, 0:8192]
    scat = w_hiF[:, 0:8192]

    ps_s = [psumt(f"ps_s{h}", [128, 512]) for h in range(H)]
    ps_w = [psumt(f"ps_w{i}", [128, 512]) for i in range(4)]
    ps_all = ps_s + ps_w

    ld(consts_s[:], cvec[:])
    ld(ident_s[:], ident[:])
    ldq.wait_all("vector")
    op("vector", "tensor_copy", out=identb[:], in_=ident_s[:])
    ld(ones_s[:], ones[:])
    ld(mlow_s[:], mask_low[:])
    ld(mup0_s[:], mask_up0[:])
    ld(cw_s[:].rearrange("b (a c) -> b a c", a=24),
       convw.rearrange("(a b) c -> b a c", b=128))
    ld(w1s_s[:], w1s[:])
    ld(b1_s[:], b1[:])
    ld(w2_s[:].rearrange("b (a c) -> b a c", a=8),
       w2.rearrange("(a b) c -> b a c", b=128))
    ld(b2_s[:], b2[:, None])
    ld(scal_s[:], scal[:, None])
    ldq.wait_all("scalar")
    ldq.wait_all("vector")
    ldq.wait_all("gpsimd")
    ldq.wait_all("tensor")

    for k in range(8):
        ld(big3[:, 0:NW] if NW <= 4096 else None, wqkvb[ts(k, 128), :])
        ldq.wait_all("vector")
        op("vector", "tensor_copy", out=w_hi[:, k * NW:(k + 1) * NW],
           in_=big3[:, 0:NW])
        op("vector", "tensor_tensor", out=big3[:, 0:NW], in0=big3[:, 0:NW],
           in1=w_hi[:, k * NW:(k + 1) * NW], op=ALU.subtract)
        op("vector", "tensor_copy", out=xnhi[:, 0:NW], in_=big3[:, 0:NW],
           inc=s_ve)
        p.wait("scalar", s_ve)
        for oc in range(24):
            st(wlo_slab[oc, k], xnhi[:, ts(oc, 128)])
        stq.wait_all("sync")
        stq.wait_all("vector")
    ld(scr[:, 0:256].rearrange("p (k c) -> p k c", k=8),
       wbeta.rearrange("(k p) c -> p k c", p=128))
    ldq.wait_all("vector")
    op("vector", "tensor_copy", out=wbeta_s[:], in_=scr[:, 0:256])
    op("vector", "tensor_tensor", out=scr[:, 0:256], in0=scr[:, 0:256],
       in1=wbeta_s[:], op=ALU.subtract)
    op("vector", "tensor_copy", out=xnhi[:, 0:256], in_=scr[:, 0:256],
       inc=s_ve)
    p.wait("scalar", s_ve)
    for k in range(8):
        st(wlo_slab[24, k][:, 0:32], xnhi[:, k * 32:(k + 1) * 32])
    stq.wait_all("sync")
    stq.wait_all("vector")
    for src, dhi, dlo in ((w1h, w1h_hi_d, w1h_lo_d), (wo, wo_hi_d, wo_lo_d)):
        for k in range(8):
            ld(big3[:, 0:D], src[ts(k, 128), :])
            ldq.wait_all("vector")
            op("vector", "tensor_copy", out=xnhi[:, 0:D], in_=big3[:, 0:D])
            op("vector", "tensor_tensor", out=big3[:, 0:D],
               in0=big3[:, 0:D], in1=xnhi[:, 0:D], op=ALU.subtract)
            op("vector", "tensor_copy", out=xnlo[:, 0:D], in_=big3[:, 0:D],
               inc=s_ve)
            p.wait("scalar", s_ve)
            st(dhi[k], xnhi[:, 0:D])
            st(dlo[k], xnlo[:, 0:D])
            stq.wait_all("sync")
            stq.wait_all("vector")

    ld(scr[0:1, 0:DH], o_norm[None, :])
    ldq.wait_all("tensor")
    op("tensor", "matmul", ps_w[0][:, 0:DH], ones_s[0:1, :], scr[0:1, 0:DH],
       start=True, stop=True)
    op("tensor", "transpose", ps_w[1][0:1, 0:4], scal_s[:],
       ident_s[0:4, 0:4], inc=s_mm)
    p.wait("scalar", s_mm)
    op("scalar", "copy", out=onorm_b[:], in_=ps_w[0][:, 0:DH])
    op("scalar", "copy", out=l2row[0:1, 0:4], in_=ps_w[1][0:1, 0:4],
       inc=s_ac)
    p.wait("tensor", s_ac)
    op("tensor", "matmul", ps_w[2][:, 0:4], ones_s[0:1, :], l2row[0:1, 0:4],
       start=True, stop=True, inc=s_mm)
    p.wait("scalar", s_mm)
    op("scalar", "copy", out=sc_rep[:], in_=ps_w[2][:, 0:4], inc=s_ac)

    ldq.wait_all("vector")
    p.wait("vector", s_mm)
    p.wait("vector", s_ac)
    op("vector", "memset", hist[:], 0.0)
    op("vector", "memset", scr[:, 0:D], 0.0, inc=s_ve)
    p.wait("scalar", s_ve)
    st(v_pad[0:64, :], scr[0:64, 0:D])

    ffw = pout
    ffa = v_out
    for k in range(8):
        ld(ffw[:, ts(k, D)], ffn_rhs[ts(k, 128), :])
    ldq.wait_all("vector")
    for j in range(4):
        op("vector", "tensor_tensor", out=ffa[:, ts(j, D)],
           in0=ffw[:, ts(2 * j, D)], in1=ffw[:, ts(2 * j + 1, D)], op=ALU.max)
        op("vector", "scalar_tensor_tensor", out=ffa[:, ts(j, D)],
           in0=ffw[:, ts(2 * j, D)], scalar=-1.0, in1=ffa[:, ts(j, D)],
           op0=ALU.mult, op1=ALU.max)
        op("vector", "scalar_tensor_tensor", out=ffa[:, ts(j, D)],
           in0=ffw[:, ts(2 * j + 1, D)], scalar=-1.0, in1=ffa[:, ts(j, D)],
           op0=ALU.mult, op1=ALU.max)
    for j in range(2):
        op("vector", "tensor_tensor", out=ffa[:, ts(j, D)],
           in0=ffa[:, ts(2 * j, D)], in1=ffa[:, ts(2 * j + 1, D)], op=ALU.max)
    op("vector", "tensor_tensor", out=ffa[:, 0:D], in0=ffa[:, 0:D],
       in1=ffa[:, ts(1, D)], op=ALU.max, inc=s_ve)
    for j in range(8):
        p.wait("tensor", s_ve)
        p.wait("tensor", s_ac)
        op("tensor", "transpose", ps_w[0][:, 0:128], ffa[:, ts(j, 128)],
           ident_s[:], inc=s_mm)
        p.wait("vector", s_mm)
        op("vector", "reduce_max", out=rsc_c[:, j:j + 1],
           in_=ps_w[0][:, 0:128], axis=AX.X, inc=s_ve)
    op("vector", "tensor_scalar", out=rsc_c[:, 0:8], in0=rsc_c[:, 0:8],
       scalar1=1e-12, scalar2=1.0 / 127.0, op0=ALU.max, op1=ALU.mult,
       inc=s_ve)
    p.wait("scalar", s_ve)
    op("scalar", "activation", out=stat_c[:, 0:8], in_=rsc_c[:, 0:8],
       func=AF.Ln)
    op("scalar", "activation", out=stat_c[:, 0:8], in_=stat_c[:, 0:8],
       func=AF.Exp, scale=-1.0, inc=s_ac)
    p.wait("tensor", s_ac)
    for j in range(8):
        op("tensor", "transpose", ps_w[j // 4][0:1, ts(j % 4, 128)],
           stat_c[:, j:j + 1], ident_s[:], inc=(s_mm if j == 7 else None))
    p.wait("scalar", s_mm)
    op("scalar", "copy", out=l2row[:, 0:512], in_=ps_w[0][0:1, 0:512])
    op("scalar", "copy", out=l2row[:, 512:1024], in_=ps_w[1][0:1, 0:512],
       inc=s_ac)
    for j in range(8):
        p.wait("tensor", s_ac)
        p.wait("tensor", s_ve)
        op("tensor", "matmul", ps_w[1][:, 0:128], ones_s[0:1, :],
           l2row[0:1, ts(j, 128)], start=True, stop=True, inc=s_mm)
        p.wait("vector", s_mm)
        op("vector", "tensor_copy", out=sq2[:, ts(j, 128)],
           in_=ps_w[1][:, 0:128], inc=s_ve)
    srep = sq2
    p.wait("vector", s_ac)

    def round_halfup(engine, dst_bf16, src_f32, tmp, tmp2):
        op(engine, "tensor_scalar", out=tmp, in0=src_f32, scalar1=128.5,
           scalar2=None, op0=ALU.add)
        for c in (256.0, 128.0, 64.0, 32.0, 16.0, 8.0, 4.0, 2.0, 1.0):
            op(engine, "tensor_scalar", out=tmp2, in0=tmp, scalar1=c,
               scalar2=c, op0=ALU.is_ge, op1=ALU.mult)
            op(engine, "tensor_tensor", out=tmp, in0=tmp, in1=tmp2,
               op=ALU.subtract)
        op(engine, "scalar_tensor_tensor", out=tmp2, in0=src_f32, scalar=0.5,
           in1=tmp, op0=ALU.add, op1=ALU.subtract)
        op(engine, "tensor_copy", out=dst_bf16, in_=tmp2)

    for k in range(8):
        op("vector", "tensor_tensor", out=ffw[:, ts(k, D)],
           in0=ffw[:, ts(k, D)], in1=srep[:, 0:D], op=ALU.mult)
        round_halfup("vector", xnhi[:, 0:D], ffw[:, ts(k, D)],
                     big3[:, 0:D], big3[:, D:2 * D])
        op("vector", "engine_nop", inc=s_ve)
        p.wait("scalar", s_ve)
        st(qrhs_d[k], xnhi[:, 0:D])
        stq.wait_all("vector")
    p.barrier(bar)
    if stop_after == "PH0":
        return _finish()

    def ln_and_xn(sbi):
        t0 = sbi * TSB
        for j in range(4):
            ld(xtm_t[:], x_tm[t0 + j * 128:t0 + (j + 1) * 128, :])
            ldq.wait_all("vector")
            ldq.wait_all("scalar")
            op("vector", "tensor_scalar", out=scr[:, 0:D], in0=xtm_t[:],
               scalar1=1.0 / D, scalar2=None, op0=ALU.mult, op1=ALU.add,
               accum_out=stat_c[:, j:j + 1], inc=s_ve)
            op("scalar", "activation", out=sq2[:, 0:D], in_=xtm_t[:],
               func=AF.Square, accum_out=stat_c[:, 4 + j:5 + j], inc=s_ac)
            p.wait("sync", s_ve)
            p.wait("sync", s_ac)
        p.wait("vector", s_ac)
        op("vector", "tensor_tensor", out=stat_c[:, 8:12],
           in0=stat_c[:, 0:4], in1=stat_c[:, 0:4], op=ALU.mult)
        op("vector", "tensor_scalar", out=stat_c[:, 4:8], in0=stat_c[:, 4:8],
           scalar1=1.0 / D, scalar2=None, op0=ALU.mult)
        op("vector", "tensor_tensor", out=stat_c[:, 4:8], in0=stat_c[:, 4:8],
           in1=stat_c[:, 8:12], op=ALU.subtract, inc=s_ve)
        p.wait("scalar", s_ve)
        op("scalar", "activation", out=stat_c[:, 4:8], in_=stat_c[:, 4:8],
           func=AF.Ln, bias=EPS_LN)
        op("scalar", "activation", out=stat_c[:, 4:8], in_=stat_c[:, 4:8],
           func=AF.Exp, scale=-0.5, inc=s_ac)
        p.wait("tensor", s_ac)
        for j in range(8):
            op("tensor", "transpose", ps_w[j // 4][0:1, ts(j % 4, 128)],
               stat_c[:, j:j + 1], ident_s[:],
               inc=(s_mm if j == 7 else None))
        p.wait("scalar", s_mm)
        op("scalar", "copy", out=l2row[:, 0:512], in_=ps_w[0][0:1, 0:512])
        op("scalar", "copy", out=l2row[:, 512:1024],
           in_=ps_w[1][0:1, 0:512], inc=s_ac)
        p.wait("tensor", s_ac)
        for j in range(4):
            op("tensor", "matmul", ps_w[1][:, 0:128], ones_s[0:1, :],
               l2row[0:1, ts(j, 128)], start=True, stop=True)
            op("tensor", "matmul", ps_w[2][:, 0:128], ones_s[0:1, :],
               l2row[0:1, ts(4 + j, 128)], start=True, stop=True, inc=s_mm)
            p.wait("scalar", s_mm)
            op("scalar", "copy", out=mu_b[:, ts(j, 128)],
               in_=ps_w[1][:, 0:128])
            op("scalar", "copy", out=rs_b[:, ts(j, 128)],
               in_=ps_w[2][:, 0:128], inc=s_ac)
            p.wait("tensor", s_ac)
        x3 = x_fm.rearrange("(a b) c -> b a c", b=128)
        for hf in range(2):
            ld(xfm_t[:].rearrange("p (a c) -> p a c", a=4),
               x3[:, hf * 4:(hf + 1) * 4, t0:t0 + TSB])
            ldq.wait_all("vector")
            p.wait("vector", s_ac)
            for kk in range(4):
                k = hf * 4 + kk
                op("vector", "tensor_tensor", out=t1[:],
                   in0=xfm_t[:, ts(kk, TSB)], in1=mu_b[:], op=ALU.subtract)
                op("vector", "tensor_tensor", out=t1[:], in0=t1[:],
                   in1=rs_b[:], op=ALU.mult, inc=s_ve)
                p.wait("scalar", s_ve)
                op("scalar", "activation", out=xnhi[:, ts(k, TSB)],
                   in_=t1[:], func=AF.Copy, inc=s_ac)
                p.wait("vector", s_ac)
                op("vector", "tensor_tensor", out=xnlo[:, ts(k, TSB)],
                   in0=t1[:], in1=xnhi[:, ts(k, TSB)], op=ALU.subtract,
                   inc=s_ve)
            p.wait("sync", s_ve)
        if t0 >= TAIL:
            p.wait("scalar", s_ve)
            for k in range(8):
                st(xn_d[0, k][:, t0 - TAIL:t0 - TAIL + TSB],
                   xnhi[:, ts(k, TSB)])
                st(xn_d[1, k][:, t0 - TAIL:t0 - TAIL + TSB],
                   xnlo[:, ts(k, TSB)])

    def proj_conv(sbi, oc_limit=25, do_tail=True, tail_lim=99,
                  bank_override=None):
        mm_marks = []
        copy_marks = []
        ld_marks = []
        for oc in range(oc_limit):
            bank = bank_override or ps_w[oc % 2]
            ncols = 128 if oc < 24 else 32
            buf = wkl[oc % 2]
            if len(mm_marks) >= 2:
                p.wait("sync", s_mm, mm_marks[-2])
                p.wait("scalar", s_mm, mm_marks[-2])
            if ncols == 128:
                m1 = ldq.dma(
                    buf[:, 0:4 * 128].rearrange("p (k c) -> p k c", k=4),
                    wlo_slab[oc][0:4].rearrange("k p c -> p k c"))
                m2 = ld2q.dma(
                    buf[:, 4 * 128:8 * 128].rearrange("p (k c) -> p k c",
                                                      k=4),
                    wlo_slab[oc][4:8].rearrange("k p c -> p k c"))
            else:
                m1 = ldq.dma(
                    buf[:, 0:128].rearrange("p (k c) -> p k c", k=4),
                    wlo_slab[oc][0:4, :, 0:32].rearrange("k p c -> p k c"))
                m2 = ld2q.dma(
                    buf[:, 128:256].rearrange("p (k c) -> p k c", k=4),
                    wlo_slab[oc][4:8, :, 0:32].rearrange("k p c -> p k c"))
            ld_marks.append((m1, m2))
            if len(copy_marks) >= 2:
                p.wait("tensor", s_ac, copy_marks[-2])
            for k in range(8):
                hi = (w_hi[:, k * NW + oc * 128:k * NW + oc * 128 + 128]
                      if oc < 24 else wbeta_s[:, k * 32:(k + 1) * 32])
                op("tensor", "matmul", bank[0:ncols, 0:TSB], hi,
                   xnhi[:, ts(k, TSB)], start=(k == 0), stop=False)
                op("tensor", "matmul", bank[0:ncols, 0:TSB], hi,
                   xnlo[:, ts(k, TSB)], start=False, stop=False)
            for sem, tgt in ld_marks[-1]:
                p.wait("tensor", sem, tgt)
            for k in range(8):
                op("tensor", "matmul", bank[0:ncols, 0:TSB],
                   buf[:, k * ncols:(k + 1) * ncols], xnhi[:, ts(k, TSB)],
                   start=False, stop=(k == 7), inc=(s_mm if k == 7 else None))
            mm_marks.append(s_mm.n)
            if not do_tail:
                copy_marks.append(s_ac.n)
                continue
            if oc < 24:
                if tail_lim > 0 and tail_lim != 115:
                    p.wait("scalar", s_mm)
                    op("scalar", "copy", out=raw[:, 4:4 + TSB],
                       in_=bank[0:128, 0:TSB], inc=s_ac)
                    copy_marks.append(s_ac.n)
                dst = (pout[:, ts(oc, TSB)] if oc < 16
                       else v_out[:, ts(oc - 16, TSB)])
                if tail_lim == 101:
                    p.wait("vector", s_mm)
                    op("vector", "tensor_copy", out=dst,
                       in_=bank[0:128, 0:TSB], inc=s_ve)
                elif tail_lim == 102:
                    p.wait("vector", s_mm)
                    op("vector", "tensor_scalar", out=dst,
                       in0=xnhi[:, 0:TSB],
                       scalar1=cw_s[:, oc * 4 + 3:oc * 4 + 4], scalar2=None,
                       op0=ALU.mult, inc=s_ve)
                elif tail_lim == 103:
                    p.wait("vector", s_mm)
                    op("vector", "tensor_scalar", out=sq2[:, 0:TSB],
                       in0=bank[0:128, 0:TSB],
                       scalar1=cw_s[:, oc * 4 + 3:oc * 4 + 4], scalar2=None,
                       op0=ALU.mult, inc=s_ve)
                elif tail_lim == 104:
                    p.wait("vector", s_mm)
                elif tail_lim == 105:
                    op("vector", "tensor_copy", out=sq2[:, 0:TSB],
                       in_=xnhi[:, 0:TSB], inc=s_ve)
                elif tail_lim == 115:
                    p.wait("vector", s_mm)
                    op("vector", "tensor_scalar", out=dst,
                       in0=bank[0:128, 0:TSB],
                       scalar1=cw_s[:, oc * 4 + 3:oc * 4 + 4], scalar2=None,
                       op0=ALU.mult, inc=s_ve)
                elif tail_lim == 110:
                    p.wait("vector", s_ac)
                    op("vector", "tensor_scalar", out=sq2[:, 0:TSB],
                       in0=bank[0:128, 0:TSB],
                       scalar1=cw_s[:, oc * 4 + 3:oc * 4 + 4], scalar2=None,
                       op0=ALU.mult, inc=s_ve)
                elif tail_lim == 111:
                    p.wait("gpsimd", s_mm)
                    op("gpsimd", "tensor_scalar", out=sq2[:, 0:TSB],
                       in0=bank[0:128, 0:TSB],
                       scalar1=cw_s[:, oc * 4 + 3:oc * 4 + 4], scalar2=None,
                       op0=ALU.mult, inc=s_gp)
                elif tail_lim == 106:
                    p.wait("vector", s_mm)
                    op("vector", "tensor_copy", out=sq2[:, 0:128],
                       in_=bank[0:128, 0:128], inc=s_ve)
                elif tail_lim == 107:
                    p.wait("vector", s_mm)
                    for q4 in range(4):
                        op("vector", "tensor_copy",
                           out=sq2[:, ts(q4, 128)],
                           in_=bank[0:128, ts(q4, 128)],
                           inc=(s_ve if q4 == 3 else None))
                elif tail_lim > 1:
                    p.wait("vector", s_ac)
                    op("vector", "tensor_scalar", out=dst,
                       in0=raw[:, 4:4 + TSB],
                       scalar1=cw_s[:, oc * 4 + 3:oc * 4 + 4], scalar2=None,
                       op0=ALU.mult, inc=s_ve)
                if 2 < tail_lim <= 99:
                    op("vector", "tensor_copy", out=raw[:, 0:4],
                       in_=hist[:, oc * 4:oc * 4 + 4])
                for jt in range(3):
                    if 3 + jt < tail_lim <= 99:
                        op("vector", "scalar_tensor_tensor", out=dst,
                           in0=raw[:, 1 + jt:1 + jt + TSB],
                           scalar=cw_s[:, oc * 4 + jt:oc * 4 + jt + 1],
                           in1=dst, op0=ALU.mult, op1=ALU.add)
                if 6 < tail_lim <= 99:
                    op("vector", "tensor_copy",
                       out=hist[:, oc * 4:oc * 4 + 4],
                       in_=raw[:, TSB:TSB + 4], inc=s_ve)
                if 7 < tail_lim <= 99:
                    p.wait("scalar", s_ve)
                    op("scalar", "activation", out=dst, in_=dst, func=AF.Silu,
                       inc=s_ac)
            else:
                p.wait("scalar", s_mm)
                op("scalar", "activation", out=beta_r[:],
                   in_=bank[0:4, 0:TSB], func=AF.Sigmoid, inc=s_ac)
                copy_marks.append(s_ac.n)
        p.wait("vector", s_ac)
        op("vector", "engine_nop", inc=s_ve)

    def l2norm(sbi):
        for tnsr in range(2):
            for h in range(H):
                oc0 = tnsr * 8 + 2 * h
                bank = ps_w[(tnsr * H + h) % 2]
                p.wait("scalar", s_ve)
                op("scalar", "activation", out=sq2[:, 0:TSB],
                   in_=pout[:, ts(oc0, TSB)], func=AF.Square)
                op("scalar", "activation", out=sq2[:, TSB:2 * TSB],
                   in_=pout[:, ts(oc0 + 1, TSB)], func=AF.Square, inc=s_ac)
                p.wait("tensor", s_ac)
                p.wait("tensor", s_ve)
                op("tensor", "matmul", bank[:, 0:TSB], ones_s[:],
                   sq2[:, 0:TSB], start=True, stop=False)
                op("tensor", "matmul", bank[:, 0:TSB], ones_s[:],
                   sq2[:, TSB:2 * TSB], start=False, stop=True, inc=s_mm)
                p.wait("scalar", s_mm)
                op("scalar", "activation", out=t1[:], in_=bank[:, 0:TSB],
                   func=AF.Ln, bias=EPS_L2)
                op("scalar", "activation", out=t1[:], in_=t1[:], func=AF.Exp,
                   scale=-0.5, inc=s_ac)
                p.wait("vector", s_ac)
                op("vector", "tensor_tensor", out=pout[:, ts(oc0, TSB)],
                   in0=pout[:, ts(oc0, TSB)], in1=t1[:], op=ALU.mult)
                op("vector", "tensor_tensor", out=pout[:, ts(oc0 + 1, TSB)],
                   in0=pout[:, ts(oc0 + 1, TSB)], in1=t1[:], op=ALU.mult,
                   inc=s_ve)

    def delta(sbi):
        t0 = sbi * TSB
        p.wait("tensor", s_ve)
        p.wait("tensor", s_ac)
        for ci in range(NCPS):
            op("tensor", "transpose", ps_w[3][:, ts(ci, 4)],
               beta_r[:, ts(ci, 128)], ident_s[0:4, 0:4],
               inc=(s_mm if ci == NCPS - 1 else None))
        p.wait("scalar", s_mm)
        op("scalar", "copy", out=beta_tm[:], in_=ps_w[3][:, 0:16], inc=s_ac)

        for h in range(H):
            gc0 = sbi * NCPS
            tail0 = gc0 * CK >= TAIL

            def qsl(c, ci):
                base = (2 * h + c) * TSB + ci * 128
                return pout[:, base:base + 128]

            def ksl(c, ci):
                base = (8 + 2 * h + c) * TSB + ci * 128
                return pout[:, base:base + 128]

            def vsl(c, ci):
                base = (2 * h + c) * TSB + ci * 128
                return v_out[:, base:base + 128]

            p.wait("tensor", s_ac)
            p.wait("tensor", s_ve)
            for ci in range(NCPS):
                op("tensor", "transpose", ps_w[ci][:, 0:128], ksl(0, ci),
                   ident_s[:])
                op("tensor", "transpose", ps_w[ci][:, 128:256], ksl(1, ci),
                   ident_s[:])
                op("tensor", "transpose", ps_w[ci][:, 256:384], vsl(0, ci),
                   ident_s[:])
                op("tensor", "transpose", ps_w[ci][:, 384:512], vsl(1, ci),
                   ident_s[:], inc=(s_mm if ci == NCPS - 1 else None))
            p.wait("scalar", s_mm)
            p.wait("vector", s_mm)
            for ci in range(NCPS):
                eng, cp, sm = (("scalar", "copy", s_ac) if ci < 2
                               else ("vector", "tensor_copy", s_ve))
                op(eng, cp, out=kntm4[:, ts(ci, DH)],
                   in_=ps_w[ci][:, 0:256])
                op(eng, cp, out=vbtm4[:, ts(ci, DH)],
                   in_=ps_w[ci][:, 256:512],
                   inc=(sm if ci % 2 == 1 else None))
            p.wait("scalar", s_ve)
            for ci in range(NCPS):
                st(v_pad[64 + t0 + ci * 128:64 + t0 + (ci + 1) * 128,
                         h * DH:(h + 1) * DH], vbtm4[:, ts(ci, DH)])
            stq.wait_all("vector")
            p.wait("vector", s_ac)
            for ci in range(NCPS):
                bcol = beta_tm[:, ci * 4 + h:ci * 4 + h + 1]
                op("vector", "tensor_scalar", out=kbtm4[:, ts(ci, DH)],
                   in0=kntm4[:, ts(ci, DH)], scalar1=bcol, scalar2=None,
                   op0=ALU.mult)
                op("vector", "tensor_scalar", out=vbtm4[:, ts(ci, DH)],
                   in0=vbtm4[:, ts(ci, DH)], scalar1=bcol, scalar2=None,
                   op0=ALU.mult)
            op("vector", "engine_nop", inc=s_ve)
            p.wait("tensor", s_ve)
            for ci in range(NCPS):
                last = (ci == NCPS - 1)
                op("tensor", "matmul", ps_w[ci][:, 0:128], ksl(0, ci),
                   ksl(0, ci), start=True, stop=False)
                op("tensor", "matmul", ps_w[ci][:, 0:128], ksl(1, ci),
                   ksl(1, ci), start=False, stop=True,
                   inc=(s_mm if last and not tail0 else None))
                if tail0:
                    op("tensor", "matmul", ps_w[ci][:, 128:256], ksl(0, ci),
                       qsl(0, ci), start=True, stop=False)
                    op("tensor", "matmul", ps_w[ci][:, 128:256], ksl(1, ci),
                       qsl(1, ci), start=False, stop=True,
                       inc=(s_mm if last else None))
            p.wait("vector", s_mm)
            for ci in range(NCPS):
                bcol = beta_tm[:, ci * 4 + h:ci * 4 + h + 1]
                op("vector", "tensor_scalar", out=a4[:, ts(ci, 128)],
                   in0=ps_w[ci][:, 0:128], scalar1=bcol, scalar2=None,
                   op0=ALU.mult)
                op("vector", "tensor_tensor", out=a4[:, ts(ci, 128)],
                   in0=a4[:, ts(ci, 128)], in1=mlow_s[:], op=ALU.mult)
                if tail0:
                    op("vector", "tensor_tensor", out=attn4[:, ts(ci, 128)],
                       in0=ps_w[ci][:, 128:256], in1=mup0_s[:], op=ALU.mult)
            op("vector", "engine_nop", inc=s_ve)
            p.wait("tensor", s_ve)
            for ci in range(NCPS):
                op("tensor", "transpose", ps_w[ci][:, 256:384],
                   a4[:, ts(ci, 128)], ident_s[:],
                   inc=(s_mm if ci == NCPS - 1 else None))
            p.wait("scalar", s_mm)
            for ci in range(NCPS):
                op("scalar", "copy", out=at4[:, ts(ci, 128)],
                   in_=ps_w[ci][:, 256:384],
                   inc=(s_ac if ci == NCPS - 1 else None))
            p.wait("vector", s_ac)
            for ci in range(NCPS):
                op("vector", "tensor_tensor", out=tta[:, ts(ci, 128)],
                   in0=at4[:, ts(ci, 128)], in1=ident_s[:], op=ALU.add)
            op("vector", "engine_nop", inc=s_ve)
            tt_cur, tt_nxt = tta, ttb
            for j in range(1, 7):
                p.wait("tensor", s_ve)
                p.wait("tensor", s_ac)
                for ci in range(NCPS):
                    last = (ci == NCPS - 1)
                    op("tensor", "matmul", ps_w[ci][:, 0:128],
                       at4[:, ts(ci, 128)], a4[:, ts(ci, 128)],
                       start=True, stop=True,
                       inc=(s_mm if last and j == 6 else None))
                    if j < 6:
                        op("tensor", "matmul", ps_w[ci][:, 128:256],
                           a4[:, ts(ci, 128)], at4[:, ts(ci, 128)],
                           start=True, stop=True,
                           inc=(s_mm if last else None))
                p.wait("scalar", s_mm)
                p.wait("vector", s_mm)
                for ci in range(NCPS):
                    last = (ci == NCPS - 1)
                    eng, cp, sm = (("scalar", "copy", s_ac) if ci < 2
                                   else ("vector", "tensor_copy", s_ve))
                    op(eng, cp, out=a4[:, ts(ci, 128)],
                       in_=ps_w[ci][:, 0:128],
                       inc=(sm if ci % 2 == 1 and j == 6 else None))
                    if j < 6:
                        op(eng, cp, out=at4[:, ts(ci, 128)],
                           in_=ps_w[ci][:, 128:256],
                           inc=(sm if ci % 2 == 1 else None))
                p.wait("tensor", s_ac)
                p.wait("tensor", s_ve)
                for ci in range(NCPS):
                    op("tensor", "matmul", ps_w[ci][:, 256:384],
                       a4[:, ts(ci, 128)], tt_cur[:, ts(ci, 128)],
                       start=True, stop=True,
                       inc=(s_mm if ci == NCPS - 1 else None))
                p.wait("vector", s_mm)
                for ci in range(NCPS):
                    op("vector", "scalar_tensor_tensor",
                       out=tt_nxt[:, ts(ci, 128)], in0=ps_w[ci][:, 256:384],
                       scalar=1.0, in1=tt_cur[:, ts(ci, 128)],
                       op0=ALU.mult, op1=ALU.add)
                op("vector", "engine_nop", inc=s_ve)
                tt_cur, tt_nxt = tt_nxt, tt_cur
            p.wait("tensor", s_ve)
            for ci in range(NCPS):
                op("tensor", "matmul", ps_w[ci][:, 0:256],
                   tt_cur[:, ts(ci, 128)], vbtm4[:, ts(ci, DH)],
                   start=True, stop=True)
                op("tensor", "matmul", ps_w[ci][:, 256:384],
                   kbtm4[:, ci * DH:ci * DH + 128], tt_cur[:, ts(ci, 128)],
                   start=True, stop=True)
                op("tensor", "matmul", ps_w[ci][:, 384:512],
                   kbtm4[:, ci * DH + 128:ci * DH + 256],
                   tt_cur[:, ts(ci, 128)], start=True, stop=True,
                   inc=(s_mm if ci == NCPS - 1 else None))
            p.wait("scalar", s_mm)
            p.wait("vector", s_mm)
            for ci in range(NCPS):
                eng, cp, sm = (("scalar", "copy", s_ac) if ci < 2
                               else ("vector", "tensor_copy", s_ve))
                op(eng, cp, out=utm4[:, ts(ci, DH)],
                   in_=ps_w[ci][:, 0:256])
                op(eng, cp, out=wfm4[:, ts(ci, DH)],
                   in_=ps_w[ci][:, 256:512],
                   inc=(sm if ci % 2 == 1 else None))
            for ci in range(NCPS):
                gc = gc0 + ci
                tail = gc * CK >= TAIL
                p.wait("tensor", s_ac)
                p.wait("tensor", s_ve)
                if gc > 0:
                    p.wait("scalar", s_mm)
                    p.wait("scalar", s_ve)
                    p.wait("vector", s_mm)
                    p.wait("vector", s_ac)
                    op("scalar", "copy", out=ssb[:, 0:512],
                       in_=ps_s[h][:, 0:512], inc=s_ac)
                    op("vector", "engine_nop", inc=s_ve)
                    p.wait("tensor", s_ac)
                    p.wait("tensor", s_ve)
                    op("tensor", "matmul", ps_w[0][:, 0:256],
                       wfm4[:, ci * DH:ci * DH + 128], ssb[:, 0:256],
                       start=True, stop=False, skip_group_check=True)
                    op("tensor", "matmul", ps_w[0][:, 0:256],
                       wfm4[:, ci * DH + 128:ci * DH + 256],
                       ssb[:, 256:512], start=False, stop=True,
                       inc=(None if tail else s_mm), skip_group_check=True)
                    if tail:
                        op("tensor", "matmul", ps_w[0][:, 256:512],
                           qsl(0, ci), ssb[:, 0:256], start=True, stop=False,
                           skip_group_check=True)
                        op("tensor", "matmul", ps_w[0][:, 256:512],
                           qsl(1, ci), ssb[:, 256:512], start=False,
                           stop=False, inc=s_mm, skip_group_check=True)
                    p.wait("vector", s_mm)
                    stq.wait_all("vector")
                    op("vector", "tensor_tensor", out=uadj[:],
                       in0=utm4[:, ts(ci, DH)], in1=ps_w[0][:, 0:256],
                       op=ALU.subtract, inc=s_ve)
                    ua = uadj[:]
                    p.wait("tensor", s_ve)
                else:
                    ua = utm4[:, ts(ci, DH)]
                if tail:
                    op("tensor", "matmul", ps_w[0][:, 256:512],
                       attn4[:, ts(ci, 128)], ua, start=False, stop=True,
                       skip_group_check=True)
                op("tensor", "matmul", ps_s[h][:, 0:256],
                   kntm4[:, ci * DH:ci * DH + 128], ua,
                   start=(gc == 0), stop=False, skip_group_check=True)
                op("tensor", "matmul", ps_s[h][:, 256:512],
                   kntm4[:, ci * DH + 128:ci * DH + 256], ua,
                   start=False, stop=(gc == NCK - 1), inc=s_mm,
                   skip_group_check=True)
                if tail:
                    p.wait("vector", s_mm)
                    op("vector", "tensor_copy", out=doutb[:],
                       in_=ps_w[0][:, 256:512], inc=s_ve)
                    p.wait("scalar", s_ve)
                    trow = gc * CK - TAIL
                    st(delta_d[trow:trow + 128, h * DH:(h + 1) * DH],
                       doutb[:])
                    p.wait("tensor", s_ve)

    for sbi in range(NSB):
        ln_and_xn(sbi)
        p.barrier(bar)
        if stop_after == f"PH1:ln:{sbi}":
            return _finish()
        if sbi == 0 and stop_after and stop_after.startswith("PH1:pj"):
            kind, n = stop_after.split(":")[1], int(stop_after.split(":")[2])
            if kind == "pjtl" and n == 108:
                p.wait("vector", s_mm)
                op("vector", "tensor_copy", out=sq2[:, 0:128],
                   in_=ps_w[0][0:128, 0:128], inc=s_ve)
            elif kind == "pjtl" and n in (112, 114):
                p.wait("tensor", s_ve)
                p.wait("tensor", s_ac)
                bank = ps_w[0]
                for k in range(8):
                    for r in range(3):
                        if n == 112:
                            lh = w_hi[:, k * 3072:k * 3072 + 128]
                        else:
                            lh = xnhi[:, ts(k, TSB)][:, 0:128]
                        rhs = xnlo[:, ts(k, TSB)] if r == 1 \
                            else xnhi[:, ts(k, TSB)]
                        op("tensor", "matmul", bank[0:128, 0:TSB], lh, rhs,
                           start=(k == 0 and r == 0),
                           stop=(k == 7 and r == 2),
                           inc=(s_mm if (k == 7 and r == 2) else None))
                p.wait("vector", s_mm)
                op("vector", "tensor_copy", out=sq2[:, 0:TSB],
                   in_=bank[0:128, 0:TSB], inc=s_ve)
            elif kind == "pjtl" and n == 109:
                proj_conv(0, oc_limit=1, do_tail=True, tail_lim=103,
                          bank_override=ps_s[0])
            elif kind == "pjtl":
                proj_conv(0, oc_limit=1, do_tail=True, tail_lim=n)
            else:
                proj_conv(0, oc_limit=n + 1, do_tail=(kind == "pjoc"))
            p.barrier(bar)
            return _finish()
        proj_conv(sbi)
        p.barrier(bar)
        if stop_after == f"PH1:proj:{sbi}":
            return _finish()
        l2norm(sbi)
        p.barrier(bar)
        if stop_after == f"PH1:l2:{sbi}":
            return _finish()
        delta(sbi)
        p.barrier(bar)
        if stop_after == f"PH1:sb:{sbi}":
            return _finish()
        if "d_qn" in dbg:
            for k in range(16):
                st(dbg["d_qn"][sbi][:, ts(k, TSB)], pout[:, ts(k, TSB)])
        if "d_v" in dbg:
            for k in range(8):
                st(dbg["d_v"][sbi][:, ts(k, TSB)], v_out[:, ts(k, TSB)])
        if "d_beta" in dbg:
            st(dbg["d_beta"][sbi], beta_r[:])
        if dbg:
            stq.wait_all("scalar")
            p.barrier(bar)

    if "d_delta" in dbg:
        for r in range(8):
            ld(pout[:, 0:4096].rearrange("p (a c) -> p a c", a=4),
               delta_d[r * 512:(r + 1) * 512, :].rearrange(
                   "(a b) c -> b a c", b=128))
            ldq.wait_all("scalar")
            for j in range(4):
                st(dbg["d_delta"][r * 512 + j * 128:
                                  r * 512 + (j + 1) * 128, :],
                   pout[:, ts(j, D)])
            stq.wait_all("sync")
        p.barrier(bar)
    if stop_after == "PH1":
        return _finish()

    vp3 = v_pad.rearrange("(u p) c -> p u c", p=64)
    mm_mark = [0]
    for cs in range(8):
        p.wait("gpsimd", s_mm)
        g2(rbuf[0:64, 0:8192].rearrange("p (u c) -> p u c", u=64),
           vp3[:, 64:128, cs * 128:(cs + 1) * 128])
        g2(rbuf[64:128, 0:8192].rearrange("p (u c) -> p u c", u=64),
           vp3[:, 65:129, cs * 128:(cs + 1) * 128])
        gpq.wait_all("tensor")
        for ch in range(128):
            gch = cs * 128 + ch
            tb = ttoep[ch % 2]
            if mm_mark[0] > 0:
                p.wait("sync", s_mm, mm_mark[0])
            ld(tb[:], fir_toep[gch])
            mm_mark[0] = s_mm.n
            ldq.wait_all("tensor")
            rap = rbuf.rearrange("p (u c) -> p c u", u=64)[:, ch, :]
            op("tensor", "matmul", ps_w[ch % 4][:, 0:64], tb[:], rap,
               start=True, stop=True, inc=s_mm)
            if ch % 2 == 0:
                p.wait("vector", s_mm)
                op("vector", "tensor_copy",
                   out=scat.rearrange("p (u c) -> p c u", u=64)[:, ch, :],
                   in_=ps_w[ch % 4][:, 0:64], inc=s_ve)
                p.wait("tensor", s_ve)
            else:
                p.wait("scalar", s_mm)
                op("scalar", "copy",
                   out=scat.rearrange("p (u c) -> p c u", u=64)[:, ch, :],
                   in_=ps_w[ch % 4][:, 0:64], inc=s_ac)
                p.wait("tensor", s_ac)
        p.wait("gpsimd", s_ve)
        p.wait("gpsimd", s_ac)
        ld3 = l_d.rearrange("(u o) c -> o u c", o=64)
        g2(ld3[:, :, cs * 128:(cs + 1) * 128],
           scat[0:64, 0:8192].rearrange("p (u c) -> p u c", u=64))
        g2(ld3[:, :, D + cs * 128:D + (cs + 1) * 128],
           scat[64:128, 0:8192].rearrange("p (u c) -> p u c", u=64))
        gpq.wait_all("gpsimd")
    p.barrier(bar)
    if "d_L" in dbg:
        for r in range(8):
            ld(pout[:, 0:8192].rearrange("p (a c) -> p a c", a=4),
               l_d[r * 512:(r + 1) * 512, :].rearrange(
                   "(a b) c -> b a c", b=128))
            ldq.wait_all("scalar")
            for j in range(4):
                st(dbg["d_L"][r * 512 + j * 128:r * 512 + (j + 1) * 128, :],
                   pout[:, j * 2048:(j + 1) * 2048])
            stq.wait_all("sync")
        p.barrier(bar)
    if stop_after == "PH2":
        return _finish()

    d_buf = w_hiF[:, 0:4096]
    v_buf = w_hiF[:, 4096:8192]
    hdn = w_hiF[:, 8192:12288]
    L_buf = pout
    base_s = v_out
    o_buf = big3
    stt_tm = scr
    stt_fm = xfm_t[0:16, 0:4 * TSB]

    def softmax_ocombine(sl):
        t0g = sl * TSB
        for bk in range(4):
            p.wait("vector", s_ac)
            op("vector", "tensor_copy", out=fwc[:].rearrange(
                "p (h j) -> p h j", h=4),
               in_=lgtm[:].rearrange("p (h b j) -> p b h j", h=4, b=4)[:, bk],
               inc=s_ve)
            op("vector", "tensor_reduce", out=mx4[:, 0:4],
               in_=fwc[:].rearrange("p (h j) -> p h j", h=4), axis=AX.X,
               op=ALU.max)
            for h in range(H):
                op("vector", "tensor_scalar", out=fwc[:, ts(h, 4)],
                   in0=fwc[:, ts(h, 4)], scalar1=mx4[:, h:h + 1],
                   scalar2=None, op0=ALU.subtract)
            op("vector", "engine_nop", inc=s_ve)
            p.wait("scalar", s_ve)
            op("scalar", "activation", out=fwc[:], in_=fwc[:], func=AF.Exp,
               inc=s_ac)
            p.wait("vector", s_ac)
            op("vector", "tensor_reduce", out=mx4[:, 4:8],
               in_=fwc[:].rearrange("p (h j) -> p h j", h=4), axis=AX.X,
               op=ALU.add, inc=s_ve)
            p.wait("scalar", s_ve)
            op("scalar", "activation", out=mx4[:, 4:8], in_=mx4[:, 4:8],
               func=AF.Ln)
            op("scalar", "activation", out=mx4[:, 4:8], in_=mx4[:, 4:8],
               func=AF.Exp, scale=-1.0, inc=s_ac)
            p.wait("vector", s_ac)
            for h in range(H):
                op("vector", "tensor_scalar", out=fwc[:, ts(h, 4)],
                   in0=fwc[:, ts(h, 4)], scalar1=mx4[:, 4 + h:5 + h],
                   scalar2=None, op0=ALU.mult)
            for h in range(H):
                op("vector", "scalar_tensor_tensor",
                   out=fwc[:, h * 4:h * 4 + 1], in0=sc_rep[:, 1:2],
                   scalar=1.0, in1=fwc[:, h * 4:h * 4 + 1],
                   op0=ALU.mult, op1=ALU.add)
                op("vector", "scalar_tensor_tensor",
                   out=fwc[:, h * 4 + 1:h * 4 + 2], in0=sc_rep[:, 2:3],
                   scalar=1.0, in1=fwc[:, h * 4 + 1:h * 4 + 2],
                   op0=ALU.mult, op1=ALU.add)
            op("vector", "engine_nop", inc=s_ve)
            if "d_fw" in dbg:
                p.wait("scalar", s_ve)
                st(dbg["d_fw"][sl * 4 + bk], fwc[:])
                stq.wait_all("scalar")
                stq.wait_all("vector")
            j = bk
            for h in range(H):
                srt = L_buf[:, j * 2048 + 1024 + h * DH:
                            j * 2048 + 1024 + (h + 1) * DH]
                lng = L_buf[:, j * 2048 + h * DH:j * 2048 + (h + 1) * DH]
                dlt = d_buf[:, j * 1024 + h * DH:j * 1024 + (h + 1) * DH]
                vdr = v_buf[:, j * 1024 + h * DH:j * 1024 + (h + 1) * DH]
                odst = o_buf[:, j * 1024 + h * DH:j * 1024 + (h + 1) * DH]
                op("vector", "tensor_scalar", out=odst, in0=srt,
                   scalar1=fwc[:, h * 4:h * 4 + 1], scalar2=None,
                   op0=ALU.mult)
                op("vector", "scalar_tensor_tensor", out=odst, in0=lng,
                   scalar=fwc[:, h * 4 + 1:h * 4 + 2], in1=odst,
                   op0=ALU.mult, op1=ALU.add)
                op("vector", "scalar_tensor_tensor", out=odst, in0=dlt,
                   scalar=fwc[:, h * 4 + 2:h * 4 + 3], in1=odst,
                   op0=ALU.mult, op1=ALU.add)
                op("vector", "scalar_tensor_tensor", out=odst, in0=vdr,
                   scalar=fwc[:, h * 4 + 3:h * 4 + 4], in1=odst,
                   op0=ALU.mult, op1=ALU.add, inc=s_ve)
                p.wait("scalar", s_ve)
                op("scalar", "activation", out=sq2[:, 0:DH], in_=odst,
                   func=AF.Square,
                   accum_out=rms_c[:, j * 4 + h:j * 4 + h + 1], inc=s_ac)
        p.wait("vector", s_ac)
        op("vector", "tensor_scalar", out=rms_c[:], in0=rms_c[:],
           scalar1=1.0 / DH, scalar2=None, op0=ALU.mult, inc=s_ve)
        p.wait("scalar", s_ve)
        op("scalar", "activation", out=rms_c[:], in_=rms_c[:], func=AF.Ln,
           bias=EPS_LN)
        op("scalar", "activation", out=rms_c[:], in_=rms_c[:], func=AF.Exp,
           scale=-0.5, inc=s_ac)
        p.wait("vector", s_ac)
        for j in range(4):
            for h in range(H):
                odst = o_buf[:, j * 1024 + h * DH:j * 1024 + (h + 1) * DH]
                op("vector", "tensor_scalar", out=odst, in0=odst,
                   scalar1=rms_c[:, j * 4 + h:j * 4 + h + 1], scalar2=None,
                   op0=ALU.mult)
                op("vector", "tensor_tensor", out=odst, in0=odst,
                   in1=onorm_b[:], op=ALU.mult)
        op("vector", "engine_nop", inc=s_ve)
        if "d_o" in dbg:
            p.wait("scalar", s_ve)
            for j in range(4):
                st(dbg["d_o"][t0g + j * 128:t0g + (j + 1) * 128, :],
                   o_buf[:, ts(j, D)])
            stq.wait_all("scalar")
            stq.wait_all("vector")
        p.wait("scalar", s_ve)
        for j in range(4):
            op("scalar", "activation", out=xnhi[:, j * 1024:(j + 1) * 1024]
               .bitcast(BF16)[:, 0:1024] if False else
               xnhi[:, ts(2 * j, TSB)], in_=o_buf[:, j * 1024:j * 1024 + 512],
               func=AF.Copy)
            op("scalar", "activation", out=xnhi[:, ts(2 * j + 1, TSB)],
               in_=o_buf[:, j * 1024 + 512:(j + 1) * 1024], func=AF.Copy,
               inc=s_ac)
        p.wait("vector", s_ac)
        for j in range(4):
            op("vector", "tensor_tensor", out=xnlo[:, ts(2 * j, TSB)],
               in0=o_buf[:, j * 1024:j * 1024 + 512],
               in1=xnhi[:, ts(2 * j, TSB)], op=ALU.subtract)
            op("vector", "tensor_tensor", out=xnlo[:, ts(2 * j + 1, TSB)],
               in0=o_buf[:, j * 1024 + 512:(j + 1) * 1024],
               in1=xnhi[:, ts(2 * j + 1, TSB)], op=ALU.subtract,
               inc=(s_ve if j == 3 else None))
        p.wait("scalar", s_ve)
        for j in range(4):
            st(o_d[0, t0g + j * 128:t0g + (j + 1) * 128, :],
               xnhi[:, ts(2 * j, TSB)].bitcast(BF16)[:, 0:512]
               if False else xnhi[:, 2 * j * TSB:2 * j * TSB + 1024])
            st(o_d[1, t0g + j * 128:t0g + (j + 1) * 128, :],
               xnlo[:, 2 * j * TSB:2 * j * TSB + 1024])
        stq.wait_all("sync")
        stq.wait_all("vector")
        stq.wait_all("gpsimd")

    for sl in range(8):
        t0 = sl * TSB
        ld(d_buf.rearrange("p (a c) -> p a c", a=4),
           delta_d[t0:t0 + TSB, :].rearrange("(a b) c -> b a c", b=128))
        ld(v_buf.rearrange("p (a c) -> p a c", a=4),
           v_pad[64 + TAIL + t0:64 + TAIL + t0 + TSB, :].rearrange(
               "(a b) c -> b a c", b=128))
        ld(L_buf[:, 0:8192].rearrange("p (a c) -> p a c", a=4),
           l_d[t0:t0 + TSB, :].rearrange("(a b) c -> b a c", b=128))
        ld(xnhi[:].rearrange("p (a c) -> p a c", a=8),
           xn_d[0][:, :, t0:t0 + TSB].rearrange("a b c -> b a c"))
        ld(xnlo[:].rearrange("p (a c) -> p a c", a=8),
           xn_d[1][:, :, t0:t0 + TSB].rearrange("a b c -> b a c"))
        ldq.wait_all("vector")
        ldq.wait_all("scalar")
        ldq.wait_all("tensor")
        op("scalar", "activation", out=stat_c[0:1, 0:1],
           in_=scal_s[0:1, :], func=AF.Exp)
        op("scalar", "activation", out=stat_c[0:1, 0:1],
           in_=stat_c[0:1, 0:1], func=AF.Ln, bias=1.0)
        op("scalar", "activation", out=stat_c[0:1, 1:2],
           in_=stat_c[0:1, 0:1], func=AF.Ln, bias=1e-4, inc=s_ac)
        p.wait("tensor", s_ac)
        op("tensor", "matmul", ps_w[2][0:4, 0:1], ones_s[0:1, 0:4],
           stat_c[0:1, 1:2], start=True, stop=True, inc=s_mm)
        p.wait("scalar", s_mm)
        op("scalar", "activation", out=stat_c[0:4, 2:3],
           in_=ps_w[2][0:4, 0:1], func=AF.Exp, scale=-1.0, inc=s_ac)
        p.wait("vector", s_ac)
        op("vector", "tensor_tensor", out=stat_c[0:4, 3:4], in0=b2_s[:],
           in1=stat_c[0:4, 2:3], op=ALU.mult, inc=s_ve)
        p.wait("scalar", s_ve)
        for j in range(4):
            p.wait("scalar", s_mm)
            p.wait("vector", s_mm)
            for h in range(H):
                srcs = [L_buf[:, j * 2048 + 1024 + h * DH:
                              j * 2048 + 1024 + (h + 1) * DH],
                        L_buf[:, j * 2048 + h * DH:j * 2048 + (h + 1) * DH],
                        d_buf[:, j * 1024 + h * DH:j * 1024 + (h + 1) * DH],
                        v_buf[:, j * 1024 + h * DH:j * 1024 + (h + 1) * DH]]
                for si, src in enumerate(srcs):
                    c0 = h * 16 + si * 4
                    op("vector", "tensor_scalar", out=t1[:, 0:DH], in0=src,
                       scalar1=1.0 / DH, scalar2=None, op0=ALU.mult,
                       op1=ALU.add, accum_out=stt_tm[:, c0:c0 + 1])
                    op("scalar", "activation", out=sq2[:, 0:DH], in_=src,
                       func=AF.Square, accum_out=stt_tm[:, c0 + 1:c0 + 2])
                    op("scalar", "activation", out=sq2[:, DH:2 * DH],
                       in_=src, func=AF.Abs,
                       accum_out=stt_tm[:, c0 + 2:c0 + 3])
            op("vector", "engine_nop", inc=s_ve)
            p.wait("scalar", s_ve)
            p.wait("scalar", s_ac)
            sv = stt_tm[:, 0:64].rearrange("p (a c) -> p a c", a=16)
            op("scalar", "activation", out=sv[:, :, 3], in_=sv[:, :, 1],
               func=AF.Sqrt, inc=s_ac)
            p.wait("vector", s_ac)
            op("vector", "tensor_tensor", out=t1[:, 0:16], in0=sv[:, :, 0],
               in1=sv[:, :, 0], op=ALU.mult)
            op("vector", "tensor_scalar", out=sv[:, :, 1], in0=sv[:, :, 1],
               scalar1=1.0 / DH, scalar2=None, op0=ALU.mult)
            op("vector", "tensor_tensor", out=sv[:, :, 1], in0=sv[:, :, 1],
               in1=t1[:, 0:16], op=ALU.subtract)
            op("vector", "tensor_scalar", out=sv[:, :, 2], in0=sv[:, :, 2],
               scalar1=1.0 / DH, scalar2=None, op0=ALU.mult, inc=s_ve)
            if "d_stats" in dbg:
                p.wait("scalar", s_ve)
                st(dbg["d_stats"][sl * 4 + j], stt_tm[:, 0:64])
                stq.wait_all("scalar")
                stq.wait_all("vector")
            p.wait("tensor", s_ve)
            p.wait("tensor", s_ac)
            for h in range(H):
                op("tensor", "transpose", ps_w[3][0:16, ts(h, 128)],
                   stt_tm[:, ts(h, 16)], ident_s[:],
                   inc=(s_mm if h == H - 1 else None))
            p.wait("vector", s_mm)
            for h in range(H):
                op("vector", "tensor_copy",
                   out=stt_fm[:, h * TSB + j * 128:h * TSB + (j + 1) * 128],
                   in_=ps_w[3][0:16, ts(h, 128)],
                   inc=(s_ve if h == H - 1 else None))
            p.wait("tensor", s_ve)
        for k in range(8):
            p.wait("sync", s_mm)
            ld(whh[0][:], w1h_hi_d[k])
            ld(whh[1][:], w1h_lo_d[k])
            ldq.wait_all("tensor")
            for oc in range(8):
                op("tensor", "matmul", ps_all[oc][:, 0:TSB],
                   whh[0][:, ts(oc, 128)], xnhi[:, ts(k, TSB)],
                   start=(k == 0), stop=False)
                op("tensor", "matmul", ps_all[oc][:, 0:TSB],
                   whh[0][:, ts(oc, 128)], xnlo[:, ts(k, TSB)],
                   start=False, stop=False)
                op("tensor", "matmul", ps_all[oc][:, 0:TSB],
                   whh[1][:, ts(oc, 128)], xnhi[:, ts(k, TSB)],
                   start=False, stop=(k == 7),
                   inc=(s_mm if oc == 7 else None))
        p.wait("scalar", s_mm)
        for oc in range(8):
            op("scalar", "copy", out=base_s[:, ts(oc, TSB)],
               in_=ps_all[oc][:, 0:TSB], inc=(s_ac if oc == 7 else None))
        for h in range(H):
            p.wait("tensor", s_ac)
            p.wait("tensor", s_ve)
            for oc in range(8):
                for j in range(4):
                    op("tensor", "matmul",
                       ps_all[oc][:, j * 128:(j + 1) * 128],
                       w1s_s[:, ts(oc, 128)],
                       stt_fm[:, h * TSB + j * 128:h * TSB + (j + 1) * 128],
                       start=True, stop=True,
                       inc=(s_mm if oc == 7 and j == 3 else None))
            p.wait("vector", s_mm)
            for oc in range(8):
                op("vector", "tensor_tensor", out=hdn[:, ts(oc, TSB)],
                   in0=base_s[:, ts(oc, TSB)], in1=ps_all[oc][:, 0:TSB],
                   op=ALU.add, inc=(s_ve if oc == 7 else None))
            p.wait("scalar", s_ve)
            for oc in range(8):
                op("scalar", "activation", out=hdn[:, ts(oc, TSB)],
                   in_=hdn[:, ts(oc, TSB)], func=AF.Gelu,
                   bias=b1_s[:, oc:oc + 1], inc=(s_ac if oc == 7 else None))
            p.wait("tensor", s_ac)
            for oc in range(8):
                op("tensor", "matmul", ps_w[3][0:4, 0:TSB],
                   w2_s[:, ts(oc, 4)], hdn[:, ts(oc, TSB)],
                   start=(oc == 0), stop=(oc == 7),
                   inc=(s_mm if oc == 7 else None))
            p.wait("vector", s_mm)
            op("vector", "tensor_scalar", out=lgs[:],
               in0=ps_w[3][0:4, 0:TSB], scalar1=stat_c[0:4, 2:3],
               scalar2=stat_c[0:4, 3:4], op0=ALU.mult, op1=ALU.add,
               inc=s_ve)
            p.wait("tensor", s_ve)
            for bk in range(4):
                op("tensor", "transpose", ps_w[2][:, ts(bk, 4)],
                   lgs[:, ts(bk, 128)], ident_s[0:4, 0:4],
                   inc=(s_mm if bk == 3 else None))
            p.wait("vector", s_mm)
            op("vector", "tensor_copy", out=lgtm[:, ts(h, 16)],
               in_=ps_w[2][:, 0:16], inc=s_ve)
            p.wait("tensor", s_ve)
        softmax_ocombine(sl)
        p.barrier(bar)
    if stop_after == "PH3":
        return _finish()

    qrhs5 = w_hi[:, 16384:24576]
    otm_hi = w_hi[:, 0:4096]
    otm_lo = w_hi[:, 4096:8192]
    ofm_hi = xnhi
    ofm_lo = xnlo
    y1 = pout
    y1sq = big3
    cent = w_hiF[:, 4096:8192]
    aa = v_out
    ld(qrhs5.rearrange("p (a c) -> p a c", a=8),
       qrhs_d.rearrange("a b c -> b a c"))
    for tl in range(8):
        t0 = tl * TSB
        ld(otm_hi.rearrange("p (a c) -> p a c", a=4),
           o_d[0][t0:t0 + TSB, :].rearrange("(a b) c -> b a c", b=128))
        ld(otm_lo.rearrange("p (a c) -> p a c", a=4),
           o_d[1][t0:t0 + TSB, :].rearrange("(a b) c -> b a c", b=128))
        ldq.wait_all("tensor")
        for pair in range(2):
            src = otm_hi if pair == 0 else otm_lo
            dst = ofm_hi if pair == 0 else ofm_lo
            for j in range(4):
                for oc in range(8):
                    op("tensor", "transpose",
                       ps_w[(j * 8 + oc) % 4][:, 0:64].bitcast(BF16)[:, 0:128],
                       src[:, j * 1024 + oc * 128:j * 1024 + (oc + 1) * 128],
                       identb[:], inc=s_mm)
                    eng = "vector" if (j * 8 + oc) % 2 == 0 else "scalar"
                    esm = s_ve if (j * 8 + oc) % 2 == 0 else s_ac
                    p.wait(eng, s_mm)
                    meth = "tensor_copy" if eng == "vector" else "copy"
                    op(eng, meth,
                       out=dst[:, oc * TSB + j * 128:oc * TSB + (j + 1) * 128],
                       in_=ps_w[(j * 8 + oc) % 4][:, 0:64]
                       .bitcast(BF16)[:, 0:128], inc=esm)
                    p.wait("tensor", esm)
        for k in range(8):
            p.wait("sync", s_mm)
            ld(whh[0][:], wo_hi_d[k])
            ld(whh[1][:], wo_lo_d[k])
            ldq.wait_all("tensor")
            for oc in range(8):
                op("tensor", "matmul", ps_all[oc][:, 0:TSB],
                   whh[0][:, ts(oc, 128)], ofm_hi[:, ts(k, TSB)],
                   start=(k == 0), stop=False)
                op("tensor", "matmul", ps_all[oc][:, 0:TSB],
                   whh[0][:, ts(oc, 128)], ofm_lo[:, ts(k, TSB)],
                   start=False, stop=False)
                op("tensor", "matmul", ps_all[oc][:, 0:TSB],
                   whh[1][:, ts(oc, 128)], ofm_hi[:, ts(k, TSB)],
                   start=False, stop=(k == 7),
                   inc=(s_mm if oc == 7 else None))
        x3 = x_fm.rearrange("(a b) c -> b a c", b=128)
        ld(xfm_t[:].rearrange("p (a c) -> p a c", a=4),
           x3[:, 0:4, TAIL + t0:TAIL + t0 + TSB])
        ldq.wait_all("vector")
        p.wait("vector", s_mm)
        for oc in range(4):
            op("vector", "scalar_tensor_tensor", out=y1[:, ts(oc, TSB)],
               in0=ps_all[oc][:, 0:TSB], scalar=1.0,
               in1=xfm_t[:, ts(oc, TSB)], op0=ALU.mult, op1=ALU.add,
               inc=(s_ve if oc == 3 else None))
        p.wait("sync", s_ve)
        ld(xfm_t[:].rearrange("p (a c) -> p a c", a=4),
           x3[:, 4:8, TAIL + t0:TAIL + t0 + TSB])
        ldq.wait_all("vector")
        for oc in range(4, 8):
            op("vector", "scalar_tensor_tensor", out=y1[:, ts(oc, TSB)],
               in0=ps_all[oc][:, 0:TSB], scalar=1.0,
               in1=xfm_t[:, ts(oc - 4, TSB)], op0=ALU.mult, op1=ALU.add,
               inc=(s_ve if oc == 7 else None))
        if "d_y1" in dbg:
            p.wait("scalar", s_ve)
            st(dbg["d_y1"][tl], y1[:, 0:4096])
            stq.wait_all("scalar")
            stq.wait_all("vector")
        p.wait("scalar", s_ve)
        for oc in range(8):
            op("scalar", "activation", out=y1sq[:, ts(oc, TSB)],
               in_=y1[:, ts(oc, TSB)], func=AF.Square,
               inc=(s_ac if oc == 7 else None))
        p.wait("tensor", s_ve)
        p.wait("tensor", s_ac)
        for oc in range(8):
            op("tensor", "matmul", ps_w[0][:, 0:TSB], ones_s[:],
               y1[:, ts(oc, TSB)], start=(oc == 0), stop=(oc == 7))
            op("tensor", "matmul", ps_w[1][:, 0:TSB], ones_s[:],
               y1sq[:, ts(oc, TSB)], start=(oc == 0), stop=(oc == 7),
               inc=(s_mm if oc == 7 else None))
        p.wait("scalar", s_mm)
        op("scalar", "activation", out=mu_b[:], in_=ps_w[0][:, 0:TSB],
           func=AF.Copy, scale=1.0 / D)
        op("scalar", "activation", out=rs_b[:], in_=ps_w[1][:, 0:TSB],
           func=AF.Copy, scale=1.0 / D, inc=s_ac)
        p.wait("vector", s_ac)
        op("vector", "tensor_tensor", out=t1[:], in0=mu_b[:], in1=mu_b[:],
           op=ALU.mult)
        op("vector", "tensor_tensor", out=rs_b[:], in0=rs_b[:], in1=t1[:],
           op=ALU.subtract, inc=s_ve)
        p.wait("scalar", s_ve)
        op("scalar", "activation", out=rs_b[:], in_=rs_b[:], func=AF.Ln,
           bias=EPS_LN)
        op("scalar", "activation", out=rs_b[:], in_=rs_b[:], func=AF.Exp,
           scale=-0.5, inc=s_ac)
        p.wait("vector", s_ac)
        for oc in range(8):
            op("vector", "tensor_tensor", out=cent[:, ts(oc, TSB)],
               in0=y1[:, ts(oc, TSB)], in1=mu_b[:], op=ALU.subtract,
               inc=(s_ve if oc == 7 else None))
        p.wait("scalar", s_ve)
        for oc in range(8):
            op("scalar", "activation", out=aa[:, ts(oc, TSB)],
               in_=cent[:, ts(oc, TSB)], func=AF.Abs,
               inc=(s_ac if oc == 7 else None))
        p.wait("vector", s_ac)
        for j in range(4):
            op("vector", "tensor_tensor", out=aa[:, ts(j, TSB)],
               in0=aa[:, ts(2 * j, TSB)], in1=aa[:, ts(2 * j + 1, TSB)],
               op=ALU.max)
        for j in range(2):
            op("vector", "tensor_tensor", out=aa[:, ts(j, TSB)],
               in0=aa[:, ts(2 * j, TSB)], in1=aa[:, ts(2 * j + 1, TSB)],
               op=ALU.max)
        op("vector", "tensor_tensor", out=aa[:, 0:TSB], in0=aa[:, 0:TSB],
           in1=aa[:, TSB:2 * TSB], op=ALU.max, inc=s_ve)
        for bk in range(4):
            p.wait("tensor", s_ve)
            p.wait("tensor", s_ac)
            op("tensor", "transpose", ps_w[2][:, 0:128],
               aa[:, bk * 128:(bk + 1) * 128], ident_s[:], inc=s_mm)
            p.wait("vector", s_mm)
            op("vector", "reduce_max", out=stat_c[:, bk:bk + 1],
               in_=ps_w[2][:, 0:128], axis=AX.X, inc=s_ve)
        p.wait("tensor", s_ac)
        p.wait("tensor", s_ve)
        for bk in range(4):
            op("tensor", "transpose", ps_w[2][0:1, ts(bk, 128)],
               stat_c[:, bk:bk + 1], ident_s[:],
               inc=(s_mm if bk == 3 else None))
        p.wait("scalar", s_mm)
        op("scalar", "copy", out=l2row[:, 0:512], in_=ps_w[2][0:1, 0:512],
           inc=s_ac)
        p.wait("tensor", s_ac)
        for bk in range(4):
            op("tensor", "matmul", ps_w[3][:, ts(bk, 128)], ones_s[0:1, :],
               l2row[0:1, ts(bk, 128)], start=True, stop=True,
               inc=(s_mm if bk == 3 else None))
        p.wait("vector", s_mm)
        op("vector", "tensor_copy", out=t1[:], in_=ps_w[3][:, 0:TSB],
           inc=s_ve)
        op("vector", "tensor_tensor", out=t1[:], in0=t1[:], in1=rs_b[:],
           op=ALU.mult)
        op("vector", "tensor_scalar", out=t1[:], in0=t1[:], scalar1=1e-12,
           scalar2=None, op0=ALU.max, inc=s_ve)
        p.wait("scalar", s_ve)
        op("scalar", "activation", out=t1[:], in_=t1[:], func=AF.Ln,
           inc=s_ac)
        p.wait("scalar", s_ac)
        op("scalar", "activation", out=mu_b[:], in_=t1[:], func=AF.Exp,
           scale=-1.0, bias=LN127)
        op("scalar", "activation", out=t1[:], in_=t1[:], func=AF.Exp,
           scale=1.0, bias=-LN127, inc=s_ac)
        p.wait("vector", s_ac)
        op("vector", "tensor_tensor", out=mu_b[:], in0=mu_b[:], in1=rs_b[:],
           op=ALU.mult, inc=s_ve)
        p.wait("vector", s_ve)
        for oc in range(8):
            op("vector", "tensor_tensor", out=cent[:, ts(oc, TSB)],
               in0=cent[:, ts(oc, TSB)], in1=mu_b[:], op=ALU.mult)
            round_halfup("vector", ofm_hi[:, ts(oc, TSB)],
                         cent[:, ts(oc, TSB)], sq2[:, 0:TSB],
                         sq2[:, TSB:2 * TSB])
        op("vector", "engine_nop", inc=s_ve)
        if "d_qlhs" in dbg:
            p.wait("scalar", s_ve)
            for oc in range(8):
                op("scalar", "activation", out=y1sq[:, ts(oc, TSB)][:, 0:TSB],
                   in_=ofm_hi[:, ts(oc, TSB)], func=AF.Copy,
                   inc=(s_ac if oc == 7 else None))
            p.wait("scalar", s_ac)
            st(dbg["d_qlhs"][tl], y1sq[:, 0:4096])
            stq.wait_all("scalar")
            stq.wait_all("vector")
        p.wait("tensor", s_ve)
        for oc in range(8):
            for k in range(8):
                op("tensor", "matmul", ps_all[oc][:, 0:TSB],
                   qrhs5[:, k * D + oc * 128:k * D + (oc + 1) * 128],
                   ofm_hi[:, ts(k, TSB)], start=(k == 0), stop=(k == 7),
                   inc=(s_mm if k == 7 else None))
        p.wait("vector", s_mm)
        for oc in range(8):
            op("vector", "tensor_scalar", out=y1sq[:, ts(oc, TSB)],
               in0=ps_all[oc][:, 0:TSB], scalar1=rsc_c[:, oc:oc + 1],
               scalar2=None, op0=ALU.mult)
            op("vector", "tensor_tensor", out=y1sq[:, ts(oc, TSB)],
               in0=y1sq[:, ts(oc, TSB)], in1=t1[:], op=ALU.mult)
            op("vector", "tensor_tensor", out=y1sq[:, ts(oc, TSB)],
               in0=y1sq[:, ts(oc, TSB)], in1=y1[:, ts(oc, TSB)], op=ALU.add,
               inc=(s_ve if oc == 7 else None))
        p.wait("scalar", s_ve)
        for oc in range(8):
            st(out_fm[ts(oc, 128), t0:t0 + TSB], y1sq[:, ts(oc, TSB)])
        stq.wait_all("sync")
        p.barrier(bar)

    return _finish()



def make_fir_toeplitz(fir_long, fir_short):
    T = np.zeros((D, 128, 128), dtype=np.float32)
    wl = fir_long.reshape(D, 64)
    ws = fir_short.reshape(D, 5)
    for po in range(64):
        for k in range(64):
            pp = po + 1 + k
            if pp < 128:
                T[:, pp, po] = wl[:, k]
        for k in range(5):
            pp = po + 60 + k
            if pp < 128:
                T[:, pp, 64 + po] = ws[:, k]
    return T


def stage_inputs(inputs):
    x = np.ascontiguousarray(inputs["L0_attn_hidden_states"], np.float32)
    wq = np.asarray(inputs["L0_attn_q_proj_w"], np.float32)
    wk = np.asarray(inputs["L0_attn_k_proj_w"], np.float32)
    wv = np.asarray(inputs["L0_attn_v_proj_w"], np.float32)
    bp = np.asarray(inputs["L0_attn_b_proj_w"], np.float32)
    wqkvb = np.ascontiguousarray(
        np.concatenate([wq.T, wk.T, wv.T], axis=1))
    wbeta = np.zeros((D, 32), np.float32)
    wbeta[:, 0:4] = bp.T
    convw = np.ascontiguousarray(np.concatenate(
        [np.asarray(inputs["L0_attn_q_conv_w"], np.float32),
         np.asarray(inputs["L0_attn_k_conv_w"], np.float32),
         np.asarray(inputs["L0_attn_v_conv_w"], np.float32)], axis=0))
    fir_toep = make_fir_toeplitz(
        np.asarray(inputs["L0_attn_fir_long_w"], np.float32),
        np.asarray(inputs["L0_attn_fir_short_w"], np.float32))
    fw1 = np.asarray(inputs["L0_attn_fuse_w1"], np.float32)
    shared = dict(
        wqkvb=wqkvb, wbeta=wbeta, convw=convw, fir_toep=fir_toep,
        w1h=np.ascontiguousarray(fw1[:, :D].T),
        w1s=np.ascontiguousarray(fw1[:, D:].T),
        b1=np.ascontiguousarray(np.asarray(
            inputs["L0_attn_fuse_b1"], np.float32).reshape(8, 128).T),
        w2=np.ascontiguousarray(
            np.asarray(inputs["L0_attn_fuse_w2"], np.float32).T),
        b2=np.asarray(inputs["L0_attn_fuse_b2"], np.float32),
        wo=np.ascontiguousarray(
            np.asarray(inputs["L0_attn_o_proj_w"], np.float32).T),
        ffn_rhs=np.ascontiguousarray(inputs["L0_ffn_rhs"], np.float32),
        o_norm=np.asarray(inputs["L0_attn_o_norm_w"], np.float32),
        scal=np.array([inputs["L0_attn_logit_temp"][0],
                       inputs["L0_attn_res_scale_short"][0],
                       inputs["L0_attn_res_scale_long"][0], 0.0], np.float32),
        ident=np.eye(128, dtype=np.float32),
        cvec=np.tile(np.array([1e-5, 1e-12, 1e-4, 1.0, -0.5, -1.0,
                               1.0 / 1024, LN127, -LN127, 0.5, -1e-12, 2.0,
                               0.0, 3.0, 4.0, 5.0], np.float32), (128, 1)),
        ones=np.ones((128, 128), np.float32),
        mask_low=(-np.tril(np.ones((128, 128), np.float32), -1)),
        mask_up0=np.triu(np.ones((128, 128), np.float32), 0),
    )
    in_maps = []
    for c in range(8):
        b, half = c // 2, c % 2
        buf = np.zeros((TBUF, D), np.float32)
        if half == 0:
            buf[TAIL:] = x[b, :TAIL]
        else:
            buf[:] = x[b]
        m = dict(shared)
        m["x_tm"] = buf
        m["x_fm"] = np.ascontiguousarray(buf.T)
        in_maps.append(m)
    return in_maps


_NC_CACHE = {}


def kernel(**inputs):
    in_maps = stage_inputs(inputs)
    if "nc" not in _NC_CACHE:
        _NC_CACHE["nc"] = build_program()[0]
    nc = _NC_CACHE["nc"]
    res = run_bass_kernel_spmd(nc, in_maps, core_ids=list(range(8)))
    out = np.empty((B, L, D), np.float32)
    for c in range(8):
        b, half = c // 2, c % 2
        out[b, half * TAIL:(half + 1) * TAIL] = res.results[c]["out_fm"].T
    return out


# version 07
# speedup vs baseline: 1.0398x; 1.0015x over previous
import numpy as np
import concourse.bass as bass
import concourse.mybir as mybir
from concourse.bass_utils import run_bass_kernel_spmd

F32 = mybir.dt.float32
BF16 = mybir.dt.bfloat16
AF = mybir.ActivationFunctionType
ALU = mybir.AluOpType
AX = mybir.AxisListType

B, L, D, H, DH = 4, 8192, 1024, 4, 256
TBUF, TAIL = 8192, 4096
TSB = 512
NSB = TBUF // TSB
CK = 128
NCPS = TSB // CK
NCK = TBUF // CK
NW = 3 * D
EPS_LN = 1e-5
EPS_L2 = 1e-12
LN127 = float(np.log(np.float64(127.0)))

DBG = ()
CONSTVALS = [1e-5, 1e-12, 1e-4, 1.0, -0.5, -1.0, 1.0 / 1024, LN127, -LN127,
             0.5, -1e-12, 2.0, 0.0, 3.0, 4.0, 5.0]


def ts(i, n):
    return slice(i * n, (i + 1) * n)


class Sem:
    def __init__(self, cm):
        self._cm = cm
        self.h = cm.__enter__()
        self.n = 0

    def inc(self, k=1):
        self.n += k
        return self.h


class P:
    ENGINES = ("tensor", "vector", "scalar", "gpsimd", "sync")

    def __init__(self, nc):
        self.nc = nc
        self.q = {e: [] for e in self.ENGINES}
        self.sems = []

    def sem(self, name):
        s = Sem(self.nc.semaphore(name))
        self.sems.append(s)
        return s

    def on(self, engine, fn):
        self.q[engine].append(fn)

    def wait(self, engine, sem, val=None):
        v = sem.n if val is None else val
        if v <= 0:
            return
        self.q[engine].append(lambda eng, s=sem.h, vv=v: eng.wait_ge(s, vv))

    def barrier(self, bar):
        target = bar.n + len(self.ENGINES)
        for e in self.ENGINES:
            self.q[e].append(lambda eng, s=bar.h: eng.sem_inc(s, 1))
        bar.n = target
        for e in self.ENGINES:
            self.q[e].append(lambda eng, s=bar.h, v=target: eng.wait_ge(s, v))

    def emit(self):
        def mk(e):
            def run(eng):
                for f in self.q[e]:
                    f(eng)
            return run
        with self.nc.Block() as block:
            block.tensor(mk("tensor"))
            block.vector(mk("vector"))
            block.scalar(mk("scalar"))
            block.gpsimd(mk("gpsimd"))
            block.sync(mk("sync"))


def build_program(debug=DBG, stop_after=None):
    nc = bass.Bass("TRN2", target_bir_lowering=False, debug=False)
    p = P(nc)
    alloc = []

    def sbuf(name, shape, dt=F32):
        cm = nc.sbuf_tensor(name, shape, dt)
        alloc.append(cm)
        return cm.__enter__()

    def psumt(name, shape, dt=F32):
        cm = nc.psum_tensor(name, shape, dt)
        alloc.append(cm)
        return cm.__enter__()

    def I(name, shape, dt=F32):
        return nc.dram_tensor(name, shape, dt, kind="ExternalInput").ap()

    x_tm = I("x_tm", [TBUF, D])
    x_fm = I("x_fm", [D, TBUF])
    wqkvb = I("wqkvb", [D, NW])
    wbeta = I("wbeta", [D, 32])
    convw = I("convw", [3 * D, 4])
    fir_toep = I("fir_toep", [D, 128, 128])
    w1h = I("w1h", [D, D])
    w1s = I("w1s", [16, D])
    b1 = I("b1", [128, 8])
    w2 = I("w2", [D, 4])
    b2 = I("b2", [4])
    wo = I("wo", [D, D])
    ffn_rhs = I("ffn_rhs", [D, D])
    o_norm = I("o_norm", [DH])
    scal = I("scal", [4])
    ident = I("ident", [128, 128])
    ones = I("ones", [128, 128])
    mask_low = I("mask_low", [128, 128])
    mask_up0 = I("mask_up0", [128, 128])
    cvec = I("cvec", [128, 16])

    out_fm = nc.dram_tensor("out_fm", [D, TAIL], F32, kind="ExternalOutput").ap()
    dbg = {}
    for name, shape in [
        ("d_qn", [NSB, 128, 16 * TSB]), ("d_v", [NSB, 128, 8 * TSB]),
        ("d_beta", [NSB, 4, TSB]), ("d_delta", [TAIL, D]),
        ("d_stats", [TAIL // 128, 128, 64]), ("d_fw", [TAIL // 128, 128, 16]),
        ("d_o", [TAIL, D]), ("d_y1", [8, 128, 8 * TSB]), ("d_L", [TAIL, 2 * D]),
        ("d_qlhs", [8, 128, 8 * TSB]),
    ]:
        if name in debug:
            dbg[name] = nc.dram_tensor(name, shape, F32,
                                       kind="ExternalOutput").ap()

    wlo_slab = nc.dram_tensor("wlo_slab", [25, 8, 128, 128], BF16).ap()
    w1h_hi_d = nc.dram_tensor("w1h_hi_d", [8, 128, D], BF16).ap()
    w1h_lo_d = nc.dram_tensor("w1h_lo_d", [8, 128, D], BF16).ap()
    wo_hi_d = nc.dram_tensor("wo_hi_d", [8, 128, D], BF16).ap()
    wo_lo_d = nc.dram_tensor("wo_lo_d", [8, 128, D], BF16).ap()
    v_pad = nc.dram_tensor("v_pad", [64 + TBUF, D], F32).ap()
    delta_d = nc.dram_tensor("delta_d", [TAIL, D], F32).ap()
    l_d = nc.dram_tensor("l_d", [TAIL, 2 * D], F32).ap()
    xn_d = nc.dram_tensor("xn_d", [2, 8, 128, TAIL], BF16).ap()
    o_d = nc.dram_tensor("o_d", [2, TAIL, D], BF16).ap()
    qrhs_d = nc.dram_tensor("qrhs_d", [8, 128, D], BF16).ap()

    bar = p.sem("bar")
    s_mm = p.sem("s_mm")
    s_ve = p.sem("s_ve")
    s_ac = p.sem("s_ac")
    s_gp = p.sem("s_gp")

    class DmaQ:
        def __init__(self, engine, name, depth=6, compute_guard=True):
            self.engine = engine
            self.depth = depth
            self.guard = compute_guard
            self.sems = [p.sem(f"{name}{i}") for i in range(depth)]
            self.cnt = 0

        def dma(self, dst, src):
            s = self.sems[self.cnt % self.depth]
            if self.cnt >= self.depth:
                p.wait(self.engine, s)
            if self.engine == "scalar" and self.guard:
                p.wait("scalar", s_ac)
                p.wait("scalar", s_ve)
            self.cnt += 1
            tgt = s.inc(16)
            p.on(self.engine, lambda eng, d=dst, ss=src, h=tgt:
                 eng.dma_start(out=d, in_=ss).then_inc(h, 16))
            return s, s.n

        def wait_all(self, engine):
            for s in self.sems:
                p.wait(engine, s)

    ldq = DmaQ("sync", "qld", 6)
    ld2q = DmaQ("scalar", "ql2", 4, compute_guard=False)
    stq = DmaQ("scalar", "qst", 6)
    gpq = DmaQ("scalar", "qgp", 4)

    def ld(dst, src):
        ldq.dma(dst, src)

    def st(dst, src):
        stq.dma(dst, src)

    def g2(dst, src):
        gpq.dma(dst, src)

    osems = {"vector": s_ve, "scalar": s_ac, "gpsimd": s_gp}

    def op(engine, meth, *a, inc=None, **kw):
        if engine in osems:
            osem = osems[engine]
            prior = osem.n
            h = osem.inc()

            def f(eng, meth=meth, a=a, kw=kw, h=h, prior=prior):
                if prior > 0:
                    eng.wait_ge(h, prior)
                getattr(eng, meth)(*a, **kw).then_inc(h, 1)
            p.on(engine, f)
        else:
            h = inc.inc() if inc is not None else None

            def f(eng, meth=meth, a=a, kw=kw, h=h):
                inst = getattr(eng, meth)(*a, **kw)
                if h is not None:
                    inst.then_inc(h, 1)
            p.on(engine, f)

    def _finish():
        stq.wait_all("scalar")
        ldq.wait_all("sync")
        gpq.wait_all("gpsimd")
        p.barrier(bar)
        p.emit()
        for cm in reversed(alloc):
            cm.__exit__(None, None, None)
        for s in reversed(p.sems):
            s._cm.__exit__(None, None, None)
        return nc, dbg

    w_hi = sbuf("w_hi", [128, 8 * NW], BF16)
    wbeta_s = sbuf("wbeta_s", [128, 8 * 32], BF16)
    w_hiF = w_hi[:].bitcast(F32)
    wkl = [sbuf(f"wkl{i}", [128, 1024], BF16) for i in range(3)]
    ident_s = sbuf("ident_s", [128, 128])
    identb = sbuf("identb", [128, 128], BF16)
    consts_s = sbuf("consts_s", [128, 16])
    for _i, _v in enumerate(CONSTVALS):
        nc.const_aps.register(F32, _v, consts_s[:, _i:_i + 1])
    ones_s = sbuf("ones_s", [128, 128])
    mlow_s = sbuf("mlow_s", [128, 128])
    mup0_s = sbuf("mup0_s", [128, 128])
    cw_s = sbuf("cw_s", [128, 96])
    w1s_s = sbuf("w1s_s", [16, D])
    b1_s = sbuf("b1_s", [128, 8])
    w2_s = sbuf("w2_s", [128, 32])
    b2_s = sbuf("b2_s", [4, 1])
    scal_s = sbuf("scal_s", [4, 1])
    sc_rep = sbuf("sc_rep", [128, 4])
    onorm_b = sbuf("onorm_b", [128, DH])
    rsc_c = sbuf("rsc_c", [128, 8])
    xtm_t = sbuf("xtm_t", [128, D])
    stat_c = sbuf("stat_c", [128, 16])
    scr = sbuf("scr", [128, D])
    l2row = sbuf("l2row", [1, 1024])
    mu_b = sbuf("mu_b", [128, TSB])
    rs_b = sbuf("rs_b", [128, TSB])
    xfm_t = sbuf("xfm_t", [128, 4 * TSB])
    t1 = sbuf("t1", [128, TSB])
    xnhi = sbuf("xnhi", [128, 8 * TSB], BF16)
    xnlo = sbuf("xnlo", [128, 8 * TSB], BF16)
    pout = sbuf("pout", [128, 16 * TSB])
    v_out = sbuf("v_out", [128, 8 * TSB])
    raw = sbuf("raw", [128, TSB + 4])
    hist = sbuf("hist", [128, 96])
    sq2 = sbuf("sq2", [128, 2 * TSB])
    beta_r = sbuf("beta_r", [4, TSB])
    beta_tm = sbuf("beta_tm", [128, 4 * NCPS])
    big3 = sbuf("big3", [128, 4096])
    a4 = sbuf("a4", [128, 4 * 128])
    at4 = sbuf("at4", [128, 4 * 128])
    tta = sbuf("tta", [128, 4 * 128])
    ttb = sbuf("ttb", [128, 4 * 128])
    attn4 = sq2[:, 512:1024]
    kntm4 = sbuf("kntm4", [128, 4 * DH])
    vbtm4 = sbuf("vbtm4", [128, 4 * DH])
    kbtm4 = sbuf("kbtm4", [128, 4 * DH])
    utm4 = sbuf("utm4", [128, 4 * DH])
    wfm4 = sbuf("wfm4", [128, 4 * DH])
    uadj = sbuf("uadj", [128, DH])
    doutb = uadj
    ssb = sq2[:, 0:512]
    lgs = beta_r[:, 0:TSB]
    lgtm = sbuf("lgtm", [128, 64])
    fwc = sbuf("fwc", [128, 16])
    rms_c = sbuf("rms_c", [128, 16])
    mx4 = sbuf("mx4", [128, 8])
    whh = wkl
    ttoep = [v_out[:, 0:128], v_out[:, 128:256]]
    rbuf = pout[:, 0:8192]
    scat = w_hiF[:, 0:8192]

    ps_s = [psumt(f"ps_s{h}", [128, 512]) for h in range(H)]
    ps_w = [psumt(f"ps_w{i}", [128, 512]) for i in range(4)]
    ps_all = ps_s + ps_w

    ld(consts_s[:], cvec[:])
    ld(ident_s[:], ident[:])
    ldq.wait_all("vector")
    op("vector", "tensor_copy", out=identb[:], in_=ident_s[:])
    ld(ones_s[:], ones[:])
    ld(mlow_s[:], mask_low[:])
    ld(mup0_s[:], mask_up0[:])
    ld(cw_s[:].rearrange("b (a c) -> b a c", a=24),
       convw.rearrange("(a b) c -> b a c", b=128))
    ld(w1s_s[:], w1s[:])
    ld(b1_s[:], b1[:])
    ld(w2_s[:].rearrange("b (a c) -> b a c", a=8),
       w2.rearrange("(a b) c -> b a c", b=128))
    ld(b2_s[:], b2[:, None])
    ld(scal_s[:], scal[:, None])
    ldq.wait_all("scalar")
    ldq.wait_all("vector")
    ldq.wait_all("gpsimd")
    ldq.wait_all("tensor")

    for k in range(8):
        ld(big3[:, 0:NW] if NW <= 4096 else None, wqkvb[ts(k, 128), :])
        ldq.wait_all("vector")
        op("vector", "tensor_copy", out=w_hi[:, k * NW:(k + 1) * NW],
           in_=big3[:, 0:NW])
        op("vector", "tensor_tensor", out=big3[:, 0:NW], in0=big3[:, 0:NW],
           in1=w_hi[:, k * NW:(k + 1) * NW], op=ALU.subtract)
        op("vector", "tensor_copy", out=xnhi[:, 0:NW], in_=big3[:, 0:NW],
           inc=s_ve)
        p.wait("scalar", s_ve)
        for oc in range(24):
            st(wlo_slab[oc, k], xnhi[:, ts(oc, 128)])
        stq.wait_all("sync")
        stq.wait_all("vector")
    ld(scr[:, 0:256].rearrange("p (k c) -> p k c", k=8),
       wbeta.rearrange("(k p) c -> p k c", p=128))
    ldq.wait_all("vector")
    op("vector", "tensor_copy", out=wbeta_s[:], in_=scr[:, 0:256])
    op("vector", "tensor_tensor", out=scr[:, 0:256], in0=scr[:, 0:256],
       in1=wbeta_s[:], op=ALU.subtract)
    op("vector", "tensor_copy", out=xnhi[:, 0:256], in_=scr[:, 0:256],
       inc=s_ve)
    p.wait("scalar", s_ve)
    for k in range(8):
        st(wlo_slab[24, k][:, 0:32], xnhi[:, k * 32:(k + 1) * 32])
    stq.wait_all("sync")
    stq.wait_all("vector")
    for src, dhi, dlo in ((w1h, w1h_hi_d, w1h_lo_d), (wo, wo_hi_d, wo_lo_d)):
        for k in range(8):
            ld(big3[:, 0:D], src[ts(k, 128), :])
            ldq.wait_all("vector")
            op("vector", "tensor_copy", out=xnhi[:, 0:D], in_=big3[:, 0:D])
            op("vector", "tensor_tensor", out=big3[:, 0:D],
               in0=big3[:, 0:D], in1=xnhi[:, 0:D], op=ALU.subtract)
            op("vector", "tensor_copy", out=xnlo[:, 0:D], in_=big3[:, 0:D],
               inc=s_ve)
            p.wait("scalar", s_ve)
            st(dhi[k], xnhi[:, 0:D])
            st(dlo[k], xnlo[:, 0:D])
            stq.wait_all("sync")
            stq.wait_all("vector")

    ld(scr[0:1, 0:DH], o_norm[None, :])
    ldq.wait_all("tensor")
    op("tensor", "matmul", ps_w[0][:, 0:DH], ones_s[0:1, :], scr[0:1, 0:DH],
       start=True, stop=True)
    op("tensor", "transpose", ps_w[1][0:1, 0:4], scal_s[:],
       ident_s[0:4, 0:4], inc=s_mm)
    p.wait("scalar", s_mm)
    op("scalar", "copy", out=onorm_b[:], in_=ps_w[0][:, 0:DH])
    op("scalar", "copy", out=l2row[0:1, 0:4], in_=ps_w[1][0:1, 0:4],
       inc=s_ac)
    p.wait("tensor", s_ac)
    op("tensor", "matmul", ps_w[2][:, 0:4], ones_s[0:1, :], l2row[0:1, 0:4],
       start=True, stop=True, inc=s_mm)
    p.wait("scalar", s_mm)
    op("scalar", "copy", out=sc_rep[:], in_=ps_w[2][:, 0:4], inc=s_ac)

    ldq.wait_all("vector")
    p.wait("vector", s_mm)
    p.wait("vector", s_ac)
    op("vector", "memset", hist[:], 0.0)
    op("vector", "memset", scr[:, 0:D], 0.0, inc=s_ve)
    p.wait("scalar", s_ve)
    st(v_pad[0:64, :], scr[0:64, 0:D])

    ffw = pout
    ffa = v_out
    for k in range(8):
        ld(ffw[:, ts(k, D)], ffn_rhs[ts(k, 128), :])
    ldq.wait_all("vector")
    for j in range(4):
        op("vector", "tensor_tensor", out=ffa[:, ts(j, D)],
           in0=ffw[:, ts(2 * j, D)], in1=ffw[:, ts(2 * j + 1, D)], op=ALU.max)
        op("vector", "scalar_tensor_tensor", out=ffa[:, ts(j, D)],
           in0=ffw[:, ts(2 * j, D)], scalar=-1.0, in1=ffa[:, ts(j, D)],
           op0=ALU.mult, op1=ALU.max)
        op("vector", "scalar_tensor_tensor", out=ffa[:, ts(j, D)],
           in0=ffw[:, ts(2 * j + 1, D)], scalar=-1.0, in1=ffa[:, ts(j, D)],
           op0=ALU.mult, op1=ALU.max)
    for j in range(2):
        op("vector", "tensor_tensor", out=ffa[:, ts(j, D)],
           in0=ffa[:, ts(2 * j, D)], in1=ffa[:, ts(2 * j + 1, D)], op=ALU.max)
    op("vector", "tensor_tensor", out=ffa[:, 0:D], in0=ffa[:, 0:D],
       in1=ffa[:, ts(1, D)], op=ALU.max, inc=s_ve)
    for j in range(8):
        p.wait("tensor", s_ve)
        p.wait("tensor", s_ac)
        op("tensor", "transpose", ps_w[0][:, 0:128], ffa[:, ts(j, 128)],
           ident_s[:], inc=s_mm)
        p.wait("vector", s_mm)
        op("vector", "reduce_max", out=rsc_c[:, j:j + 1],
           in_=ps_w[0][:, 0:128], axis=AX.X, inc=s_ve)
    op("vector", "tensor_scalar", out=rsc_c[:, 0:8], in0=rsc_c[:, 0:8],
       scalar1=1e-12, scalar2=1.0 / 127.0, op0=ALU.max, op1=ALU.mult,
       inc=s_ve)
    p.wait("scalar", s_ve)
    op("scalar", "activation", out=stat_c[:, 0:8], in_=rsc_c[:, 0:8],
       func=AF.Ln)
    op("scalar", "activation", out=stat_c[:, 0:8], in_=stat_c[:, 0:8],
       func=AF.Exp, scale=-1.0, inc=s_ac)
    p.wait("tensor", s_ac)
    for j in range(8):
        op("tensor", "transpose", ps_w[j // 4][0:1, ts(j % 4, 128)],
           stat_c[:, j:j + 1], ident_s[:], inc=(s_mm if j == 7 else None))
    p.wait("scalar", s_mm)
    op("scalar", "copy", out=l2row[:, 0:512], in_=ps_w[0][0:1, 0:512])
    op("scalar", "copy", out=l2row[:, 512:1024], in_=ps_w[1][0:1, 0:512],
       inc=s_ac)
    for j in range(8):
        p.wait("tensor", s_ac)
        p.wait("tensor", s_ve)
        op("tensor", "matmul", ps_w[1][:, 0:128], ones_s[0:1, :],
           l2row[0:1, ts(j, 128)], start=True, stop=True, inc=s_mm)
        p.wait("vector", s_mm)
        op("vector", "tensor_copy", out=sq2[:, ts(j, 128)],
           in_=ps_w[1][:, 0:128], inc=s_ve)
    srep = sq2
    p.wait("vector", s_ac)

    def round_halfup(engine, dst_bf16, src_f32, tmp, tmp2):
        op(engine, "tensor_scalar", out=tmp, in0=src_f32, scalar1=128.5,
           scalar2=None, op0=ALU.add)
        for c in (256.0, 128.0, 64.0, 32.0, 16.0, 8.0, 4.0, 2.0, 1.0):
            op(engine, "tensor_scalar", out=tmp2, in0=tmp, scalar1=c,
               scalar2=c, op0=ALU.is_ge, op1=ALU.mult)
            op(engine, "tensor_tensor", out=tmp, in0=tmp, in1=tmp2,
               op=ALU.subtract)
        op(engine, "scalar_tensor_tensor", out=tmp2, in0=src_f32, scalar=0.5,
           in1=tmp, op0=ALU.add, op1=ALU.subtract)
        op(engine, "tensor_copy", out=dst_bf16, in_=tmp2)

    for k in range(8):
        op("vector", "tensor_tensor", out=ffw[:, ts(k, D)],
           in0=ffw[:, ts(k, D)], in1=srep[:, 0:D], op=ALU.mult)
        round_halfup("vector", xnhi[:, 0:D], ffw[:, ts(k, D)],
                     big3[:, 0:D], big3[:, D:2 * D])
        op("vector", "engine_nop", inc=s_ve)
        p.wait("scalar", s_ve)
        st(qrhs_d[k], xnhi[:, 0:D])
        stq.wait_all("vector")
    p.barrier(bar)
    if stop_after == "PH0":
        return _finish()

    def ln_and_xn(sbi):
        t0 = sbi * TSB
        for j in range(4):
            ld(xtm_t[:], x_tm[t0 + j * 128:t0 + (j + 1) * 128, :])
            ldq.wait_all("vector")
            ldq.wait_all("scalar")
            op("vector", "tensor_scalar", out=scr[:, 0:D], in0=xtm_t[:],
               scalar1=1.0 / D, scalar2=None, op0=ALU.mult, op1=ALU.add,
               accum_out=stat_c[:, j:j + 1], inc=s_ve)
            op("scalar", "activation", out=sq2[:, 0:D], in_=xtm_t[:],
               func=AF.Square, accum_out=stat_c[:, 4 + j:5 + j], inc=s_ac)
            p.wait("sync", s_ve)
            p.wait("sync", s_ac)
        p.wait("vector", s_ac)
        op("vector", "tensor_tensor", out=stat_c[:, 8:12],
           in0=stat_c[:, 0:4], in1=stat_c[:, 0:4], op=ALU.mult)
        op("vector", "tensor_scalar", out=stat_c[:, 4:8], in0=stat_c[:, 4:8],
           scalar1=1.0 / D, scalar2=None, op0=ALU.mult)
        op("vector", "tensor_tensor", out=stat_c[:, 4:8], in0=stat_c[:, 4:8],
           in1=stat_c[:, 8:12], op=ALU.subtract, inc=s_ve)
        p.wait("scalar", s_ve)
        op("scalar", "activation", out=stat_c[:, 4:8], in_=stat_c[:, 4:8],
           func=AF.Ln, bias=EPS_LN)
        op("scalar", "activation", out=stat_c[:, 4:8], in_=stat_c[:, 4:8],
           func=AF.Exp, scale=-0.5, inc=s_ac)
        p.wait("tensor", s_ac)
        for j in range(8):
            op("tensor", "transpose", ps_w[j // 4][0:1, ts(j % 4, 128)],
               stat_c[:, j:j + 1], ident_s[:],
               inc=(s_mm if j == 7 else None))
        p.wait("scalar", s_mm)
        op("scalar", "copy", out=l2row[:, 0:512], in_=ps_w[0][0:1, 0:512])
        op("scalar", "copy", out=l2row[:, 512:1024],
           in_=ps_w[1][0:1, 0:512], inc=s_ac)
        p.wait("tensor", s_ac)
        for j in range(4):
            op("tensor", "matmul", ps_w[1][:, 0:128], ones_s[0:1, :],
               l2row[0:1, ts(j, 128)], start=True, stop=True)
            op("tensor", "matmul", ps_w[2][:, 0:128], ones_s[0:1, :],
               l2row[0:1, ts(4 + j, 128)], start=True, stop=True, inc=s_mm)
            p.wait("scalar", s_mm)
            op("scalar", "copy", out=mu_b[:, ts(j, 128)],
               in_=ps_w[1][:, 0:128])
            op("scalar", "copy", out=rs_b[:, ts(j, 128)],
               in_=ps_w[2][:, 0:128], inc=s_ac)
            p.wait("tensor", s_ac)
        x3 = x_fm.rearrange("(a b) c -> b a c", b=128)
        for hf in range(2):
            ld(xfm_t[:].rearrange("p (a c) -> p a c", a=4),
               x3[:, hf * 4:(hf + 1) * 4, t0:t0 + TSB])
            ldq.wait_all("vector")
            p.wait("vector", s_ac)
            for kk in range(4):
                k = hf * 4 + kk
                op("vector", "tensor_tensor", out=t1[:],
                   in0=xfm_t[:, ts(kk, TSB)], in1=mu_b[:], op=ALU.subtract)
                op("vector", "tensor_tensor", out=t1[:], in0=t1[:],
                   in1=rs_b[:], op=ALU.mult, inc=s_ve)
                p.wait("scalar", s_ve)
                op("scalar", "activation", out=xnhi[:, ts(k, TSB)],
                   in_=t1[:], func=AF.Copy, inc=s_ac)
                p.wait("vector", s_ac)
                op("vector", "tensor_tensor", out=xnlo[:, ts(k, TSB)],
                   in0=t1[:], in1=xnhi[:, ts(k, TSB)], op=ALU.subtract,
                   inc=s_ve)
            p.wait("sync", s_ve)
        if t0 >= TAIL:
            p.wait("scalar", s_ve)
            for k in range(8):
                st(xn_d[0, k][:, t0 - TAIL:t0 - TAIL + TSB],
                   xnhi[:, ts(k, TSB)])
                st(xn_d[1, k][:, t0 - TAIL:t0 - TAIL + TSB],
                   xnlo[:, ts(k, TSB)])

    def proj_conv(sbi, oc_limit=25, do_tail=True, tail_lim=99,
                  bank_override=None):
        mm_marks = []
        copy_marks = []
        ld_marks = []
        for oc in range(oc_limit):
            bank = bank_override or ps_w[oc % 2]
            ncols = 128 if oc < 24 else 32
            buf = wkl[oc % 3]
            if len(mm_marks) >= 3:
                p.wait("sync", s_mm, mm_marks[-3])
                p.wait("scalar", s_mm, mm_marks[-3])
            if ncols == 128:
                m1 = ldq.dma(
                    buf[:, 0:4 * 128].rearrange("p (k c) -> p k c", k=4),
                    wlo_slab[oc][0:4].rearrange("k p c -> p k c"))
                m2 = ld2q.dma(
                    buf[:, 4 * 128:8 * 128].rearrange("p (k c) -> p k c",
                                                      k=4),
                    wlo_slab[oc][4:8].rearrange("k p c -> p k c"))
            else:
                m1 = ldq.dma(
                    buf[:, 0:128].rearrange("p (k c) -> p k c", k=4),
                    wlo_slab[oc][0:4, :, 0:32].rearrange("k p c -> p k c"))
                m2 = ld2q.dma(
                    buf[:, 128:256].rearrange("p (k c) -> p k c", k=4),
                    wlo_slab[oc][4:8, :, 0:32].rearrange("k p c -> p k c"))
            ld_marks.append((m1, m2))
            if len(copy_marks) >= 2:
                p.wait("tensor", s_ac, copy_marks[-2])
            for k in range(8):
                hi = (w_hi[:, k * NW + oc * 128:k * NW + oc * 128 + 128]
                      if oc < 24 else wbeta_s[:, k * 32:(k + 1) * 32])
                op("tensor", "matmul", bank[0:ncols, 0:TSB], hi,
                   xnhi[:, ts(k, TSB)], start=(k == 0), stop=False)
                op("tensor", "matmul", bank[0:ncols, 0:TSB], hi,
                   xnlo[:, ts(k, TSB)], start=False, stop=False)
            for sem, tgt in ld_marks[-1]:
                p.wait("tensor", sem, tgt)
            for k in range(8):
                op("tensor", "matmul", bank[0:ncols, 0:TSB],
                   buf[:, k * ncols:(k + 1) * ncols], xnhi[:, ts(k, TSB)],
                   start=False, stop=(k == 7), inc=(s_mm if k == 7 else None))
            mm_marks.append(s_mm.n)
            if not do_tail:
                copy_marks.append(s_ac.n)
                continue
            if oc < 24:
                if tail_lim > 0 and tail_lim != 115:
                    p.wait("scalar", s_mm)
                    op("scalar", "copy", out=raw[:, 4:4 + TSB],
                       in_=bank[0:128, 0:TSB], inc=s_ac)
                    copy_marks.append(s_ac.n)
                dst = (pout[:, ts(oc, TSB)] if oc < 16
                       else v_out[:, ts(oc - 16, TSB)])
                if tail_lim == 101:
                    p.wait("vector", s_mm)
                    op("vector", "tensor_copy", out=dst,
                       in_=bank[0:128, 0:TSB], inc=s_ve)
                elif tail_lim == 102:
                    p.wait("vector", s_mm)
                    op("vector", "tensor_scalar", out=dst,
                       in0=xnhi[:, 0:TSB],
                       scalar1=cw_s[:, oc * 4 + 3:oc * 4 + 4], scalar2=None,
                       op0=ALU.mult, inc=s_ve)
                elif tail_lim == 103:
                    p.wait("vector", s_mm)
                    op("vector", "tensor_scalar", out=sq2[:, 0:TSB],
                       in0=bank[0:128, 0:TSB],
                       scalar1=cw_s[:, oc * 4 + 3:oc * 4 + 4], scalar2=None,
                       op0=ALU.mult, inc=s_ve)
                elif tail_lim == 104:
                    p.wait("vector", s_mm)
                elif tail_lim == 105:
                    op("vector", "tensor_copy", out=sq2[:, 0:TSB],
                       in_=xnhi[:, 0:TSB], inc=s_ve)
                elif tail_lim == 115:
                    p.wait("vector", s_mm)
                    op("vector", "tensor_scalar", out=dst,
                       in0=bank[0:128, 0:TSB],
                       scalar1=cw_s[:, oc * 4 + 3:oc * 4 + 4], scalar2=None,
                       op0=ALU.mult, inc=s_ve)
                elif tail_lim == 110:
                    p.wait("vector", s_ac)
                    op("vector", "tensor_scalar", out=sq2[:, 0:TSB],
                       in0=bank[0:128, 0:TSB],
                       scalar1=cw_s[:, oc * 4 + 3:oc * 4 + 4], scalar2=None,
                       op0=ALU.mult, inc=s_ve)
                elif tail_lim == 111:
                    p.wait("gpsimd", s_mm)
                    op("gpsimd", "tensor_scalar", out=sq2[:, 0:TSB],
                       in0=bank[0:128, 0:TSB],
                       scalar1=cw_s[:, oc * 4 + 3:oc * 4 + 4], scalar2=None,
                       op0=ALU.mult, inc=s_gp)
                elif tail_lim == 106:
                    p.wait("vector", s_mm)
                    op("vector", "tensor_copy", out=sq2[:, 0:128],
                       in_=bank[0:128, 0:128], inc=s_ve)
                elif tail_lim == 107:
                    p.wait("vector", s_mm)
                    for q4 in range(4):
                        op("vector", "tensor_copy",
                           out=sq2[:, ts(q4, 128)],
                           in_=bank[0:128, ts(q4, 128)],
                           inc=(s_ve if q4 == 3 else None))
                elif tail_lim > 1:
                    p.wait("vector", s_ac)
                    op("vector", "tensor_scalar", out=dst,
                       in0=raw[:, 4:4 + TSB],
                       scalar1=cw_s[:, oc * 4 + 3:oc * 4 + 4], scalar2=None,
                       op0=ALU.mult, inc=s_ve)
                if 2 < tail_lim <= 99:
                    op("vector", "tensor_copy", out=raw[:, 0:4],
                       in_=hist[:, oc * 4:oc * 4 + 4])
                for jt in range(3):
                    if 3 + jt < tail_lim <= 99:
                        op("vector", "scalar_tensor_tensor", out=dst,
                           in0=raw[:, 1 + jt:1 + jt + TSB],
                           scalar=cw_s[:, oc * 4 + jt:oc * 4 + jt + 1],
                           in1=dst, op0=ALU.mult, op1=ALU.add)
                if 6 < tail_lim <= 99:
                    op("vector", "tensor_copy",
                       out=hist[:, oc * 4:oc * 4 + 4],
                       in_=raw[:, TSB:TSB + 4], inc=s_ve)
                if 7 < tail_lim <= 99:
                    p.wait("scalar", s_ve)
                    op("scalar", "activation", out=dst, in_=dst, func=AF.Silu,
                       inc=s_ac)
            else:
                p.wait("scalar", s_mm)
                op("scalar", "activation", out=beta_r[:],
                   in_=bank[0:4, 0:TSB], func=AF.Sigmoid, inc=s_ac)
                copy_marks.append(s_ac.n)
        p.wait("vector", s_ac)
        op("vector", "engine_nop", inc=s_ve)

    def l2norm(sbi):
        for tnsr in range(2):
            for h in range(H):
                oc0 = tnsr * 8 + 2 * h
                bank = ps_w[(tnsr * H + h) % 2]
                p.wait("scalar", s_ve)
                op("scalar", "activation", out=sq2[:, 0:TSB],
                   in_=pout[:, ts(oc0, TSB)], func=AF.Square)
                op("scalar", "activation", out=sq2[:, TSB:2 * TSB],
                   in_=pout[:, ts(oc0 + 1, TSB)], func=AF.Square, inc=s_ac)
                p.wait("tensor", s_ac)
                p.wait("tensor", s_ve)
                op("tensor", "matmul", bank[:, 0:TSB], ones_s[:],
                   sq2[:, 0:TSB], start=True, stop=False)
                op("tensor", "matmul", bank[:, 0:TSB], ones_s[:],
                   sq2[:, TSB:2 * TSB], start=False, stop=True, inc=s_mm)
                p.wait("scalar", s_mm)
                op("scalar", "activation", out=t1[:], in_=bank[:, 0:TSB],
                   func=AF.Ln, bias=EPS_L2)
                op("scalar", "activation", out=t1[:], in_=t1[:], func=AF.Exp,
                   scale=-0.5, inc=s_ac)
                p.wait("vector", s_ac)
                op("vector", "tensor_tensor", out=pout[:, ts(oc0, TSB)],
                   in0=pout[:, ts(oc0, TSB)], in1=t1[:], op=ALU.mult)
                op("vector", "tensor_tensor", out=pout[:, ts(oc0 + 1, TSB)],
                   in0=pout[:, ts(oc0 + 1, TSB)], in1=t1[:], op=ALU.mult,
                   inc=s_ve)

    def delta(sbi):
        t0 = sbi * TSB
        p.wait("tensor", s_ve)
        p.wait("tensor", s_ac)
        for ci in range(NCPS):
            op("tensor", "transpose", ps_w[3][:, ts(ci, 4)],
               beta_r[:, ts(ci, 128)], ident_s[0:4, 0:4],
               inc=(s_mm if ci == NCPS - 1 else None))
        p.wait("scalar", s_mm)
        op("scalar", "copy", out=beta_tm[:], in_=ps_w[3][:, 0:16], inc=s_ac)

        for h in range(H):
            gc0 = sbi * NCPS
            tail0 = gc0 * CK >= TAIL

            def qsl(c, ci):
                base = (2 * h + c) * TSB + ci * 128
                return pout[:, base:base + 128]

            def ksl(c, ci):
                base = (8 + 2 * h + c) * TSB + ci * 128
                return pout[:, base:base + 128]

            def vsl(c, ci):
                base = (2 * h + c) * TSB + ci * 128
                return v_out[:, base:base + 128]

            p.wait("tensor", s_ac)
            p.wait("tensor", s_ve)
            for ci in range(NCPS):
                op("tensor", "transpose", ps_w[ci][:, 0:128], ksl(0, ci),
                   ident_s[:])
                op("tensor", "transpose", ps_w[ci][:, 128:256], ksl(1, ci),
                   ident_s[:])
                op("tensor", "transpose", ps_w[ci][:, 256:384], vsl(0, ci),
                   ident_s[:])
                op("tensor", "transpose", ps_w[ci][:, 384:512], vsl(1, ci),
                   ident_s[:], inc=(s_mm if ci == NCPS - 1 else None))
            p.wait("scalar", s_mm)
            p.wait("vector", s_mm)
            for ci in range(NCPS):
                eng, cp, sm = (("scalar", "copy", s_ac) if ci < 2
                               else ("vector", "tensor_copy", s_ve))
                op(eng, cp, out=kntm4[:, ts(ci, DH)],
                   in_=ps_w[ci][:, 0:256])
                op(eng, cp, out=vbtm4[:, ts(ci, DH)],
                   in_=ps_w[ci][:, 256:512],
                   inc=(sm if ci % 2 == 1 else None))
            p.wait("scalar", s_ve)
            for ci in range(NCPS):
                st(v_pad[64 + t0 + ci * 128:64 + t0 + (ci + 1) * 128,
                         h * DH:(h + 1) * DH], vbtm4[:, ts(ci, DH)])
            stq.wait_all("vector")
            p.wait("vector", s_ac)
            for ci in range(NCPS):
                bcol = beta_tm[:, ci * 4 + h:ci * 4 + h + 1]
                op("vector", "tensor_scalar", out=kbtm4[:, ts(ci, DH)],
                   in0=kntm4[:, ts(ci, DH)], scalar1=bcol, scalar2=None,
                   op0=ALU.mult)
                op("vector", "tensor_scalar", out=vbtm4[:, ts(ci, DH)],
                   in0=vbtm4[:, ts(ci, DH)], scalar1=bcol, scalar2=None,
                   op0=ALU.mult)
            op("vector", "engine_nop", inc=s_ve)
            p.wait("tensor", s_ve)
            for ci in range(NCPS):
                last = (ci == NCPS - 1)
                op("tensor", "matmul", ps_w[ci][:, 0:128], ksl(0, ci),
                   ksl(0, ci), start=True, stop=False)
                op("tensor", "matmul", ps_w[ci][:, 0:128], ksl(1, ci),
                   ksl(1, ci), start=False, stop=True,
                   inc=(s_mm if last and not tail0 else None))
                if tail0:
                    op("tensor", "matmul", ps_w[ci][:, 128:256], ksl(0, ci),
                       qsl(0, ci), start=True, stop=False)
                    op("tensor", "matmul", ps_w[ci][:, 128:256], ksl(1, ci),
                       qsl(1, ci), start=False, stop=True,
                       inc=(s_mm if last else None))
            p.wait("vector", s_mm)
            for ci in range(NCPS):
                bcol = beta_tm[:, ci * 4 + h:ci * 4 + h + 1]
                op("vector", "tensor_scalar", out=a4[:, ts(ci, 128)],
                   in0=ps_w[ci][:, 0:128], scalar1=bcol, scalar2=None,
                   op0=ALU.mult)
                op("vector", "tensor_tensor", out=a4[:, ts(ci, 128)],
                   in0=a4[:, ts(ci, 128)], in1=mlow_s[:], op=ALU.mult)
                if tail0:
                    op("vector", "tensor_tensor", out=attn4[:, ts(ci, 128)],
                       in0=ps_w[ci][:, 128:256], in1=mup0_s[:], op=ALU.mult)
            op("vector", "engine_nop", inc=s_ve)
            p.wait("tensor", s_ve)
            for ci in range(NCPS):
                op("tensor", "transpose", ps_w[ci][:, 256:384],
                   a4[:, ts(ci, 128)], ident_s[:],
                   inc=(s_mm if ci == NCPS - 1 else None))
            p.wait("scalar", s_mm)
            for ci in range(NCPS):
                op("scalar", "copy", out=at4[:, ts(ci, 128)],
                   in_=ps_w[ci][:, 256:384],
                   inc=(s_ac if ci == NCPS - 1 else None))
            p.wait("vector", s_ac)
            for ci in range(NCPS):
                op("vector", "tensor_tensor", out=tta[:, ts(ci, 128)],
                   in0=at4[:, ts(ci, 128)], in1=ident_s[:], op=ALU.add)
            op("vector", "engine_nop", inc=s_ve)
            tt_cur, tt_nxt = tta, ttb
            for j in range(1, 7):
                p.wait("tensor", s_ve)
                p.wait("tensor", s_ac)
                for ci in range(NCPS):
                    last = (ci == NCPS - 1)
                    op("tensor", "matmul", ps_w[ci][:, 0:128],
                       at4[:, ts(ci, 128)], a4[:, ts(ci, 128)],
                       start=True, stop=True,
                       inc=(s_mm if last and j == 6 else None))
                    if j < 6:
                        op("tensor", "matmul", ps_w[ci][:, 128:256],
                           a4[:, ts(ci, 128)], at4[:, ts(ci, 128)],
                           start=True, stop=True,
                           inc=(s_mm if last else None))
                p.wait("scalar", s_mm)
                p.wait("vector", s_mm)
                for ci in range(NCPS):
                    last = (ci == NCPS - 1)
                    eng, cp, sm = (("scalar", "copy", s_ac) if ci < 2
                                   else ("vector", "tensor_copy", s_ve))
                    op(eng, cp, out=a4[:, ts(ci, 128)],
                       in_=ps_w[ci][:, 0:128],
                       inc=(sm if ci % 2 == 1 and j == 6 else None))
                    if j < 6:
                        op(eng, cp, out=at4[:, ts(ci, 128)],
                           in_=ps_w[ci][:, 128:256],
                           inc=(sm if ci % 2 == 1 else None))
                p.wait("tensor", s_ac)
                p.wait("tensor", s_ve)
                for ci in range(NCPS):
                    op("tensor", "matmul", ps_w[ci][:, 256:384],
                       a4[:, ts(ci, 128)], tt_cur[:, ts(ci, 128)],
                       start=True, stop=True,
                       inc=(s_mm if ci == NCPS - 1 else None))
                p.wait("vector", s_mm)
                for ci in range(NCPS):
                    op("vector", "scalar_tensor_tensor",
                       out=tt_nxt[:, ts(ci, 128)], in0=ps_w[ci][:, 256:384],
                       scalar=1.0, in1=tt_cur[:, ts(ci, 128)],
                       op0=ALU.mult, op1=ALU.add)
                op("vector", "engine_nop", inc=s_ve)
                tt_cur, tt_nxt = tt_nxt, tt_cur
            p.wait("tensor", s_ve)
            for ci in range(NCPS):
                op("tensor", "matmul", ps_w[ci][:, 0:256],
                   tt_cur[:, ts(ci, 128)], vbtm4[:, ts(ci, DH)],
                   start=True, stop=True)
                op("tensor", "matmul", ps_w[ci][:, 256:384],
                   kbtm4[:, ci * DH:ci * DH + 128], tt_cur[:, ts(ci, 128)],
                   start=True, stop=True)
                op("tensor", "matmul", ps_w[ci][:, 384:512],
                   kbtm4[:, ci * DH + 128:ci * DH + 256],
                   tt_cur[:, ts(ci, 128)], start=True, stop=True,
                   inc=(s_mm if ci == NCPS - 1 else None))
            p.wait("scalar", s_mm)
            p.wait("vector", s_mm)
            for ci in range(NCPS):
                eng, cp, sm = (("scalar", "copy", s_ac) if ci < 2
                               else ("vector", "tensor_copy", s_ve))
                op(eng, cp, out=utm4[:, ts(ci, DH)],
                   in_=ps_w[ci][:, 0:256])
                op(eng, cp, out=wfm4[:, ts(ci, DH)],
                   in_=ps_w[ci][:, 256:512],
                   inc=(sm if ci % 2 == 1 else None))
            for ci in range(NCPS):
                gc = gc0 + ci
                tail = gc * CK >= TAIL
                p.wait("tensor", s_ac)
                p.wait("tensor", s_ve)
                if gc > 0:
                    p.wait("scalar", s_mm)
                    p.wait("scalar", s_ve)
                    p.wait("vector", s_mm)
                    p.wait("vector", s_ac)
                    op("scalar", "copy", out=ssb[:, 0:512],
                       in_=ps_s[h][:, 0:512], inc=s_ac)
                    op("vector", "engine_nop", inc=s_ve)
                    p.wait("tensor", s_ac)
                    p.wait("tensor", s_ve)
                    op("tensor", "matmul", ps_w[0][:, 0:256],
                       wfm4[:, ci * DH:ci * DH + 128], ssb[:, 0:256],
                       start=True, stop=False, skip_group_check=True)
                    op("tensor", "matmul", ps_w[0][:, 0:256],
                       wfm4[:, ci * DH + 128:ci * DH + 256],
                       ssb[:, 256:512], start=False, stop=True,
                       inc=(None if tail else s_mm), skip_group_check=True)
                    if tail:
                        op("tensor", "matmul", ps_w[0][:, 256:512],
                           qsl(0, ci), ssb[:, 0:256], start=True, stop=False,
                           skip_group_check=True)
                        op("tensor", "matmul", ps_w[0][:, 256:512],
                           qsl(1, ci), ssb[:, 256:512], start=False,
                           stop=False, inc=s_mm, skip_group_check=True)
                    p.wait("vector", s_mm)
                    stq.wait_all("vector")
                    op("vector", "tensor_tensor", out=uadj[:],
                       in0=utm4[:, ts(ci, DH)], in1=ps_w[0][:, 0:256],
                       op=ALU.subtract, inc=s_ve)
                    ua = uadj[:]
                    p.wait("tensor", s_ve)
                else:
                    ua = utm4[:, ts(ci, DH)]
                if tail:
                    op("tensor", "matmul", ps_w[0][:, 256:512],
                       attn4[:, ts(ci, 128)], ua, start=False, stop=True,
                       skip_group_check=True)
                op("tensor", "matmul", ps_s[h][:, 0:256],
                   kntm4[:, ci * DH:ci * DH + 128], ua,
                   start=(gc == 0), stop=False, skip_group_check=True)
                op("tensor", "matmul", ps_s[h][:, 256:512],
                   kntm4[:, ci * DH + 128:ci * DH + 256], ua,
                   start=False, stop=(gc == NCK - 1), inc=s_mm,
                   skip_group_check=True)
                if tail:
                    p.wait("vector", s_mm)
                    op("vector", "tensor_copy", out=doutb[:],
                       in_=ps_w[0][:, 256:512], inc=s_ve)
                    p.wait("scalar", s_ve)
                    trow = gc * CK - TAIL
                    st(delta_d[trow:trow + 128, h * DH:(h + 1) * DH],
                       doutb[:])
                    p.wait("tensor", s_ve)

    for sbi in range(NSB):
        ln_and_xn(sbi)
        p.barrier(bar)
        if stop_after == f"PH1:ln:{sbi}":
            return _finish()
        if sbi == 0 and stop_after and stop_after.startswith("PH1:pj"):
            kind, n = stop_after.split(":")[1], int(stop_after.split(":")[2])
            if kind == "pjtl" and n == 108:
                p.wait("vector", s_mm)
                op("vector", "tensor_copy", out=sq2[:, 0:128],
                   in_=ps_w[0][0:128, 0:128], inc=s_ve)
            elif kind == "pjtl" and n in (112, 114):
                p.wait("tensor", s_ve)
                p.wait("tensor", s_ac)
                bank = ps_w[0]
                for k in range(8):
                    for r in range(3):
                        if n == 112:
                            lh = w_hi[:, k * 3072:k * 3072 + 128]
                        else:
                            lh = xnhi[:, ts(k, TSB)][:, 0:128]
                        rhs = xnlo[:, ts(k, TSB)] if r == 1 \
                            else xnhi[:, ts(k, TSB)]
                        op("tensor", "matmul", bank[0:128, 0:TSB], lh, rhs,
                           start=(k == 0 and r == 0),
                           stop=(k == 7 and r == 2),
                           inc=(s_mm if (k == 7 and r == 2) else None))
                p.wait("vector", s_mm)
                op("vector", "tensor_copy", out=sq2[:, 0:TSB],
                   in_=bank[0:128, 0:TSB], inc=s_ve)
            elif kind == "pjtl" and n == 109:
                proj_conv(0, oc_limit=1, do_tail=True, tail_lim=103,
                          bank_override=ps_s[0])
            elif kind == "pjtl":
                proj_conv(0, oc_limit=1, do_tail=True, tail_lim=n)
            else:
                proj_conv(0, oc_limit=n + 1, do_tail=(kind == "pjoc"))
            p.barrier(bar)
            return _finish()
        proj_conv(sbi)
        p.barrier(bar)
        if stop_after == f"PH1:proj:{sbi}":
            return _finish()
        l2norm(sbi)
        p.barrier(bar)
        if stop_after == f"PH1:l2:{sbi}":
            return _finish()
        delta(sbi)
        p.barrier(bar)
        if stop_after == f"PH1:sb:{sbi}":
            return _finish()
        if "d_qn" in dbg:
            for k in range(16):
                st(dbg["d_qn"][sbi][:, ts(k, TSB)], pout[:, ts(k, TSB)])
        if "d_v" in dbg:
            for k in range(8):
                st(dbg["d_v"][sbi][:, ts(k, TSB)], v_out[:, ts(k, TSB)])
        if "d_beta" in dbg:
            st(dbg["d_beta"][sbi], beta_r[:])
        if dbg:
            stq.wait_all("scalar")
            p.barrier(bar)

    if "d_delta" in dbg:
        for r in range(8):
            ld(pout[:, 0:4096].rearrange("p (a c) -> p a c", a=4),
               delta_d[r * 512:(r + 1) * 512, :].rearrange(
                   "(a b) c -> b a c", b=128))
            ldq.wait_all("scalar")
            for j in range(4):
                st(dbg["d_delta"][r * 512 + j * 128:
                                  r * 512 + (j + 1) * 128, :],
                   pout[:, ts(j, D)])
            stq.wait_all("sync")
        p.barrier(bar)
    if stop_after == "PH1":
        return _finish()

    vp3 = v_pad.rearrange("(u p) c -> p u c", p=64)
    mm_mark = [0]
    for cs in range(8):
        p.wait("gpsimd", s_mm)
        g2(rbuf[0:64, 0:8192].rearrange("p (u c) -> p u c", u=64),
           vp3[:, 64:128, cs * 128:(cs + 1) * 128])
        g2(rbuf[64:128, 0:8192].rearrange("p (u c) -> p u c", u=64),
           vp3[:, 65:129, cs * 128:(cs + 1) * 128])
        gpq.wait_all("tensor")
        for ch in range(128):
            gch = cs * 128 + ch
            tb = ttoep[ch % 2]
            if mm_mark[0] > 0:
                p.wait("sync", s_mm, mm_mark[0])
            ld(tb[:], fir_toep[gch])
            mm_mark[0] = s_mm.n
            ldq.wait_all("tensor")
            rap = rbuf.rearrange("p (u c) -> p c u", u=64)[:, ch, :]
            op("tensor", "matmul", ps_w[ch % 4][:, 0:64], tb[:], rap,
               start=True, stop=True, inc=s_mm)
            if ch % 2 == 0:
                p.wait("vector", s_mm)
                op("vector", "tensor_copy",
                   out=scat.rearrange("p (u c) -> p c u", u=64)[:, ch, :],
                   in_=ps_w[ch % 4][:, 0:64], inc=s_ve)
                p.wait("tensor", s_ve)
            else:
                p.wait("scalar", s_mm)
                op("scalar", "copy",
                   out=scat.rearrange("p (u c) -> p c u", u=64)[:, ch, :],
                   in_=ps_w[ch % 4][:, 0:64], inc=s_ac)
                p.wait("tensor", s_ac)
        p.wait("gpsimd", s_ve)
        p.wait("gpsimd", s_ac)
        ld3 = l_d.rearrange("(u o) c -> o u c", o=64)
        g2(ld3[:, :, cs * 128:(cs + 1) * 128],
           scat[0:64, 0:8192].rearrange("p (u c) -> p u c", u=64))
        g2(ld3[:, :, D + cs * 128:D + (cs + 1) * 128],
           scat[64:128, 0:8192].rearrange("p (u c) -> p u c", u=64))
        gpq.wait_all("gpsimd")
    p.barrier(bar)
    if "d_L" in dbg:
        for r in range(8):
            ld(pout[:, 0:8192].rearrange("p (a c) -> p a c", a=4),
               l_d[r * 512:(r + 1) * 512, :].rearrange(
                   "(a b) c -> b a c", b=128))
            ldq.wait_all("scalar")
            for j in range(4):
                st(dbg["d_L"][r * 512 + j * 128:r * 512 + (j + 1) * 128, :],
                   pout[:, j * 2048:(j + 1) * 2048])
            stq.wait_all("sync")
        p.barrier(bar)
    if stop_after == "PH2":
        return _finish()

    d_buf = w_hiF[:, 0:4096]
    v_buf = w_hiF[:, 4096:8192]
    hdn = w_hiF[:, 8192:12288]
    L_buf = pout
    base_s = v_out
    o_buf = big3
    stt_tm = scr
    stt_fm = xfm_t[0:16, 0:4 * TSB]

    def softmax_ocombine(sl):
        t0g = sl * TSB
        for bk in range(4):
            p.wait("vector", s_ac)
            op("vector", "tensor_copy", out=fwc[:].rearrange(
                "p (h j) -> p h j", h=4),
               in_=lgtm[:].rearrange("p (h b j) -> p b h j", h=4, b=4)[:, bk],
               inc=s_ve)
            op("vector", "tensor_reduce", out=mx4[:, 0:4],
               in_=fwc[:].rearrange("p (h j) -> p h j", h=4), axis=AX.X,
               op=ALU.max)
            for h in range(H):
                op("vector", "tensor_scalar", out=fwc[:, ts(h, 4)],
                   in0=fwc[:, ts(h, 4)], scalar1=mx4[:, h:h + 1],
                   scalar2=None, op0=ALU.subtract)
            op("vector", "engine_nop", inc=s_ve)
            p.wait("scalar", s_ve)
            op("scalar", "activation", out=fwc[:], in_=fwc[:], func=AF.Exp,
               inc=s_ac)
            p.wait("vector", s_ac)
            op("vector", "tensor_reduce", out=mx4[:, 4:8],
               in_=fwc[:].rearrange("p (h j) -> p h j", h=4), axis=AX.X,
               op=ALU.add, inc=s_ve)
            p.wait("scalar", s_ve)
            op("scalar", "activation", out=mx4[:, 4:8], in_=mx4[:, 4:8],
               func=AF.Ln)
            op("scalar", "activation", out=mx4[:, 4:8], in_=mx4[:, 4:8],
               func=AF.Exp, scale=-1.0, inc=s_ac)
            p.wait("vector", s_ac)
            for h in range(H):
                op("vector", "tensor_scalar", out=fwc[:, ts(h, 4)],
                   in0=fwc[:, ts(h, 4)], scalar1=mx4[:, 4 + h:5 + h],
                   scalar2=None, op0=ALU.mult)
            for h in range(H):
                op("vector", "scalar_tensor_tensor",
                   out=fwc[:, h * 4:h * 4 + 1], in0=sc_rep[:, 1:2],
                   scalar=1.0, in1=fwc[:, h * 4:h * 4 + 1],
                   op0=ALU.mult, op1=ALU.add)
                op("vector", "scalar_tensor_tensor",
                   out=fwc[:, h * 4 + 1:h * 4 + 2], in0=sc_rep[:, 2:3],
                   scalar=1.0, in1=fwc[:, h * 4 + 1:h * 4 + 2],
                   op0=ALU.mult, op1=ALU.add)
            op("vector", "engine_nop", inc=s_ve)
            if "d_fw" in dbg:
                p.wait("scalar", s_ve)
                st(dbg["d_fw"][sl * 4 + bk], fwc[:])
                stq.wait_all("scalar")
                stq.wait_all("vector")
            j = bk
            for h in range(H):
                srt = L_buf[:, j * 2048 + 1024 + h * DH:
                            j * 2048 + 1024 + (h + 1) * DH]
                lng = L_buf[:, j * 2048 + h * DH:j * 2048 + (h + 1) * DH]
                dlt = d_buf[:, j * 1024 + h * DH:j * 1024 + (h + 1) * DH]
                vdr = v_buf[:, j * 1024 + h * DH:j * 1024 + (h + 1) * DH]
                odst = o_buf[:, j * 1024 + h * DH:j * 1024 + (h + 1) * DH]
                op("vector", "tensor_scalar", out=odst, in0=srt,
                   scalar1=fwc[:, h * 4:h * 4 + 1], scalar2=None,
                   op0=ALU.mult)
                op("vector", "scalar_tensor_tensor", out=odst, in0=lng,
                   scalar=fwc[:, h * 4 + 1:h * 4 + 2], in1=odst,
                   op0=ALU.mult, op1=ALU.add)
                op("vector", "scalar_tensor_tensor", out=odst, in0=dlt,
                   scalar=fwc[:, h * 4 + 2:h * 4 + 3], in1=odst,
                   op0=ALU.mult, op1=ALU.add)
                op("vector", "scalar_tensor_tensor", out=odst, in0=vdr,
                   scalar=fwc[:, h * 4 + 3:h * 4 + 4], in1=odst,
                   op0=ALU.mult, op1=ALU.add, inc=s_ve)
                p.wait("scalar", s_ve)
                op("scalar", "activation", out=sq2[:, 0:DH], in_=odst,
                   func=AF.Square,
                   accum_out=rms_c[:, j * 4 + h:j * 4 + h + 1], inc=s_ac)
        p.wait("vector", s_ac)
        op("vector", "tensor_scalar", out=rms_c[:], in0=rms_c[:],
           scalar1=1.0 / DH, scalar2=None, op0=ALU.mult, inc=s_ve)
        p.wait("scalar", s_ve)
        op("scalar", "activation", out=rms_c[:], in_=rms_c[:], func=AF.Ln,
           bias=EPS_LN)
        op("scalar", "activation", out=rms_c[:], in_=rms_c[:], func=AF.Exp,
           scale=-0.5, inc=s_ac)
        p.wait("vector", s_ac)
        for j in range(4):
            for h in range(H):
                odst = o_buf[:, j * 1024 + h * DH:j * 1024 + (h + 1) * DH]
                op("vector", "tensor_scalar", out=odst, in0=odst,
                   scalar1=rms_c[:, j * 4 + h:j * 4 + h + 1], scalar2=None,
                   op0=ALU.mult)
                op("vector", "tensor_tensor", out=odst, in0=odst,
                   in1=onorm_b[:], op=ALU.mult)
        op("vector", "engine_nop", inc=s_ve)
        if "d_o" in dbg:
            p.wait("scalar", s_ve)
            for j in range(4):
                st(dbg["d_o"][t0g + j * 128:t0g + (j + 1) * 128, :],
                   o_buf[:, ts(j, D)])
            stq.wait_all("scalar")
            stq.wait_all("vector")
        p.wait("scalar", s_ve)
        for j in range(4):
            op("scalar", "activation", out=xnhi[:, j * 1024:(j + 1) * 1024]
               .bitcast(BF16)[:, 0:1024] if False else
               xnhi[:, ts(2 * j, TSB)], in_=o_buf[:, j * 1024:j * 1024 + 512],
               func=AF.Copy)
            op("scalar", "activation", out=xnhi[:, ts(2 * j + 1, TSB)],
               in_=o_buf[:, j * 1024 + 512:(j + 1) * 1024], func=AF.Copy,
               inc=s_ac)
        p.wait("vector", s_ac)
        for j in range(4):
            op("vector", "tensor_tensor", out=xnlo[:, ts(2 * j, TSB)],
               in0=o_buf[:, j * 1024:j * 1024 + 512],
               in1=xnhi[:, ts(2 * j, TSB)], op=ALU.subtract)
            op("vector", "tensor_tensor", out=xnlo[:, ts(2 * j + 1, TSB)],
               in0=o_buf[:, j * 1024 + 512:(j + 1) * 1024],
               in1=xnhi[:, ts(2 * j + 1, TSB)], op=ALU.subtract,
               inc=(s_ve if j == 3 else None))
        p.wait("scalar", s_ve)
        for j in range(4):
            st(o_d[0, t0g + j * 128:t0g + (j + 1) * 128, :],
               xnhi[:, ts(2 * j, TSB)].bitcast(BF16)[:, 0:512]
               if False else xnhi[:, 2 * j * TSB:2 * j * TSB + 1024])
            st(o_d[1, t0g + j * 128:t0g + (j + 1) * 128, :],
               xnlo[:, 2 * j * TSB:2 * j * TSB + 1024])
        stq.wait_all("sync")
        stq.wait_all("vector")
        stq.wait_all("gpsimd")

    for sl in range(8):
        t0 = sl * TSB
        ld(d_buf.rearrange("p (a c) -> p a c", a=4),
           delta_d[t0:t0 + TSB, :].rearrange("(a b) c -> b a c", b=128))
        ld(v_buf.rearrange("p (a c) -> p a c", a=4),
           v_pad[64 + TAIL + t0:64 + TAIL + t0 + TSB, :].rearrange(
               "(a b) c -> b a c", b=128))
        ld(L_buf[:, 0:8192].rearrange("p (a c) -> p a c", a=4),
           l_d[t0:t0 + TSB, :].rearrange("(a b) c -> b a c", b=128))
        ld(xnhi[:].rearrange("p (a c) -> p a c", a=8),
           xn_d[0][:, :, t0:t0 + TSB].rearrange("a b c -> b a c"))
        ld(xnlo[:].rearrange("p (a c) -> p a c", a=8),
           xn_d[1][:, :, t0:t0 + TSB].rearrange("a b c -> b a c"))
        ldq.wait_all("vector")
        ldq.wait_all("scalar")
        ldq.wait_all("tensor")
        op("scalar", "activation", out=stat_c[0:1, 0:1],
           in_=scal_s[0:1, :], func=AF.Exp)
        op("scalar", "activation", out=stat_c[0:1, 0:1],
           in_=stat_c[0:1, 0:1], func=AF.Ln, bias=1.0)
        op("scalar", "activation", out=stat_c[0:1, 1:2],
           in_=stat_c[0:1, 0:1], func=AF.Ln, bias=1e-4, inc=s_ac)
        p.wait("tensor", s_ac)
        op("tensor", "matmul", ps_w[2][0:4, 0:1], ones_s[0:1, 0:4],
           stat_c[0:1, 1:2], start=True, stop=True, inc=s_mm)
        p.wait("scalar", s_mm)
        op("scalar", "activation", out=stat_c[0:4, 2:3],
           in_=ps_w[2][0:4, 0:1], func=AF.Exp, scale=-1.0, inc=s_ac)
        p.wait("vector", s_ac)
        op("vector", "tensor_tensor", out=stat_c[0:4, 3:4], in0=b2_s[:],
           in1=stat_c[0:4, 2:3], op=ALU.mult, inc=s_ve)
        p.wait("scalar", s_ve)
        for j in range(4):
            p.wait("scalar", s_mm)
            p.wait("vector", s_mm)
            for h in range(H):
                srcs = [L_buf[:, j * 2048 + 1024 + h * DH:
                              j * 2048 + 1024 + (h + 1) * DH],
                        L_buf[:, j * 2048 + h * DH:j * 2048 + (h + 1) * DH],
                        d_buf[:, j * 1024 + h * DH:j * 1024 + (h + 1) * DH],
                        v_buf[:, j * 1024 + h * DH:j * 1024 + (h + 1) * DH]]
                for si, src in enumerate(srcs):
                    c0 = h * 16 + si * 4
                    op("vector", "tensor_scalar", out=t1[:, 0:DH], in0=src,
                       scalar1=1.0 / DH, scalar2=None, op0=ALU.mult,
                       op1=ALU.add, accum_out=stt_tm[:, c0:c0 + 1])
                    op("scalar", "activation", out=sq2[:, 0:DH], in_=src,
                       func=AF.Square, accum_out=stt_tm[:, c0 + 1:c0 + 2])
                    op("scalar", "activation", out=sq2[:, DH:2 * DH],
                       in_=src, func=AF.Abs,
                       accum_out=stt_tm[:, c0 + 2:c0 + 3])
            op("vector", "engine_nop", inc=s_ve)
            p.wait("scalar", s_ve)
            p.wait("scalar", s_ac)
            sv = stt_tm[:, 0:64].rearrange("p (a c) -> p a c", a=16)
            op("scalar", "activation", out=sv[:, :, 3], in_=sv[:, :, 1],
               func=AF.Sqrt, inc=s_ac)
            p.wait("vector", s_ac)
            op("vector", "tensor_tensor", out=t1[:, 0:16], in0=sv[:, :, 0],
               in1=sv[:, :, 0], op=ALU.mult)
            op("vector", "tensor_scalar", out=sv[:, :, 1], in0=sv[:, :, 1],
               scalar1=1.0 / DH, scalar2=None, op0=ALU.mult)
            op("vector", "tensor_tensor", out=sv[:, :, 1], in0=sv[:, :, 1],
               in1=t1[:, 0:16], op=ALU.subtract)
            op("vector", "tensor_scalar", out=sv[:, :, 2], in0=sv[:, :, 2],
               scalar1=1.0 / DH, scalar2=None, op0=ALU.mult, inc=s_ve)
            if "d_stats" in dbg:
                p.wait("scalar", s_ve)
                st(dbg["d_stats"][sl * 4 + j], stt_tm[:, 0:64])
                stq.wait_all("scalar")
                stq.wait_all("vector")
            p.wait("tensor", s_ve)
            p.wait("tensor", s_ac)
            for h in range(H):
                op("tensor", "transpose", ps_w[3][0:16, ts(h, 128)],
                   stt_tm[:, ts(h, 16)], ident_s[:],
                   inc=(s_mm if h == H - 1 else None))
            p.wait("vector", s_mm)
            for h in range(H):
                op("vector", "tensor_copy",
                   out=stt_fm[:, h * TSB + j * 128:h * TSB + (j + 1) * 128],
                   in_=ps_w[3][0:16, ts(h, 128)],
                   inc=(s_ve if h == H - 1 else None))
            p.wait("tensor", s_ve)
        for k in range(8):
            p.wait("sync", s_mm)
            ld(whh[0][:], w1h_hi_d[k])
            ld(whh[1][:], w1h_lo_d[k])
            ldq.wait_all("tensor")
            for oc in range(8):
                op("tensor", "matmul", ps_all[oc][:, 0:TSB],
                   whh[0][:, ts(oc, 128)], xnhi[:, ts(k, TSB)],
                   start=(k == 0), stop=False)
                op("tensor", "matmul", ps_all[oc][:, 0:TSB],
                   whh[0][:, ts(oc, 128)], xnlo[:, ts(k, TSB)],
                   start=False, stop=False)
                op("tensor", "matmul", ps_all[oc][:, 0:TSB],
                   whh[1][:, ts(oc, 128)], xnhi[:, ts(k, TSB)],
                   start=False, stop=(k == 7),
                   inc=(s_mm if oc == 7 else None))
        p.wait("scalar", s_mm)
        for oc in range(8):
            op("scalar", "copy", out=base_s[:, ts(oc, TSB)],
               in_=ps_all[oc][:, 0:TSB], inc=(s_ac if oc == 7 else None))
        for h in range(H):
            p.wait("tensor", s_ac)
            p.wait("tensor", s_ve)
            for oc in range(8):
                for j in range(4):
                    op("tensor", "matmul",
                       ps_all[oc][:, j * 128:(j + 1) * 128],
                       w1s_s[:, ts(oc, 128)],
                       stt_fm[:, h * TSB + j * 128:h * TSB + (j + 1) * 128],
                       start=True, stop=True,
                       inc=(s_mm if oc == 7 and j == 3 else None))
            p.wait("vector", s_mm)
            for oc in range(8):
                op("vector", "tensor_tensor", out=hdn[:, ts(oc, TSB)],
                   in0=base_s[:, ts(oc, TSB)], in1=ps_all[oc][:, 0:TSB],
                   op=ALU.add, inc=(s_ve if oc == 7 else None))
            p.wait("scalar", s_ve)
            for oc in range(8):
                op("scalar", "activation", out=hdn[:, ts(oc, TSB)],
                   in_=hdn[:, ts(oc, TSB)], func=AF.Gelu,
                   bias=b1_s[:, oc:oc + 1], inc=(s_ac if oc == 7 else None))
            p.wait("tensor", s_ac)
            for oc in range(8):
                op("tensor", "matmul", ps_w[3][0:4, 0:TSB],
                   w2_s[:, ts(oc, 4)], hdn[:, ts(oc, TSB)],
                   start=(oc == 0), stop=(oc == 7),
                   inc=(s_mm if oc == 7 else None))
            p.wait("vector", s_mm)
            op("vector", "tensor_scalar", out=lgs[:],
               in0=ps_w[3][0:4, 0:TSB], scalar1=stat_c[0:4, 2:3],
               scalar2=stat_c[0:4, 3:4], op0=ALU.mult, op1=ALU.add,
               inc=s_ve)
            p.wait("tensor", s_ve)
            for bk in range(4):
                op("tensor", "transpose", ps_w[2][:, ts(bk, 4)],
                   lgs[:, ts(bk, 128)], ident_s[0:4, 0:4],
                   inc=(s_mm if bk == 3 else None))
            p.wait("vector", s_mm)
            op("vector", "tensor_copy", out=lgtm[:, ts(h, 16)],
               in_=ps_w[2][:, 0:16], inc=s_ve)
            p.wait("tensor", s_ve)
        softmax_ocombine(sl)
        p.barrier(bar)
    if stop_after == "PH3":
        return _finish()

    qrhs5 = w_hi[:, 16384:24576]
    otm_hi = w_hi[:, 0:4096]
    otm_lo = w_hi[:, 4096:8192]
    ofm_hi = xnhi
    ofm_lo = xnlo
    y1 = pout
    y1sq = big3
    cent = w_hiF[:, 4096:8192]
    aa = v_out
    ld(qrhs5.rearrange("p (a c) -> p a c", a=8),
       qrhs_d.rearrange("a b c -> b a c"))
    for tl in range(8):
        t0 = tl * TSB
        ld(otm_hi.rearrange("p (a c) -> p a c", a=4),
           o_d[0][t0:t0 + TSB, :].rearrange("(a b) c -> b a c", b=128))
        ld(otm_lo.rearrange("p (a c) -> p a c", a=4),
           o_d[1][t0:t0 + TSB, :].rearrange("(a b) c -> b a c", b=128))
        ldq.wait_all("tensor")
        for pair in range(2):
            src = otm_hi if pair == 0 else otm_lo
            dst = ofm_hi if pair == 0 else ofm_lo
            for j in range(4):
                for oc in range(8):
                    op("tensor", "transpose",
                       ps_w[(j * 8 + oc) % 4][:, 0:64].bitcast(BF16)[:, 0:128],
                       src[:, j * 1024 + oc * 128:j * 1024 + (oc + 1) * 128],
                       identb[:], inc=s_mm)
                    eng = "vector" if (j * 8 + oc) % 2 == 0 else "scalar"
                    esm = s_ve if (j * 8 + oc) % 2 == 0 else s_ac
                    p.wait(eng, s_mm)
                    meth = "tensor_copy" if eng == "vector" else "copy"
                    op(eng, meth,
                       out=dst[:, oc * TSB + j * 128:oc * TSB + (j + 1) * 128],
                       in_=ps_w[(j * 8 + oc) % 4][:, 0:64]
                       .bitcast(BF16)[:, 0:128], inc=esm)
                    p.wait("tensor", esm)
        for k in range(8):
            p.wait("sync", s_mm)
            ld(whh[0][:], wo_hi_d[k])
            ld(whh[1][:], wo_lo_d[k])
            ldq.wait_all("tensor")
            for oc in range(8):
                op("tensor", "matmul", ps_all[oc][:, 0:TSB],
                   whh[0][:, ts(oc, 128)], ofm_hi[:, ts(k, TSB)],
                   start=(k == 0), stop=False)
                op("tensor", "matmul", ps_all[oc][:, 0:TSB],
                   whh[0][:, ts(oc, 128)], ofm_lo[:, ts(k, TSB)],
                   start=False, stop=False)
                op("tensor", "matmul", ps_all[oc][:, 0:TSB],
                   whh[1][:, ts(oc, 128)], ofm_hi[:, ts(k, TSB)],
                   start=False, stop=(k == 7),
                   inc=(s_mm if oc == 7 else None))
        x3 = x_fm.rearrange("(a b) c -> b a c", b=128)
        ld(xfm_t[:].rearrange("p (a c) -> p a c", a=4),
           x3[:, 0:4, TAIL + t0:TAIL + t0 + TSB])
        ldq.wait_all("vector")
        p.wait("vector", s_mm)
        for oc in range(4):
            op("vector", "scalar_tensor_tensor", out=y1[:, ts(oc, TSB)],
               in0=ps_all[oc][:, 0:TSB], scalar=1.0,
               in1=xfm_t[:, ts(oc, TSB)], op0=ALU.mult, op1=ALU.add,
               inc=(s_ve if oc == 3 else None))
        p.wait("sync", s_ve)
        ld(xfm_t[:].rearrange("p (a c) -> p a c", a=4),
           x3[:, 4:8, TAIL + t0:TAIL + t0 + TSB])
        ldq.wait_all("vector")
        for oc in range(4, 8):
            op("vector", "scalar_tensor_tensor", out=y1[:, ts(oc, TSB)],
               in0=ps_all[oc][:, 0:TSB], scalar=1.0,
               in1=xfm_t[:, ts(oc - 4, TSB)], op0=ALU.mult, op1=ALU.add,
               inc=(s_ve if oc == 7 else None))
        if "d_y1" in dbg:
            p.wait("scalar", s_ve)
            st(dbg["d_y1"][tl], y1[:, 0:4096])
            stq.wait_all("scalar")
            stq.wait_all("vector")
        p.wait("scalar", s_ve)
        for oc in range(8):
            op("scalar", "activation", out=y1sq[:, ts(oc, TSB)],
               in_=y1[:, ts(oc, TSB)], func=AF.Square,
               inc=(s_ac if oc == 7 else None))
        p.wait("tensor", s_ve)
        p.wait("tensor", s_ac)
        for oc in range(8):
            op("tensor", "matmul", ps_w[0][:, 0:TSB], ones_s[:],
               y1[:, ts(oc, TSB)], start=(oc == 0), stop=(oc == 7))
            op("tensor", "matmul", ps_w[1][:, 0:TSB], ones_s[:],
               y1sq[:, ts(oc, TSB)], start=(oc == 0), stop=(oc == 7),
               inc=(s_mm if oc == 7 else None))
        p.wait("scalar", s_mm)
        op("scalar", "activation", out=mu_b[:], in_=ps_w[0][:, 0:TSB],
           func=AF.Copy, scale=1.0 / D)
        op("scalar", "activation", out=rs_b[:], in_=ps_w[1][:, 0:TSB],
           func=AF.Copy, scale=1.0 / D, inc=s_ac)
        p.wait("vector", s_ac)
        op("vector", "tensor_tensor", out=t1[:], in0=mu_b[:], in1=mu_b[:],
           op=ALU.mult)
        op("vector", "tensor_tensor", out=rs_b[:], in0=rs_b[:], in1=t1[:],
           op=ALU.subtract, inc=s_ve)
        p.wait("scalar", s_ve)
        op("scalar", "activation", out=rs_b[:], in_=rs_b[:], func=AF.Ln,
           bias=EPS_LN)
        op("scalar", "activation", out=rs_b[:], in_=rs_b[:], func=AF.Exp,
           scale=-0.5, inc=s_ac)
        p.wait("vector", s_ac)
        for oc in range(8):
            op("vector", "tensor_tensor", out=cent[:, ts(oc, TSB)],
               in0=y1[:, ts(oc, TSB)], in1=mu_b[:], op=ALU.subtract,
               inc=(s_ve if oc == 7 else None))
        p.wait("scalar", s_ve)
        for oc in range(8):
            op("scalar", "activation", out=aa[:, ts(oc, TSB)],
               in_=cent[:, ts(oc, TSB)], func=AF.Abs,
               inc=(s_ac if oc == 7 else None))
        p.wait("vector", s_ac)
        for j in range(4):
            op("vector", "tensor_tensor", out=aa[:, ts(j, TSB)],
               in0=aa[:, ts(2 * j, TSB)], in1=aa[:, ts(2 * j + 1, TSB)],
               op=ALU.max)
        for j in range(2):
            op("vector", "tensor_tensor", out=aa[:, ts(j, TSB)],
               in0=aa[:, ts(2 * j, TSB)], in1=aa[:, ts(2 * j + 1, TSB)],
               op=ALU.max)
        op("vector", "tensor_tensor", out=aa[:, 0:TSB], in0=aa[:, 0:TSB],
           in1=aa[:, TSB:2 * TSB], op=ALU.max, inc=s_ve)
        for bk in range(4):
            p.wait("tensor", s_ve)
            p.wait("tensor", s_ac)
            op("tensor", "transpose", ps_w[2][:, 0:128],
               aa[:, bk * 128:(bk + 1) * 128], ident_s[:], inc=s_mm)
            p.wait("vector", s_mm)
            op("vector", "reduce_max", out=stat_c[:, bk:bk + 1],
               in_=ps_w[2][:, 0:128], axis=AX.X, inc=s_ve)
        p.wait("tensor", s_ac)
        p.wait("tensor", s_ve)
        for bk in range(4):
            op("tensor", "transpose", ps_w[2][0:1, ts(bk, 128)],
               stat_c[:, bk:bk + 1], ident_s[:],
               inc=(s_mm if bk == 3 else None))
        p.wait("scalar", s_mm)
        op("scalar", "copy", out=l2row[:, 0:512], in_=ps_w[2][0:1, 0:512],
           inc=s_ac)
        p.wait("tensor", s_ac)
        for bk in range(4):
            op("tensor", "matmul", ps_w[3][:, ts(bk, 128)], ones_s[0:1, :],
               l2row[0:1, ts(bk, 128)], start=True, stop=True,
               inc=(s_mm if bk == 3 else None))
        p.wait("vector", s_mm)
        op("vector", "tensor_copy", out=t1[:], in_=ps_w[3][:, 0:TSB],
           inc=s_ve)
        op("vector", "tensor_tensor", out=t1[:], in0=t1[:], in1=rs_b[:],
           op=ALU.mult)
        op("vector", "tensor_scalar", out=t1[:], in0=t1[:], scalar1=1e-12,
           scalar2=None, op0=ALU.max, inc=s_ve)
        p.wait("scalar", s_ve)
        op("scalar", "activation", out=t1[:], in_=t1[:], func=AF.Ln,
           inc=s_ac)
        p.wait("scalar", s_ac)
        op("scalar", "activation", out=mu_b[:], in_=t1[:], func=AF.Exp,
           scale=-1.0, bias=LN127)
        op("scalar", "activation", out=t1[:], in_=t1[:], func=AF.Exp,
           scale=1.0, bias=-LN127, inc=s_ac)
        p.wait("vector", s_ac)
        op("vector", "tensor_tensor", out=mu_b[:], in0=mu_b[:], in1=rs_b[:],
           op=ALU.mult, inc=s_ve)
        p.wait("vector", s_ve)
        for oc in range(8):
            op("vector", "tensor_tensor", out=cent[:, ts(oc, TSB)],
               in0=cent[:, ts(oc, TSB)], in1=mu_b[:], op=ALU.mult)
            round_halfup("vector", ofm_hi[:, ts(oc, TSB)],
                         cent[:, ts(oc, TSB)], sq2[:, 0:TSB],
                         sq2[:, TSB:2 * TSB])
        op("vector", "engine_nop", inc=s_ve)
        if "d_qlhs" in dbg:
            p.wait("scalar", s_ve)
            for oc in range(8):
                op("scalar", "activation", out=y1sq[:, ts(oc, TSB)][:, 0:TSB],
                   in_=ofm_hi[:, ts(oc, TSB)], func=AF.Copy,
                   inc=(s_ac if oc == 7 else None))
            p.wait("scalar", s_ac)
            st(dbg["d_qlhs"][tl], y1sq[:, 0:4096])
            stq.wait_all("scalar")
            stq.wait_all("vector")
        p.wait("tensor", s_ve)
        for oc in range(8):
            for k in range(8):
                op("tensor", "matmul", ps_all[oc][:, 0:TSB],
                   qrhs5[:, k * D + oc * 128:k * D + (oc + 1) * 128],
                   ofm_hi[:, ts(k, TSB)], start=(k == 0), stop=(k == 7),
                   inc=(s_mm if k == 7 else None))
        p.wait("vector", s_mm)
        for oc in range(8):
            op("vector", "tensor_scalar", out=y1sq[:, ts(oc, TSB)],
               in0=ps_all[oc][:, 0:TSB], scalar1=rsc_c[:, oc:oc + 1],
               scalar2=None, op0=ALU.mult)
            op("vector", "tensor_tensor", out=y1sq[:, ts(oc, TSB)],
               in0=y1sq[:, ts(oc, TSB)], in1=t1[:], op=ALU.mult)
            op("vector", "tensor_tensor", out=y1sq[:, ts(oc, TSB)],
               in0=y1sq[:, ts(oc, TSB)], in1=y1[:, ts(oc, TSB)], op=ALU.add,
               inc=(s_ve if oc == 7 else None))
        p.wait("scalar", s_ve)
        for oc in range(8):
            st(out_fm[ts(oc, 128), t0:t0 + TSB], y1sq[:, ts(oc, TSB)])
        stq.wait_all("sync")
        p.barrier(bar)

    return _finish()



def make_fir_toeplitz(fir_long, fir_short):
    T = np.zeros((D, 128, 128), dtype=np.float32)
    wl = fir_long.reshape(D, 64)
    ws = fir_short.reshape(D, 5)
    for po in range(64):
        for k in range(64):
            pp = po + 1 + k
            if pp < 128:
                T[:, pp, po] = wl[:, k]
        for k in range(5):
            pp = po + 60 + k
            if pp < 128:
                T[:, pp, 64 + po] = ws[:, k]
    return T


def stage_inputs(inputs):
    x = np.ascontiguousarray(inputs["L0_attn_hidden_states"], np.float32)
    wq = np.asarray(inputs["L0_attn_q_proj_w"], np.float32)
    wk = np.asarray(inputs["L0_attn_k_proj_w"], np.float32)
    wv = np.asarray(inputs["L0_attn_v_proj_w"], np.float32)
    bp = np.asarray(inputs["L0_attn_b_proj_w"], np.float32)
    wqkvb = np.ascontiguousarray(
        np.concatenate([wq.T, wk.T, wv.T], axis=1))
    wbeta = np.zeros((D, 32), np.float32)
    wbeta[:, 0:4] = bp.T
    convw = np.ascontiguousarray(np.concatenate(
        [np.asarray(inputs["L0_attn_q_conv_w"], np.float32),
         np.asarray(inputs["L0_attn_k_conv_w"], np.float32),
         np.asarray(inputs["L0_attn_v_conv_w"], np.float32)], axis=0))
    fir_toep = make_fir_toeplitz(
        np.asarray(inputs["L0_attn_fir_long_w"], np.float32),
        np.asarray(inputs["L0_attn_fir_short_w"], np.float32))
    fw1 = np.asarray(inputs["L0_attn_fuse_w1"], np.float32)
    shared = dict(
        wqkvb=wqkvb, wbeta=wbeta, convw=convw, fir_toep=fir_toep,
        w1h=np.ascontiguousarray(fw1[:, :D].T),
        w1s=np.ascontiguousarray(fw1[:, D:].T),
        b1=np.ascontiguousarray(np.asarray(
            inputs["L0_attn_fuse_b1"], np.float32).reshape(8, 128).T),
        w2=np.ascontiguousarray(
            np.asarray(inputs["L0_attn_fuse_w2"], np.float32).T),
        b2=np.asarray(inputs["L0_attn_fuse_b2"], np.float32),
        wo=np.ascontiguousarray(
            np.asarray(inputs["L0_attn_o_proj_w"], np.float32).T),
        ffn_rhs=np.ascontiguousarray(inputs["L0_ffn_rhs"], np.float32),
        o_norm=np.asarray(inputs["L0_attn_o_norm_w"], np.float32),
        scal=np.array([inputs["L0_attn_logit_temp"][0],
                       inputs["L0_attn_res_scale_short"][0],
                       inputs["L0_attn_res_scale_long"][0], 0.0], np.float32),
        ident=np.eye(128, dtype=np.float32),
        cvec=np.tile(np.array([1e-5, 1e-12, 1e-4, 1.0, -0.5, -1.0,
                               1.0 / 1024, LN127, -LN127, 0.5, -1e-12, 2.0,
                               0.0, 3.0, 4.0, 5.0], np.float32), (128, 1)),
        ones=np.ones((128, 128), np.float32),
        mask_low=(-np.tril(np.ones((128, 128), np.float32), -1)),
        mask_up0=np.triu(np.ones((128, 128), np.float32), 0),
    )
    in_maps = []
    for c in range(8):
        b, half = c // 2, c % 2
        buf = np.zeros((TBUF, D), np.float32)
        if half == 0:
            buf[TAIL:] = x[b, :TAIL]
        else:
            buf[:] = x[b]
        m = dict(shared)
        m["x_tm"] = buf
        m["x_fm"] = np.ascontiguousarray(buf.T)
        in_maps.append(m)
    return in_maps


_NC_CACHE = {}


def kernel(**inputs):
    in_maps = stage_inputs(inputs)
    if "nc" not in _NC_CACHE:
        _NC_CACHE["nc"] = build_program()[0]
    nc = _NC_CACHE["nc"]
    res = run_bass_kernel_spmd(nc, in_maps, core_ids=list(range(8)))
    out = np.empty((B, L, D), np.float32)
    for c in range(8):
        b, half = c // 2, c % 2
        out[b, half * TAIL:(half + 1) * TAIL] = res.results[c]["out_fm"].T
    return out


# version 08 (final)
# speedup vs baseline: 1.0480x; 1.0079x over previous
import numpy as np
import concourse.bass as bass
import concourse.mybir as mybir
from concourse.bass_utils import run_bass_kernel_spmd

F32 = mybir.dt.float32
BF16 = mybir.dt.bfloat16
AF = mybir.ActivationFunctionType
ALU = mybir.AluOpType
AX = mybir.AxisListType

B, L, D, H, DH = 4, 8192, 1024, 4, 256
TBUF, TAIL = 8192, 4096
TSB = 512
NSB = TBUF // TSB
CK = 128
NCPS = TSB // CK
NCK = TBUF // CK
NW = 3 * D
EPS_LN = 1e-5
EPS_L2 = 1e-12
LN127 = float(np.log(np.float64(127.0)))

DBG = ()
CONSTVALS = [1e-5, 1e-12, 1e-4, 1.0, -0.5, -1.0, 1.0 / 1024, LN127, -LN127,
             0.5, -1e-12, 2.0, 0.0, 3.0, 4.0, 5.0]


def ts(i, n):
    return slice(i * n, (i + 1) * n)


class Sem:
    def __init__(self, cm):
        self._cm = cm
        self.h = cm.__enter__()
        self.n = 0

    def inc(self, k=1):
        self.n += k
        return self.h


class P:
    ENGINES = ("tensor", "vector", "scalar", "gpsimd", "sync")

    def __init__(self, nc):
        self.nc = nc
        self.q = {e: [] for e in self.ENGINES}
        self.sems = []

    def sem(self, name):
        s = Sem(self.nc.semaphore(name))
        self.sems.append(s)
        return s

    def on(self, engine, fn):
        self.q[engine].append(fn)

    def wait(self, engine, sem, val=None):
        v = sem.n if val is None else val
        if v <= 0:
            return
        self.q[engine].append(lambda eng, s=sem.h, vv=v: eng.wait_ge(s, vv))

    def barrier(self, bar):
        target = bar.n + len(self.ENGINES)
        for e in self.ENGINES:
            self.q[e].append(lambda eng, s=bar.h: eng.sem_inc(s, 1))
        bar.n = target
        for e in self.ENGINES:
            self.q[e].append(lambda eng, s=bar.h, v=target: eng.wait_ge(s, v))

    def emit(self):
        def mk(e):
            def run(eng):
                for f in self.q[e]:
                    f(eng)
            return run
        with self.nc.Block() as block:
            block.tensor(mk("tensor"))
            block.vector(mk("vector"))
            block.scalar(mk("scalar"))
            block.gpsimd(mk("gpsimd"))
            block.sync(mk("sync"))


def build_program(debug=DBG, stop_after=None):
    nc = bass.Bass("TRN2", target_bir_lowering=False, debug=False)
    p = P(nc)
    alloc = []

    def sbuf(name, shape, dt=F32):
        cm = nc.sbuf_tensor(name, shape, dt)
        alloc.append(cm)
        return cm.__enter__()

    def psumt(name, shape, dt=F32):
        cm = nc.psum_tensor(name, shape, dt)
        alloc.append(cm)
        return cm.__enter__()

    def I(name, shape, dt=F32):
        return nc.dram_tensor(name, shape, dt, kind="ExternalInput").ap()

    x_tm = I("x_tm", [TBUF, D])
    x_fm = I("x_fm", [D, TBUF])
    wqkvb = I("wqkvb", [D, NW])
    wbeta = I("wbeta", [D, 32])
    convw = I("convw", [3 * D, 4])
    fir_toep = I("fir_toep", [D, 128, 128])
    w1h = I("w1h", [D, D])
    w1s = I("w1s", [16, D])
    b1 = I("b1", [128, 8])
    w2 = I("w2", [D, 4])
    b2 = I("b2", [4])
    wo = I("wo", [D, D])
    ffn_rhs = I("ffn_rhs", [D, D])
    o_norm = I("o_norm", [DH])
    scal = I("scal", [4])
    ident = I("ident", [128, 128])
    ones = I("ones", [128, 128])
    mask_low = I("mask_low", [128, 128])
    mask_up0 = I("mask_up0", [128, 128])
    cvec = I("cvec", [128, 16])

    out_fm = nc.dram_tensor("out_fm", [D, TAIL], F32, kind="ExternalOutput").ap()
    dbg = {}
    for name, shape in [
        ("d_qn", [NSB, 128, 16 * TSB]), ("d_v", [NSB, 128, 8 * TSB]),
        ("d_beta", [NSB, 4, TSB]), ("d_delta", [TAIL, D]),
        ("d_stats", [TAIL // 128, 128, 64]), ("d_fw", [TAIL // 128, 128, 16]),
        ("d_o", [TAIL, D]), ("d_y1", [8, 128, 8 * TSB]), ("d_L", [TAIL, 2 * D]),
        ("d_qlhs", [8, 128, 8 * TSB]),
    ]:
        if name in debug:
            dbg[name] = nc.dram_tensor(name, shape, F32,
                                       kind="ExternalOutput").ap()

    wlo_slab = nc.dram_tensor("wlo_slab", [25, 8, 128, 128], BF16).ap()
    w1h_hi_d = nc.dram_tensor("w1h_hi_d", [8, 128, D], BF16).ap()
    w1h_lo_d = nc.dram_tensor("w1h_lo_d", [8, 128, D], BF16).ap()
    wo_hi_d = nc.dram_tensor("wo_hi_d", [8, 128, D], BF16).ap()
    wo_lo_d = nc.dram_tensor("wo_lo_d", [8, 128, D], BF16).ap()
    v_pad = nc.dram_tensor("v_pad", [64 + TBUF, D], F32).ap()
    delta_d = nc.dram_tensor("delta_d", [TAIL, D], F32).ap()
    l_d = nc.dram_tensor("l_d", [TAIL, 2 * D], F32).ap()
    xn_d = nc.dram_tensor("xn_d", [2, 8, 128, TAIL], BF16).ap()
    o_d = nc.dram_tensor("o_d", [2, TAIL, D], BF16).ap()
    qrhs_d = nc.dram_tensor("qrhs_d", [8, 128, D], BF16).ap()

    bar = p.sem("bar")
    s_mm = p.sem("s_mm")
    s_ve = p.sem("s_ve")
    s_ac = p.sem("s_ac")
    s_gp = p.sem("s_gp")

    class DmaQ:
        def __init__(self, engine, name, depth=6, compute_guard=True):
            self.engine = engine
            self.depth = depth
            self.guard = compute_guard
            self.sems = [p.sem(f"{name}{i}") for i in range(depth)]
            self.cnt = 0

        def dma(self, dst, src):
            s = self.sems[self.cnt % self.depth]
            if self.cnt >= self.depth:
                p.wait(self.engine, s)
            if self.engine == "scalar" and self.guard:
                p.wait("scalar", s_ac)
                p.wait("scalar", s_ve)
            self.cnt += 1
            tgt = s.inc(16)
            p.on(self.engine, lambda eng, d=dst, ss=src, h=tgt:
                 eng.dma_start(out=d, in_=ss).then_inc(h, 16))
            return s, s.n

        def wait_all(self, engine):
            for s in self.sems:
                p.wait(engine, s)

    ldq = DmaQ("sync", "qld", 6)
    ld2q = DmaQ("scalar", "ql2", 4, compute_guard=False)
    stq = DmaQ("scalar", "qst", 6)
    gpq = DmaQ("scalar", "qgp", 4)

    def ld(dst, src):
        ldq.dma(dst, src)

    def st(dst, src):
        stq.dma(dst, src)

    def g2(dst, src):
        gpq.dma(dst, src)

    osems = {"vector": s_ve, "scalar": s_ac, "gpsimd": s_gp}

    def op(engine, meth, *a, inc=None, **kw):
        if engine in osems:
            osem = osems[engine]
            prior = osem.n
            h = osem.inc()

            def f(eng, meth=meth, a=a, kw=kw, h=h, prior=prior):
                if prior > 0:
                    eng.wait_ge(h, prior)
                getattr(eng, meth)(*a, **kw).then_inc(h, 1)
            p.on(engine, f)
        else:
            h = inc.inc() if inc is not None else None

            def f(eng, meth=meth, a=a, kw=kw, h=h):
                inst = getattr(eng, meth)(*a, **kw)
                if h is not None:
                    inst.then_inc(h, 1)
            p.on(engine, f)

    def _finish():
        stq.wait_all("scalar")
        ldq.wait_all("sync")
        gpq.wait_all("gpsimd")
        p.barrier(bar)
        p.emit()
        for cm in reversed(alloc):
            cm.__exit__(None, None, None)
        for s in reversed(p.sems):
            s._cm.__exit__(None, None, None)
        return nc, dbg

    w_hi = sbuf("w_hi", [128, 8 * NW], BF16)
    wbeta_s = sbuf("wbeta_s", [128, 8 * 32], BF16)
    w_hiF = w_hi[:].bitcast(F32)
    wkl = [sbuf(f"wkl{i}", [128, 1024], BF16) for i in range(3)]
    ident_s = sbuf("ident_s", [128, 128])
    identb = sbuf("identb", [128, 128], BF16)
    consts_s = sbuf("consts_s", [128, 16])
    for _i, _v in enumerate(CONSTVALS):
        nc.const_aps.register(F32, _v, consts_s[:, _i:_i + 1])
    ones_s = sbuf("ones_s", [128, 128])
    mlow_s = sbuf("mlow_s", [128, 128])
    mup0_s = sbuf("mup0_s", [128, 128])
    cw_s = sbuf("cw_s", [128, 96])
    w1s_s = sbuf("w1s_s", [16, D])
    b1_s = sbuf("b1_s", [128, 8])
    w2_s = sbuf("w2_s", [128, 32])
    b2_s = sbuf("b2_s", [4, 1])
    scal_s = sbuf("scal_s", [4, 1])
    sc_rep = sbuf("sc_rep", [128, 4])
    onorm_b = sbuf("onorm_b", [128, DH])
    rsc_c = sbuf("rsc_c", [128, 8])
    xtm_t = sbuf("xtm_t", [128, D])
    stat_c = sbuf("stat_c", [128, 16])
    scr = sbuf("scr", [128, D])
    l2row = sbuf("l2row", [1, 1024])
    mu_b = sbuf("mu_b", [128, TSB])
    rs_b = sbuf("rs_b", [128, TSB])
    xfm_t = sbuf("xfm_t", [128, 4 * TSB])
    t1 = sbuf("t1", [128, TSB])
    xnhi = sbuf("xnhi", [128, 8 * TSB], BF16)
    xnlo = sbuf("xnlo", [128, 8 * TSB], BF16)
    pout = sbuf("pout", [128, 16 * TSB])
    v_out = sbuf("v_out", [128, 8 * TSB])
    raw = sbuf("raw", [128, TSB + 4])
    hist = sbuf("hist", [128, 96])
    sq2 = sbuf("sq2", [128, 2 * TSB])
    beta_r = sbuf("beta_r", [4, TSB])
    beta_tm = sbuf("beta_tm", [128, 4 * NCPS])
    big3 = sbuf("big3", [128, 4096])
    a4 = sbuf("a4", [128, 4 * 128])
    at4 = sbuf("at4", [128, 4 * 128])
    tta = sbuf("tta", [128, 4 * 128])
    ttb = sbuf("ttb", [128, 4 * 128])
    attn4 = sq2[:, 512:1024]
    kntm4 = sbuf("kntm4", [128, 4 * DH])
    vbtm4 = sbuf("vbtm4", [128, 4 * DH])
    kbtm4 = sbuf("kbtm4", [128, 4 * DH])
    utm4 = sbuf("utm4", [128, 4 * DH])
    wfm4 = sbuf("wfm4", [128, 4 * DH])
    uadj = sbuf("uadj", [128, DH])
    doutb = uadj
    ssb = sq2[:, 0:512]
    lgs = beta_r[:, 0:TSB]
    lgtm = sbuf("lgtm", [128, 64])
    fwc = sbuf("fwc", [128, 16])
    rms_c = sbuf("rms_c", [128, 16])
    mx4 = sbuf("mx4", [128, 8])
    whh = wkl
    ttoep = [v_out[:, 0:128], v_out[:, 128:256]]
    rbuf = pout[:, 0:8192]
    scat = w_hiF[:, 0:8192]

    ps_s = [psumt(f"ps_s{h}", [128, 512]) for h in range(H)]
    ps_w = [psumt(f"ps_w{i}", [128, 512]) for i in range(4)]
    ps_all = ps_s + ps_w

    ld(consts_s[:], cvec[:])
    ld(ident_s[:], ident[:])
    ldq.wait_all("vector")
    op("vector", "tensor_copy", out=identb[:], in_=ident_s[:])
    ld(ones_s[:], ones[:])
    ld(mlow_s[:], mask_low[:])
    ld(mup0_s[:], mask_up0[:])
    ld(cw_s[:].rearrange("b (a c) -> b a c", a=24),
       convw.rearrange("(a b) c -> b a c", b=128))
    ld(w1s_s[:], w1s[:])
    ld(b1_s[:], b1[:])
    ld(w2_s[:].rearrange("b (a c) -> b a c", a=8),
       w2.rearrange("(a b) c -> b a c", b=128))
    ld(b2_s[:], b2[:, None])
    ld(scal_s[:], scal[:, None])
    ldq.wait_all("scalar")
    ldq.wait_all("vector")
    ldq.wait_all("gpsimd")
    ldq.wait_all("tensor")

    for k in range(8):
        ld(big3[:, 0:NW] if NW <= 4096 else None, wqkvb[ts(k, 128), :])
        ldq.wait_all("vector")
        op("vector", "tensor_copy", out=w_hi[:, k * NW:(k + 1) * NW],
           in_=big3[:, 0:NW])
        op("vector", "tensor_tensor", out=big3[:, 0:NW], in0=big3[:, 0:NW],
           in1=w_hi[:, k * NW:(k + 1) * NW], op=ALU.subtract)
        op("vector", "tensor_copy", out=xnhi[:, 0:NW], in_=big3[:, 0:NW],
           inc=s_ve)
        p.wait("scalar", s_ve)
        for oc in range(24):
            st(wlo_slab[oc, k], xnhi[:, ts(oc, 128)])
        stq.wait_all("sync")
        stq.wait_all("vector")
    ld(scr[:, 0:256].rearrange("p (k c) -> p k c", k=8),
       wbeta.rearrange("(k p) c -> p k c", p=128))
    ldq.wait_all("vector")
    op("vector", "tensor_copy", out=wbeta_s[:], in_=scr[:, 0:256])
    op("vector", "tensor_tensor", out=scr[:, 0:256], in0=scr[:, 0:256],
       in1=wbeta_s[:], op=ALU.subtract)
    op("vector", "tensor_copy", out=xnhi[:, 0:256], in_=scr[:, 0:256],
       inc=s_ve)
    p.wait("scalar", s_ve)
    for k in range(8):
        st(wlo_slab[24, k][:, 0:32], xnhi[:, k * 32:(k + 1) * 32])
    stq.wait_all("sync")
    stq.wait_all("vector")
    for src, dhi, dlo in ((w1h, w1h_hi_d, w1h_lo_d), (wo, wo_hi_d, wo_lo_d)):
        for k in range(8):
            ld(big3[:, 0:D], src[ts(k, 128), :])
            ldq.wait_all("vector")
            op("vector", "tensor_copy", out=xnhi[:, 0:D], in_=big3[:, 0:D])
            op("vector", "tensor_tensor", out=big3[:, 0:D],
               in0=big3[:, 0:D], in1=xnhi[:, 0:D], op=ALU.subtract)
            op("vector", "tensor_copy", out=xnlo[:, 0:D], in_=big3[:, 0:D],
               inc=s_ve)
            p.wait("scalar", s_ve)
            st(dhi[k], xnhi[:, 0:D])
            st(dlo[k], xnlo[:, 0:D])
            stq.wait_all("sync")
            stq.wait_all("vector")

    ld(scr[0:1, 0:DH], o_norm[None, :])
    ldq.wait_all("tensor")
    op("tensor", "matmul", ps_w[0][:, 0:DH], ones_s[0:1, :], scr[0:1, 0:DH],
       start=True, stop=True)
    op("tensor", "transpose", ps_w[1][0:1, 0:4], scal_s[:],
       ident_s[0:4, 0:4], inc=s_mm)
    p.wait("scalar", s_mm)
    op("scalar", "copy", out=onorm_b[:], in_=ps_w[0][:, 0:DH])
    op("scalar", "copy", out=l2row[0:1, 0:4], in_=ps_w[1][0:1, 0:4],
       inc=s_ac)
    p.wait("tensor", s_ac)
    op("tensor", "matmul", ps_w[2][:, 0:4], ones_s[0:1, :], l2row[0:1, 0:4],
       start=True, stop=True, inc=s_mm)
    p.wait("scalar", s_mm)
    op("scalar", "copy", out=sc_rep[:], in_=ps_w[2][:, 0:4], inc=s_ac)

    ldq.wait_all("vector")
    p.wait("vector", s_mm)
    p.wait("vector", s_ac)
    op("vector", "memset", hist[:], 0.0)
    op("vector", "memset", scr[:, 0:D], 0.0, inc=s_ve)
    p.wait("scalar", s_ve)
    st(v_pad[0:64, :], scr[0:64, 0:D])

    ffw = pout
    ffa = v_out
    for k in range(8):
        ld(ffw[:, ts(k, D)], ffn_rhs[ts(k, 128), :])
    ldq.wait_all("vector")
    for j in range(4):
        op("vector", "tensor_tensor", out=ffa[:, ts(j, D)],
           in0=ffw[:, ts(2 * j, D)], in1=ffw[:, ts(2 * j + 1, D)], op=ALU.max)
        op("vector", "scalar_tensor_tensor", out=ffa[:, ts(j, D)],
           in0=ffw[:, ts(2 * j, D)], scalar=-1.0, in1=ffa[:, ts(j, D)],
           op0=ALU.mult, op1=ALU.max)
        op("vector", "scalar_tensor_tensor", out=ffa[:, ts(j, D)],
           in0=ffw[:, ts(2 * j + 1, D)], scalar=-1.0, in1=ffa[:, ts(j, D)],
           op0=ALU.mult, op1=ALU.max)
    for j in range(2):
        op("vector", "tensor_tensor", out=ffa[:, ts(j, D)],
           in0=ffa[:, ts(2 * j, D)], in1=ffa[:, ts(2 * j + 1, D)], op=ALU.max)
    op("vector", "tensor_tensor", out=ffa[:, 0:D], in0=ffa[:, 0:D],
       in1=ffa[:, ts(1, D)], op=ALU.max, inc=s_ve)
    for j in range(8):
        p.wait("tensor", s_ve)
        p.wait("tensor", s_ac)
        op("tensor", "transpose", ps_w[0][:, 0:128], ffa[:, ts(j, 128)],
           ident_s[:], inc=s_mm)
        p.wait("vector", s_mm)
        op("vector", "reduce_max", out=rsc_c[:, j:j + 1],
           in_=ps_w[0][:, 0:128], axis=AX.X, inc=s_ve)
    op("vector", "tensor_scalar", out=rsc_c[:, 0:8], in0=rsc_c[:, 0:8],
       scalar1=1e-12, scalar2=1.0 / 127.0, op0=ALU.max, op1=ALU.mult,
       inc=s_ve)
    p.wait("scalar", s_ve)
    op("scalar", "activation", out=stat_c[:, 0:8], in_=rsc_c[:, 0:8],
       func=AF.Ln)
    op("scalar", "activation", out=stat_c[:, 0:8], in_=stat_c[:, 0:8],
       func=AF.Exp, scale=-1.0, inc=s_ac)
    p.wait("tensor", s_ac)
    for j in range(8):
        op("tensor", "transpose", ps_w[j // 4][0:1, ts(j % 4, 128)],
           stat_c[:, j:j + 1], ident_s[:], inc=(s_mm if j == 7 else None))
    p.wait("scalar", s_mm)
    op("scalar", "copy", out=l2row[:, 0:512], in_=ps_w[0][0:1, 0:512])
    op("scalar", "copy", out=l2row[:, 512:1024], in_=ps_w[1][0:1, 0:512],
       inc=s_ac)
    for j in range(8):
        p.wait("tensor", s_ac)
        p.wait("tensor", s_ve)
        op("tensor", "matmul", ps_w[1][:, 0:128], ones_s[0:1, :],
           l2row[0:1, ts(j, 128)], start=True, stop=True, inc=s_mm)
        p.wait("vector", s_mm)
        op("vector", "tensor_copy", out=sq2[:, ts(j, 128)],
           in_=ps_w[1][:, 0:128], inc=s_ve)
    srep = sq2
    p.wait("vector", s_ac)

    def round_halfup(engine, dst_bf16, src_f32, tmp, tmp2):
        op(engine, "tensor_scalar", out=tmp, in0=src_f32, scalar1=128.5,
           scalar2=None, op0=ALU.add)
        for c in (256.0, 128.0, 64.0, 32.0, 16.0, 8.0, 4.0, 2.0, 1.0):
            op(engine, "tensor_scalar", out=tmp2, in0=tmp, scalar1=c,
               scalar2=c, op0=ALU.is_ge, op1=ALU.mult)
            op(engine, "tensor_tensor", out=tmp, in0=tmp, in1=tmp2,
               op=ALU.subtract)
        op(engine, "scalar_tensor_tensor", out=tmp2, in0=src_f32, scalar=0.5,
           in1=tmp, op0=ALU.add, op1=ALU.subtract)
        op(engine, "tensor_copy", out=dst_bf16, in_=tmp2)

    for k in range(8):
        op("vector", "tensor_tensor", out=ffw[:, ts(k, D)],
           in0=ffw[:, ts(k, D)], in1=srep[:, 0:D], op=ALU.mult)
        round_halfup("vector", xnhi[:, 0:D], ffw[:, ts(k, D)],
                     big3[:, 0:D], big3[:, D:2 * D])
        op("vector", "engine_nop", inc=s_ve)
        p.wait("scalar", s_ve)
        st(qrhs_d[k], xnhi[:, 0:D])
        stq.wait_all("vector")
    p.barrier(bar)
    if stop_after == "PH0":
        return _finish()

    def ln_and_xn(sbi):
        t0 = sbi * TSB
        for j in range(4):
            ld(xtm_t[:], x_tm[t0 + j * 128:t0 + (j + 1) * 128, :])
            ldq.wait_all("vector")
            ldq.wait_all("scalar")
            op("vector", "tensor_scalar", out=scr[:, 0:D], in0=xtm_t[:],
               scalar1=1.0 / D, scalar2=None, op0=ALU.mult, op1=ALU.add,
               accum_out=stat_c[:, j:j + 1], inc=s_ve)
            op("scalar", "activation", out=sq2[:, 0:D], in_=xtm_t[:],
               func=AF.Square, accum_out=stat_c[:, 4 + j:5 + j], inc=s_ac)
            p.wait("sync", s_ve)
            p.wait("sync", s_ac)
        p.wait("vector", s_ac)
        op("vector", "tensor_tensor", out=stat_c[:, 8:12],
           in0=stat_c[:, 0:4], in1=stat_c[:, 0:4], op=ALU.mult)
        op("vector", "tensor_scalar", out=stat_c[:, 4:8], in0=stat_c[:, 4:8],
           scalar1=1.0 / D, scalar2=None, op0=ALU.mult)
        op("vector", "tensor_tensor", out=stat_c[:, 4:8], in0=stat_c[:, 4:8],
           in1=stat_c[:, 8:12], op=ALU.subtract, inc=s_ve)
        p.wait("scalar", s_ve)
        op("scalar", "activation", out=stat_c[:, 4:8], in_=stat_c[:, 4:8],
           func=AF.Ln, bias=EPS_LN)
        op("scalar", "activation", out=stat_c[:, 4:8], in_=stat_c[:, 4:8],
           func=AF.Exp, scale=-0.5, inc=s_ac)
        p.wait("tensor", s_ac)
        for j in range(8):
            op("tensor", "transpose", ps_w[j // 4][0:1, ts(j % 4, 128)],
               stat_c[:, j:j + 1], ident_s[:],
               inc=(s_mm if j == 7 else None))
        p.wait("scalar", s_mm)
        op("scalar", "copy", out=l2row[:, 0:512], in_=ps_w[0][0:1, 0:512])
        op("scalar", "copy", out=l2row[:, 512:1024],
           in_=ps_w[1][0:1, 0:512], inc=s_ac)
        p.wait("tensor", s_ac)
        for j in range(4):
            op("tensor", "matmul", ps_w[1][:, 0:128], ones_s[0:1, :],
               l2row[0:1, ts(j, 128)], start=True, stop=True)
            op("tensor", "matmul", ps_w[2][:, 0:128], ones_s[0:1, :],
               l2row[0:1, ts(4 + j, 128)], start=True, stop=True, inc=s_mm)
            p.wait("scalar", s_mm)
            op("scalar", "copy", out=mu_b[:, ts(j, 128)],
               in_=ps_w[1][:, 0:128])
            op("scalar", "copy", out=rs_b[:, ts(j, 128)],
               in_=ps_w[2][:, 0:128], inc=s_ac)
            p.wait("tensor", s_ac)
        x3 = x_fm.rearrange("(a b) c -> b a c", b=128)
        for hf in range(2):
            ld(xfm_t[:].rearrange("p (a c) -> p a c", a=4),
               x3[:, hf * 4:(hf + 1) * 4, t0:t0 + TSB])
            ldq.wait_all("vector")
            p.wait("vector", s_ac)
            for kk in range(4):
                k = hf * 4 + kk
                op("vector", "tensor_tensor", out=t1[:],
                   in0=xfm_t[:, ts(kk, TSB)], in1=mu_b[:], op=ALU.subtract)
                op("vector", "tensor_tensor", out=t1[:], in0=t1[:],
                   in1=rs_b[:], op=ALU.mult, inc=s_ve)
                p.wait("scalar", s_ve)
                op("scalar", "activation", out=xnhi[:, ts(k, TSB)],
                   in_=t1[:], func=AF.Copy, inc=s_ac)
                p.wait("vector", s_ac)
                op("vector", "tensor_tensor", out=xnlo[:, ts(k, TSB)],
                   in0=t1[:], in1=xnhi[:, ts(k, TSB)], op=ALU.subtract,
                   inc=s_ve)
            p.wait("sync", s_ve)
        if t0 >= TAIL:
            p.wait("scalar", s_ve)
            for k in range(8):
                st(xn_d[0, k][:, t0 - TAIL:t0 - TAIL + TSB],
                   xnhi[:, ts(k, TSB)])
                st(xn_d[1, k][:, t0 - TAIL:t0 - TAIL + TSB],
                   xnlo[:, ts(k, TSB)])

    def proj_conv(sbi, oc_limit=25, do_tail=True, tail_lim=99,
                  bank_override=None):
        mm_marks = []
        copy_marks = []
        ld_marks = []
        for oc in range(oc_limit):
            bank = bank_override or ps_w[oc % 2]
            ncols = 128 if oc < 24 else 32
            buf = wkl[oc % 3]
            if len(mm_marks) >= 3:
                p.wait("sync", s_mm, mm_marks[-3])
                p.wait("scalar", s_mm, mm_marks[-3])
            if ncols == 128:
                m1 = ldq.dma(
                    buf[:, 0:4 * 128].rearrange("p (k c) -> p k c", k=4),
                    wlo_slab[oc][0:4].rearrange("k p c -> p k c"))
                m2 = ld2q.dma(
                    buf[:, 4 * 128:8 * 128].rearrange("p (k c) -> p k c",
                                                      k=4),
                    wlo_slab[oc][4:8].rearrange("k p c -> p k c"))
            else:
                m1 = ldq.dma(
                    buf[:, 0:128].rearrange("p (k c) -> p k c", k=4),
                    wlo_slab[oc][0:4, :, 0:32].rearrange("k p c -> p k c"))
                m2 = ld2q.dma(
                    buf[:, 128:256].rearrange("p (k c) -> p k c", k=4),
                    wlo_slab[oc][4:8, :, 0:32].rearrange("k p c -> p k c"))
            ld_marks.append((m1, m2))
            if len(copy_marks) >= 2:
                p.wait("tensor", s_ac, copy_marks[-2])
            for k in range(8):
                hi = (w_hi[:, k * NW + oc * 128:k * NW + oc * 128 + 128]
                      if oc < 24 else wbeta_s[:, k * 32:(k + 1) * 32])
                op("tensor", "matmul", bank[0:ncols, 0:TSB], hi,
                   xnhi[:, ts(k, TSB)], start=(k == 0), stop=False)
                op("tensor", "matmul", bank[0:ncols, 0:TSB], hi,
                   xnlo[:, ts(k, TSB)], start=False, stop=False)
            for sem, tgt in ld_marks[-1]:
                p.wait("tensor", sem, tgt)
            for k in range(8):
                op("tensor", "matmul", bank[0:ncols, 0:TSB],
                   buf[:, k * ncols:(k + 1) * ncols], xnhi[:, ts(k, TSB)],
                   start=False, stop=(k == 7), inc=(s_mm if k == 7 else None))
            mm_marks.append(s_mm.n)
            if not do_tail:
                copy_marks.append(s_ac.n)
                continue
            if oc < 24:
                if tail_lim > 0 and tail_lim != 115:
                    p.wait("scalar", s_mm)
                    op("scalar", "copy", out=raw[:, 4:4 + TSB],
                       in_=bank[0:128, 0:TSB], inc=s_ac)
                    copy_marks.append(s_ac.n)
                dst = (pout[:, ts(oc, TSB)] if oc < 16
                       else v_out[:, ts(oc - 16, TSB)])
                if tail_lim == 101:
                    p.wait("vector", s_mm)
                    op("vector", "tensor_copy", out=dst,
                       in_=bank[0:128, 0:TSB], inc=s_ve)
                elif tail_lim == 102:
                    p.wait("vector", s_mm)
                    op("vector", "tensor_scalar", out=dst,
                       in0=xnhi[:, 0:TSB],
                       scalar1=cw_s[:, oc * 4 + 3:oc * 4 + 4], scalar2=None,
                       op0=ALU.mult, inc=s_ve)
                elif tail_lim == 103:
                    p.wait("vector", s_mm)
                    op("vector", "tensor_scalar", out=sq2[:, 0:TSB],
                       in0=bank[0:128, 0:TSB],
                       scalar1=cw_s[:, oc * 4 + 3:oc * 4 + 4], scalar2=None,
                       op0=ALU.mult, inc=s_ve)
                elif tail_lim == 104:
                    p.wait("vector", s_mm)
                elif tail_lim == 105:
                    op("vector", "tensor_copy", out=sq2[:, 0:TSB],
                       in_=xnhi[:, 0:TSB], inc=s_ve)
                elif tail_lim == 115:
                    p.wait("vector", s_mm)
                    op("vector", "tensor_scalar", out=dst,
                       in0=bank[0:128, 0:TSB],
                       scalar1=cw_s[:, oc * 4 + 3:oc * 4 + 4], scalar2=None,
                       op0=ALU.mult, inc=s_ve)
                elif tail_lim == 110:
                    p.wait("vector", s_ac)
                    op("vector", "tensor_scalar", out=sq2[:, 0:TSB],
                       in0=bank[0:128, 0:TSB],
                       scalar1=cw_s[:, oc * 4 + 3:oc * 4 + 4], scalar2=None,
                       op0=ALU.mult, inc=s_ve)
                elif tail_lim == 111:
                    p.wait("gpsimd", s_mm)
                    op("gpsimd", "tensor_scalar", out=sq2[:, 0:TSB],
                       in0=bank[0:128, 0:TSB],
                       scalar1=cw_s[:, oc * 4 + 3:oc * 4 + 4], scalar2=None,
                       op0=ALU.mult, inc=s_gp)
                elif tail_lim == 106:
                    p.wait("vector", s_mm)
                    op("vector", "tensor_copy", out=sq2[:, 0:128],
                       in_=bank[0:128, 0:128], inc=s_ve)
                elif tail_lim == 107:
                    p.wait("vector", s_mm)
                    for q4 in range(4):
                        op("vector", "tensor_copy",
                           out=sq2[:, ts(q4, 128)],
                           in_=bank[0:128, ts(q4, 128)],
                           inc=(s_ve if q4 == 3 else None))
                elif tail_lim > 1:
                    p.wait("vector", s_ac)
                    op("vector", "tensor_scalar", out=dst,
                       in0=raw[:, 4:4 + TSB],
                       scalar1=cw_s[:, oc * 4 + 3:oc * 4 + 4], scalar2=None,
                       op0=ALU.mult, inc=s_ve)
                if 2 < tail_lim <= 99:
                    op("vector", "tensor_copy", out=raw[:, 0:4],
                       in_=hist[:, oc * 4:oc * 4 + 4])
                for jt in range(3):
                    if 3 + jt < tail_lim <= 99:
                        op("vector", "scalar_tensor_tensor", out=dst,
                           in0=raw[:, 1 + jt:1 + jt + TSB],
                           scalar=cw_s[:, oc * 4 + jt:oc * 4 + jt + 1],
                           in1=dst, op0=ALU.mult, op1=ALU.add)
                if 6 < tail_lim <= 99:
                    op("vector", "tensor_copy",
                       out=hist[:, oc * 4:oc * 4 + 4],
                       in_=raw[:, TSB:TSB + 4], inc=s_ve)
                if 7 < tail_lim <= 99:
                    p.wait("scalar", s_ve)
                    op("scalar", "activation", out=dst, in_=dst, func=AF.Silu,
                       inc=s_ac)
            else:
                p.wait("scalar", s_mm)
                op("scalar", "activation", out=beta_r[:],
                   in_=bank[0:4, 0:TSB], func=AF.Sigmoid, inc=s_ac)
                copy_marks.append(s_ac.n)
        p.wait("vector", s_ac)
        op("vector", "engine_nop", inc=s_ve)

    def l2norm(sbi):
        for tnsr in range(2):
            for h in range(H):
                oc0 = tnsr * 8 + 2 * h
                bank = ps_w[(tnsr * H + h) % 2]
                p.wait("scalar", s_ve)
                op("scalar", "activation", out=sq2[:, 0:TSB],
                   in_=pout[:, ts(oc0, TSB)], func=AF.Square)
                op("scalar", "activation", out=sq2[:, TSB:2 * TSB],
                   in_=pout[:, ts(oc0 + 1, TSB)], func=AF.Square, inc=s_ac)
                p.wait("tensor", s_ac)
                p.wait("tensor", s_ve)
                op("tensor", "matmul", bank[:, 0:TSB], ones_s[:],
                   sq2[:, 0:TSB], start=True, stop=False)
                op("tensor", "matmul", bank[:, 0:TSB], ones_s[:],
                   sq2[:, TSB:2 * TSB], start=False, stop=True, inc=s_mm)
                p.wait("scalar", s_mm)
                op("scalar", "activation", out=t1[:], in_=bank[:, 0:TSB],
                   func=AF.Ln, bias=EPS_L2)
                op("scalar", "activation", out=t1[:], in_=t1[:], func=AF.Exp,
                   scale=-0.5, inc=s_ac)
                p.wait("vector", s_ac)
                op("vector", "tensor_tensor", out=pout[:, ts(oc0, TSB)],
                   in0=pout[:, ts(oc0, TSB)], in1=t1[:], op=ALU.mult)
                op("vector", "tensor_tensor", out=pout[:, ts(oc0 + 1, TSB)],
                   in0=pout[:, ts(oc0 + 1, TSB)], in1=t1[:], op=ALU.mult,
                   inc=s_ve)

    def delta(sbi):
        t0 = sbi * TSB
        p.wait("tensor", s_ve)
        p.wait("tensor", s_ac)
        for ci in range(NCPS):
            op("tensor", "transpose", ps_w[3][:, ts(ci, 4)],
               beta_r[:, ts(ci, 128)], ident_s[0:4, 0:4],
               inc=(s_mm if ci == NCPS - 1 else None))
        p.wait("scalar", s_mm)
        op("scalar", "copy", out=beta_tm[:], in_=ps_w[3][:, 0:16], inc=s_ac)

        for h in range(H):
            gc0 = sbi * NCPS
            tail0 = gc0 * CK >= TAIL

            def qsl(c, ci):
                base = (2 * h + c) * TSB + ci * 128
                return pout[:, base:base + 128]

            def ksl(c, ci):
                base = (8 + 2 * h + c) * TSB + ci * 128
                return pout[:, base:base + 128]

            def vsl(c, ci):
                base = (2 * h + c) * TSB + ci * 128
                return v_out[:, base:base + 128]

            p.wait("tensor", s_ac)
            p.wait("tensor", s_ve)
            for ci in range(NCPS):
                op("tensor", "transpose", ps_w[ci][:, 0:128], ksl(0, ci),
                   ident_s[:])
                op("tensor", "transpose", ps_w[ci][:, 128:256], ksl(1, ci),
                   ident_s[:])
                op("tensor", "transpose", ps_w[ci][:, 256:384], vsl(0, ci),
                   ident_s[:])
                op("tensor", "transpose", ps_w[ci][:, 384:512], vsl(1, ci),
                   ident_s[:], inc=(s_mm if ci == NCPS - 1 else None))
            p.wait("scalar", s_mm)
            p.wait("vector", s_mm)
            for ci in range(NCPS):
                eng, cp, sm = (("scalar", "copy", s_ac) if ci < 2
                               else ("vector", "tensor_copy", s_ve))
                op(eng, cp, out=kntm4[:, ts(ci, DH)],
                   in_=ps_w[ci][:, 0:256])
                op(eng, cp, out=vbtm4[:, ts(ci, DH)],
                   in_=ps_w[ci][:, 256:512],
                   inc=(sm if ci % 2 == 1 else None))
            p.wait("scalar", s_ve)
            for ci in range(NCPS):
                st(v_pad[64 + t0 + ci * 128:64 + t0 + (ci + 1) * 128,
                         h * DH:(h + 1) * DH], vbtm4[:, ts(ci, DH)])
            stq.wait_all("vector")
            p.wait("vector", s_ac)
            for ci in range(NCPS):
                bcol = beta_tm[:, ci * 4 + h:ci * 4 + h + 1]
                op("vector", "tensor_scalar", out=kbtm4[:, ts(ci, DH)],
                   in0=kntm4[:, ts(ci, DH)], scalar1=bcol, scalar2=None,
                   op0=ALU.mult)
                op("vector", "tensor_scalar", out=vbtm4[:, ts(ci, DH)],
                   in0=vbtm4[:, ts(ci, DH)], scalar1=bcol, scalar2=None,
                   op0=ALU.mult)
            op("vector", "engine_nop", inc=s_ve)
            p.wait("tensor", s_ve)
            for ci in range(NCPS):
                last = (ci == NCPS - 1)
                op("tensor", "matmul", ps_w[ci][:, 0:128], ksl(0, ci),
                   ksl(0, ci), start=True, stop=False)
                op("tensor", "matmul", ps_w[ci][:, 0:128], ksl(1, ci),
                   ksl(1, ci), start=False, stop=True,
                   inc=(s_mm if last and not tail0 else None))
                if tail0:
                    op("tensor", "matmul", ps_w[ci][:, 128:256], ksl(0, ci),
                       qsl(0, ci), start=True, stop=False)
                    op("tensor", "matmul", ps_w[ci][:, 128:256], ksl(1, ci),
                       qsl(1, ci), start=False, stop=True,
                       inc=(s_mm if last else None))
            p.wait("vector", s_mm)
            for ci in range(NCPS):
                bcol = beta_tm[:, ci * 4 + h:ci * 4 + h + 1]
                op("vector", "tensor_scalar", out=a4[:, ts(ci, 128)],
                   in0=ps_w[ci][:, 0:128], scalar1=bcol, scalar2=None,
                   op0=ALU.mult)
                op("vector", "tensor_tensor", out=a4[:, ts(ci, 128)],
                   in0=a4[:, ts(ci, 128)], in1=mlow_s[:], op=ALU.mult)
                if tail0:
                    op("vector", "tensor_tensor", out=attn4[:, ts(ci, 128)],
                       in0=ps_w[ci][:, 128:256], in1=mup0_s[:], op=ALU.mult)
            op("vector", "engine_nop", inc=s_ve)
            p.wait("tensor", s_ve)
            for ci in range(NCPS):
                op("tensor", "transpose", ps_w[ci][:, 256:384],
                   a4[:, ts(ci, 128)], ident_s[:],
                   inc=(s_mm if ci == NCPS - 1 else None))
            p.wait("scalar", s_mm)
            for ci in range(NCPS):
                op("scalar", "copy", out=at4[:, ts(ci, 128)],
                   in_=ps_w[ci][:, 256:384],
                   inc=(s_ac if ci == NCPS - 1 else None))
            p.wait("vector", s_ac)
            for ci in range(NCPS):
                op("vector", "tensor_tensor", out=tta[:, ts(ci, 128)],
                   in0=at4[:, ts(ci, 128)], in1=ident_s[:], op=ALU.add)
            op("vector", "engine_nop", inc=s_ve)
            tt_cur, tt_nxt = tta, ttb
            for j in range(1, 7):
                p.wait("tensor", s_ve)
                p.wait("tensor", s_ac)
                for ci in range(NCPS):
                    last = (ci == NCPS - 1)
                    op("tensor", "matmul", ps_w[ci][:, 0:128],
                       at4[:, ts(ci, 128)], a4[:, ts(ci, 128)],
                       start=True, stop=True,
                       inc=(s_mm if last and j == 6 else None))
                    if j < 6:
                        op("tensor", "matmul", ps_w[ci][:, 128:256],
                           a4[:, ts(ci, 128)], at4[:, ts(ci, 128)],
                           start=True, stop=True,
                           inc=(s_mm if last else None))
                p.wait("scalar", s_mm)
                p.wait("vector", s_mm)
                for ci in range(NCPS):
                    last = (ci == NCPS - 1)
                    eng, cp, sm = (("scalar", "copy", s_ac) if ci < 2
                                   else ("vector", "tensor_copy", s_ve))
                    op(eng, cp, out=a4[:, ts(ci, 128)],
                       in_=ps_w[ci][:, 0:128],
                       inc=(sm if ci % 2 == 1 and j == 6 else None))
                    if j < 6:
                        op(eng, cp, out=at4[:, ts(ci, 128)],
                           in_=ps_w[ci][:, 128:256],
                           inc=(sm if ci % 2 == 1 else None))
                p.wait("tensor", s_ac)
                p.wait("tensor", s_ve)
                for ci in range(NCPS):
                    op("tensor", "matmul", ps_w[ci][:, 256:384],
                       a4[:, ts(ci, 128)], tt_cur[:, ts(ci, 128)],
                       start=True, stop=True,
                       inc=(s_mm if ci == NCPS - 1 else None))
                p.wait("vector", s_mm)
                for ci in range(NCPS):
                    op("vector", "scalar_tensor_tensor",
                       out=tt_nxt[:, ts(ci, 128)], in0=ps_w[ci][:, 256:384],
                       scalar=1.0, in1=tt_cur[:, ts(ci, 128)],
                       op0=ALU.mult, op1=ALU.add)
                op("vector", "engine_nop", inc=s_ve)
                tt_cur, tt_nxt = tt_nxt, tt_cur
            p.wait("tensor", s_ve)
            for ci in range(NCPS):
                op("tensor", "matmul", ps_w[ci][:, 0:256],
                   tt_cur[:, ts(ci, 128)], vbtm4[:, ts(ci, DH)],
                   start=True, stop=True)
                op("tensor", "matmul", ps_w[ci][:, 256:384],
                   kbtm4[:, ci * DH:ci * DH + 128], tt_cur[:, ts(ci, 128)],
                   start=True, stop=True)
                op("tensor", "matmul", ps_w[ci][:, 384:512],
                   kbtm4[:, ci * DH + 128:ci * DH + 256],
                   tt_cur[:, ts(ci, 128)], start=True, stop=True,
                   inc=(s_mm if ci == NCPS - 1 else None))
            p.wait("scalar", s_mm)
            p.wait("vector", s_mm)
            for ci in range(NCPS):
                eng, cp, sm = (("scalar", "copy", s_ac) if ci < 2
                               else ("vector", "tensor_copy", s_ve))
                op(eng, cp, out=utm4[:, ts(ci, DH)],
                   in_=ps_w[ci][:, 0:256])
                op(eng, cp, out=wfm4[:, ts(ci, DH)],
                   in_=ps_w[ci][:, 256:512],
                   inc=(sm if ci % 2 == 1 else None))
            for ci in range(NCPS):
                gc = gc0 + ci
                tail = gc * CK >= TAIL
                p.wait("tensor", s_ac)
                p.wait("tensor", s_ve)
                if gc > 0:
                    p.wait("scalar", s_mm)
                    p.wait("scalar", s_ve)
                    p.wait("vector", s_mm)
                    p.wait("vector", s_ac)
                    op("scalar", "copy", out=ssb[:, 0:512],
                       in_=ps_s[h][:, 0:512], inc=s_ac)
                    op("vector", "engine_nop", inc=s_ve)
                    p.wait("tensor", s_ac)
                    p.wait("tensor", s_ve)
                    op("tensor", "matmul", ps_w[0][:, 0:256],
                       wfm4[:, ci * DH:ci * DH + 128], ssb[:, 0:256],
                       start=True, stop=False, skip_group_check=True)
                    op("tensor", "matmul", ps_w[0][:, 0:256],
                       wfm4[:, ci * DH + 128:ci * DH + 256],
                       ssb[:, 256:512], start=False, stop=True,
                       inc=(None if tail else s_mm), skip_group_check=True)
                    if tail:
                        op("tensor", "matmul", ps_w[0][:, 256:512],
                           qsl(0, ci), ssb[:, 0:256], start=True, stop=False,
                           skip_group_check=True)
                        op("tensor", "matmul", ps_w[0][:, 256:512],
                           qsl(1, ci), ssb[:, 256:512], start=False,
                           stop=False, inc=s_mm, skip_group_check=True)
                    p.wait("vector", s_mm)
                    stq.wait_all("vector")
                    op("vector", "tensor_tensor", out=uadj[:],
                       in0=utm4[:, ts(ci, DH)], in1=ps_w[0][:, 0:256],
                       op=ALU.subtract, inc=s_ve)
                    ua = uadj[:]
                    p.wait("tensor", s_ve)
                else:
                    ua = utm4[:, ts(ci, DH)]
                if tail:
                    op("tensor", "matmul", ps_w[0][:, 256:512],
                       attn4[:, ts(ci, 128)], ua, start=False, stop=True,
                       skip_group_check=True)
                op("tensor", "matmul", ps_s[h][:, 0:256],
                   kntm4[:, ci * DH:ci * DH + 128], ua,
                   start=(gc == 0), stop=False, skip_group_check=True)
                op("tensor", "matmul", ps_s[h][:, 256:512],
                   kntm4[:, ci * DH + 128:ci * DH + 256], ua,
                   start=False, stop=(gc == NCK - 1), inc=s_mm,
                   skip_group_check=True)
                if tail:
                    p.wait("vector", s_mm)
                    op("vector", "tensor_copy", out=doutb[:],
                       in_=ps_w[0][:, 256:512], inc=s_ve)
                    p.wait("scalar", s_ve)
                    trow = gc * CK - TAIL
                    st(delta_d[trow:trow + 128, h * DH:(h + 1) * DH],
                       doutb[:])
                    p.wait("tensor", s_ve)

    for sbi in range(NSB):
        ln_and_xn(sbi)
        p.barrier(bar)
        if stop_after == f"PH1:ln:{sbi}":
            return _finish()
        if sbi == 0 and stop_after and stop_after.startswith("PH1:pj"):
            kind, n = stop_after.split(":")[1], int(stop_after.split(":")[2])
            if kind == "pjtl" and n == 108:
                p.wait("vector", s_mm)
                op("vector", "tensor_copy", out=sq2[:, 0:128],
                   in_=ps_w[0][0:128, 0:128], inc=s_ve)
            elif kind == "pjtl" and n in (112, 114):
                p.wait("tensor", s_ve)
                p.wait("tensor", s_ac)
                bank = ps_w[0]
                for k in range(8):
                    for r in range(3):
                        if n == 112:
                            lh = w_hi[:, k * 3072:k * 3072 + 128]
                        else:
                            lh = xnhi[:, ts(k, TSB)][:, 0:128]
                        rhs = xnlo[:, ts(k, TSB)] if r == 1 \
                            else xnhi[:, ts(k, TSB)]
                        op("tensor", "matmul", bank[0:128, 0:TSB], lh, rhs,
                           start=(k == 0 and r == 0),
                           stop=(k == 7 and r == 2),
                           inc=(s_mm if (k == 7 and r == 2) else None))
                p.wait("vector", s_mm)
                op("vector", "tensor_copy", out=sq2[:, 0:TSB],
                   in_=bank[0:128, 0:TSB], inc=s_ve)
            elif kind == "pjtl" and n == 109:
                proj_conv(0, oc_limit=1, do_tail=True, tail_lim=103,
                          bank_override=ps_s[0])
            elif kind == "pjtl":
                proj_conv(0, oc_limit=1, do_tail=True, tail_lim=n)
            else:
                proj_conv(0, oc_limit=n + 1, do_tail=(kind == "pjoc"))
            p.barrier(bar)
            return _finish()
        proj_conv(sbi)
        p.barrier(bar)
        if stop_after == f"PH1:proj:{sbi}":
            return _finish()
        l2norm(sbi)
        p.barrier(bar)
        if stop_after == f"PH1:l2:{sbi}":
            return _finish()
        delta(sbi)
        p.barrier(bar)
        if stop_after == f"PH1:sb:{sbi}":
            return _finish()
        if "d_qn" in dbg:
            for k in range(16):
                st(dbg["d_qn"][sbi][:, ts(k, TSB)], pout[:, ts(k, TSB)])
        if "d_v" in dbg:
            for k in range(8):
                st(dbg["d_v"][sbi][:, ts(k, TSB)], v_out[:, ts(k, TSB)])
        if "d_beta" in dbg:
            st(dbg["d_beta"][sbi], beta_r[:])
        if dbg:
            stq.wait_all("scalar")
            p.barrier(bar)

    if "d_delta" in dbg:
        for r in range(8):
            ld(pout[:, 0:4096].rearrange("p (a c) -> p a c", a=4),
               delta_d[r * 512:(r + 1) * 512, :].rearrange(
                   "(a b) c -> b a c", b=128))
            ldq.wait_all("scalar")
            for j in range(4):
                st(dbg["d_delta"][r * 512 + j * 128:
                                  r * 512 + (j + 1) * 128, :],
                   pout[:, ts(j, D)])
            stq.wait_all("sync")
        p.barrier(bar)
    if stop_after == "PH1":
        return _finish()

    vp3 = v_pad.rearrange("(u p) c -> p u c", p=64)
    mm_mark = [0]
    for cs in range(8):
        p.wait("gpsimd", s_mm)
        g2(rbuf[0:64, 0:8192].rearrange("p (u c) -> p u c", u=64),
           vp3[:, 64:128, cs * 128:(cs + 1) * 128])
        g2(rbuf[64:128, 0:8192].rearrange("p (u c) -> p u c", u=64),
           vp3[:, 65:129, cs * 128:(cs + 1) * 128])
        gpq.wait_all("tensor")
        ev_marks = []
        for ch in range(128):
            gch = cs * 128 + ch
            tb = ttoep[ch % 2]
            q, qeng = (ldq, "sync") if ch % 2 == 0 else (ld2q, "scalar")
            if mm_mark[0] > 0:
                p.wait(qeng, s_mm, mm_mark[0])
            m = q.dma(tb[:], fir_toep[gch])
            mm_mark[0] = s_mm.n
            p.wait("tensor", m[0], m[1])
            if len(ev_marks) >= 4:
                p.wait("tensor", ev_marks[-4][0], ev_marks[-4][1])
            rap = rbuf.rearrange("p (u c) -> p c u", u=64)[:, ch, :]
            op("tensor", "matmul", ps_w[ch % 4][:, 0:64], tb[:], rap,
               start=True, stop=True, inc=s_mm)
            if ch % 2 == 0:
                p.wait("vector", s_mm)
                op("vector", "tensor_copy",
                   out=scat.rearrange("p (u c) -> p c u", u=64)[:, ch, :],
                   in_=ps_w[ch % 4][:, 0:64], inc=s_ve)
                ev_marks.append((s_ve, s_ve.n))
            else:
                p.wait("scalar", s_mm)
                op("scalar", "copy",
                   out=scat.rearrange("p (u c) -> p c u", u=64)[:, ch, :],
                   in_=ps_w[ch % 4][:, 0:64], inc=s_ac)
                ev_marks.append((s_ac, s_ac.n))
        p.wait("gpsimd", s_ve)
        p.wait("gpsimd", s_ac)
        ld3 = l_d.rearrange("(u o) c -> o u c", o=64)
        g2(ld3[:, :, cs * 128:(cs + 1) * 128],
           scat[0:64, 0:8192].rearrange("p (u c) -> p u c", u=64))
        g2(ld3[:, :, D + cs * 128:D + (cs + 1) * 128],
           scat[64:128, 0:8192].rearrange("p (u c) -> p u c", u=64))
        gpq.wait_all("gpsimd")
    p.barrier(bar)
    if "d_L" in dbg:
        for r in range(8):
            ld(pout[:, 0:8192].rearrange("p (a c) -> p a c", a=4),
               l_d[r * 512:(r + 1) * 512, :].rearrange(
                   "(a b) c -> b a c", b=128))
            ldq.wait_all("scalar")
            for j in range(4):
                st(dbg["d_L"][r * 512 + j * 128:r * 512 + (j + 1) * 128, :],
                   pout[:, j * 2048:(j + 1) * 2048])
            stq.wait_all("sync")
        p.barrier(bar)
    if stop_after == "PH2":
        return _finish()

    d_buf = w_hiF[:, 0:4096]
    v_buf = w_hiF[:, 4096:8192]
    hdn = w_hiF[:, 8192:12288]
    L_buf = pout
    base_s = v_out
    o_buf = big3
    stt_tm = scr
    stt_fm = xfm_t[0:16, 0:4 * TSB]

    def softmax_ocombine(sl):
        t0g = sl * TSB
        for bk in range(4):
            p.wait("vector", s_ac)
            op("vector", "tensor_copy", out=fwc[:].rearrange(
                "p (h j) -> p h j", h=4),
               in_=lgtm[:].rearrange("p (h b j) -> p b h j", h=4, b=4)[:, bk],
               inc=s_ve)
            op("vector", "tensor_reduce", out=mx4[:, 0:4],
               in_=fwc[:].rearrange("p (h j) -> p h j", h=4), axis=AX.X,
               op=ALU.max)
            for h in range(H):
                op("vector", "tensor_scalar", out=fwc[:, ts(h, 4)],
                   in0=fwc[:, ts(h, 4)], scalar1=mx4[:, h:h + 1],
                   scalar2=None, op0=ALU.subtract)
            op("vector", "engine_nop", inc=s_ve)
            p.wait("scalar", s_ve)
            op("scalar", "activation", out=fwc[:], in_=fwc[:], func=AF.Exp,
               inc=s_ac)
            p.wait("vector", s_ac)
            op("vector", "tensor_reduce", out=mx4[:, 4:8],
               in_=fwc[:].rearrange("p (h j) -> p h j", h=4), axis=AX.X,
               op=ALU.add, inc=s_ve)
            p.wait("scalar", s_ve)
            op("scalar", "activation", out=mx4[:, 4:8], in_=mx4[:, 4:8],
               func=AF.Ln)
            op("scalar", "activation", out=mx4[:, 4:8], in_=mx4[:, 4:8],
               func=AF.Exp, scale=-1.0, inc=s_ac)
            p.wait("vector", s_ac)
            for h in range(H):
                op("vector", "tensor_scalar", out=fwc[:, ts(h, 4)],
                   in0=fwc[:, ts(h, 4)], scalar1=mx4[:, 4 + h:5 + h],
                   scalar2=None, op0=ALU.mult)
            for h in range(H):
                op("vector", "scalar_tensor_tensor",
                   out=fwc[:, h * 4:h * 4 + 1], in0=sc_rep[:, 1:2],
                   scalar=1.0, in1=fwc[:, h * 4:h * 4 + 1],
                   op0=ALU.mult, op1=ALU.add)
                op("vector", "scalar_tensor_tensor",
                   out=fwc[:, h * 4 + 1:h * 4 + 2], in0=sc_rep[:, 2:3],
                   scalar=1.0, in1=fwc[:, h * 4 + 1:h * 4 + 2],
                   op0=ALU.mult, op1=ALU.add)
            op("vector", "engine_nop", inc=s_ve)
            if "d_fw" in dbg:
                p.wait("scalar", s_ve)
                st(dbg["d_fw"][sl * 4 + bk], fwc[:])
                stq.wait_all("scalar")
                stq.wait_all("vector")
            j = bk
            for h in range(H):
                srt = L_buf[:, j * 2048 + 1024 + h * DH:
                            j * 2048 + 1024 + (h + 1) * DH]
                lng = L_buf[:, j * 2048 + h * DH:j * 2048 + (h + 1) * DH]
                dlt = d_buf[:, j * 1024 + h * DH:j * 1024 + (h + 1) * DH]
                vdr = v_buf[:, j * 1024 + h * DH:j * 1024 + (h + 1) * DH]
                odst = o_buf[:, j * 1024 + h * DH:j * 1024 + (h + 1) * DH]
                op("vector", "tensor_scalar", out=odst, in0=srt,
                   scalar1=fwc[:, h * 4:h * 4 + 1], scalar2=None,
                   op0=ALU.mult)
                op("vector", "scalar_tensor_tensor", out=odst, in0=lng,
                   scalar=fwc[:, h * 4 + 1:h * 4 + 2], in1=odst,
                   op0=ALU.mult, op1=ALU.add)
                op("vector", "scalar_tensor_tensor", out=odst, in0=dlt,
                   scalar=fwc[:, h * 4 + 2:h * 4 + 3], in1=odst,
                   op0=ALU.mult, op1=ALU.add)
                op("vector", "scalar_tensor_tensor", out=odst, in0=vdr,
                   scalar=fwc[:, h * 4 + 3:h * 4 + 4], in1=odst,
                   op0=ALU.mult, op1=ALU.add, inc=s_ve)
                p.wait("scalar", s_ve)
                op("scalar", "activation", out=sq2[:, 0:DH], in_=odst,
                   func=AF.Square,
                   accum_out=rms_c[:, j * 4 + h:j * 4 + h + 1], inc=s_ac)
        p.wait("vector", s_ac)
        op("vector", "tensor_scalar", out=rms_c[:], in0=rms_c[:],
           scalar1=1.0 / DH, scalar2=None, op0=ALU.mult, inc=s_ve)
        p.wait("scalar", s_ve)
        op("scalar", "activation", out=rms_c[:], in_=rms_c[:], func=AF.Ln,
           bias=EPS_LN)
        op("scalar", "activation", out=rms_c[:], in_=rms_c[:], func=AF.Exp,
           scale=-0.5, inc=s_ac)
        p.wait("vector", s_ac)
        for j in range(4):
            for h in range(H):
                odst = o_buf[:, j * 1024 + h * DH:j * 1024 + (h + 1) * DH]
                op("vector", "tensor_scalar", out=odst, in0=odst,
                   scalar1=rms_c[:, j * 4 + h:j * 4 + h + 1], scalar2=None,
                   op0=ALU.mult)
                op("vector", "tensor_tensor", out=odst, in0=odst,
                   in1=onorm_b[:], op=ALU.mult)
        op("vector", "engine_nop", inc=s_ve)
        if "d_o" in dbg:
            p.wait("scalar", s_ve)
            for j in range(4):
                st(dbg["d_o"][t0g + j * 128:t0g + (j + 1) * 128, :],
                   o_buf[:, ts(j, D)])
            stq.wait_all("scalar")
            stq.wait_all("vector")
        p.wait("scalar", s_ve)
        for j in range(4):
            op("scalar", "activation", out=xnhi[:, j * 1024:(j + 1) * 1024]
               .bitcast(BF16)[:, 0:1024] if False else
               xnhi[:, ts(2 * j, TSB)], in_=o_buf[:, j * 1024:j * 1024 + 512],
               func=AF.Copy)
            op("scalar", "activation", out=xnhi[:, ts(2 * j + 1, TSB)],
               in_=o_buf[:, j * 1024 + 512:(j + 1) * 1024], func=AF.Copy,
               inc=s_ac)
        p.wait("vector", s_ac)
        for j in range(4):
            op("vector", "tensor_tensor", out=xnlo[:, ts(2 * j, TSB)],
               in0=o_buf[:, j * 1024:j * 1024 + 512],
               in1=xnhi[:, ts(2 * j, TSB)], op=ALU.subtract)
            op("vector", "tensor_tensor", out=xnlo[:, ts(2 * j + 1, TSB)],
               in0=o_buf[:, j * 1024 + 512:(j + 1) * 1024],
               in1=xnhi[:, ts(2 * j + 1, TSB)], op=ALU.subtract,
               inc=(s_ve if j == 3 else None))
        p.wait("scalar", s_ve)
        for j in range(4):
            st(o_d[0, t0g + j * 128:t0g + (j + 1) * 128, :],
               xnhi[:, ts(2 * j, TSB)].bitcast(BF16)[:, 0:512]
               if False else xnhi[:, 2 * j * TSB:2 * j * TSB + 1024])
            st(o_d[1, t0g + j * 128:t0g + (j + 1) * 128, :],
               xnlo[:, 2 * j * TSB:2 * j * TSB + 1024])
        stq.wait_all("sync")
        stq.wait_all("vector")
        stq.wait_all("gpsimd")

    for sl in range(8):
        t0 = sl * TSB
        ld(d_buf.rearrange("p (a c) -> p a c", a=4),
           delta_d[t0:t0 + TSB, :].rearrange("(a b) c -> b a c", b=128))
        ld(v_buf.rearrange("p (a c) -> p a c", a=4),
           v_pad[64 + TAIL + t0:64 + TAIL + t0 + TSB, :].rearrange(
               "(a b) c -> b a c", b=128))
        ld(L_buf[:, 0:8192].rearrange("p (a c) -> p a c", a=4),
           l_d[t0:t0 + TSB, :].rearrange("(a b) c -> b a c", b=128))
        ld(xnhi[:].rearrange("p (a c) -> p a c", a=8),
           xn_d[0][:, :, t0:t0 + TSB].rearrange("a b c -> b a c"))
        ld(xnlo[:].rearrange("p (a c) -> p a c", a=8),
           xn_d[1][:, :, t0:t0 + TSB].rearrange("a b c -> b a c"))
        ldq.wait_all("vector")
        ldq.wait_all("scalar")
        ldq.wait_all("tensor")
        op("scalar", "activation", out=stat_c[0:1, 0:1],
           in_=scal_s[0:1, :], func=AF.Exp)
        op("scalar", "activation", out=stat_c[0:1, 0:1],
           in_=stat_c[0:1, 0:1], func=AF.Ln, bias=1.0)
        op("scalar", "activation", out=stat_c[0:1, 1:2],
           in_=stat_c[0:1, 0:1], func=AF.Ln, bias=1e-4, inc=s_ac)
        p.wait("tensor", s_ac)
        op("tensor", "matmul", ps_w[2][0:4, 0:1], ones_s[0:1, 0:4],
           stat_c[0:1, 1:2], start=True, stop=True, inc=s_mm)
        p.wait("scalar", s_mm)
        op("scalar", "activation", out=stat_c[0:4, 2:3],
           in_=ps_w[2][0:4, 0:1], func=AF.Exp, scale=-1.0, inc=s_ac)
        p.wait("vector", s_ac)
        op("vector", "tensor_tensor", out=stat_c[0:4, 3:4], in0=b2_s[:],
           in1=stat_c[0:4, 2:3], op=ALU.mult, inc=s_ve)
        p.wait("scalar", s_ve)
        for j in range(4):
            p.wait("scalar", s_mm)
            p.wait("vector", s_mm)
            for h in range(H):
                srcs = [L_buf[:, j * 2048 + 1024 + h * DH:
                              j * 2048 + 1024 + (h + 1) * DH],
                        L_buf[:, j * 2048 + h * DH:j * 2048 + (h + 1) * DH],
                        d_buf[:, j * 1024 + h * DH:j * 1024 + (h + 1) * DH],
                        v_buf[:, j * 1024 + h * DH:j * 1024 + (h + 1) * DH]]
                for si, src in enumerate(srcs):
                    c0 = h * 16 + si * 4
                    op("vector", "tensor_scalar", out=t1[:, 0:DH], in0=src,
                       scalar1=1.0 / DH, scalar2=None, op0=ALU.mult,
                       op1=ALU.add, accum_out=stt_tm[:, c0:c0 + 1])
                    op("scalar", "activation", out=sq2[:, 0:DH], in_=src,
                       func=AF.Square, accum_out=stt_tm[:, c0 + 1:c0 + 2])
                    op("scalar", "activation", out=sq2[:, DH:2 * DH],
                       in_=src, func=AF.Abs,
                       accum_out=stt_tm[:, c0 + 2:c0 + 3])
            op("vector", "engine_nop", inc=s_ve)
            p.wait("scalar", s_ve)
            p.wait("scalar", s_ac)
            sv = stt_tm[:, 0:64].rearrange("p (a c) -> p a c", a=16)
            op("scalar", "activation", out=sv[:, :, 3], in_=sv[:, :, 1],
               func=AF.Sqrt, inc=s_ac)
            p.wait("vector", s_ac)
            op("vector", "tensor_tensor", out=t1[:, 0:16], in0=sv[:, :, 0],
               in1=sv[:, :, 0], op=ALU.mult)
            op("vector", "tensor_scalar", out=sv[:, :, 1], in0=sv[:, :, 1],
               scalar1=1.0 / DH, scalar2=None, op0=ALU.mult)
            op("vector", "tensor_tensor", out=sv[:, :, 1], in0=sv[:, :, 1],
               in1=t1[:, 0:16], op=ALU.subtract)
            op("vector", "tensor_scalar", out=sv[:, :, 2], in0=sv[:, :, 2],
               scalar1=1.0 / DH, scalar2=None, op0=ALU.mult, inc=s_ve)
            if "d_stats" in dbg:
                p.wait("scalar", s_ve)
                st(dbg["d_stats"][sl * 4 + j], stt_tm[:, 0:64])
                stq.wait_all("scalar")
                stq.wait_all("vector")
            p.wait("tensor", s_ve)
            p.wait("tensor", s_ac)
            for h in range(H):
                op("tensor", "transpose", ps_w[3][0:16, ts(h, 128)],
                   stt_tm[:, ts(h, 16)], ident_s[:],
                   inc=(s_mm if h == H - 1 else None))
            p.wait("vector", s_mm)
            for h in range(H):
                op("vector", "tensor_copy",
                   out=stt_fm[:, h * TSB + j * 128:h * TSB + (j + 1) * 128],
                   in_=ps_w[3][0:16, ts(h, 128)],
                   inc=(s_ve if h == H - 1 else None))
            p.wait("tensor", s_ve)
        for k in range(8):
            p.wait("sync", s_mm)
            ld(whh[0][:], w1h_hi_d[k])
            ld(whh[1][:], w1h_lo_d[k])
            ldq.wait_all("tensor")
            for oc in range(8):
                op("tensor", "matmul", ps_all[oc][:, 0:TSB],
                   whh[0][:, ts(oc, 128)], xnhi[:, ts(k, TSB)],
                   start=(k == 0), stop=False)
                op("tensor", "matmul", ps_all[oc][:, 0:TSB],
                   whh[0][:, ts(oc, 128)], xnlo[:, ts(k, TSB)],
                   start=False, stop=False)
                op("tensor", "matmul", ps_all[oc][:, 0:TSB],
                   whh[1][:, ts(oc, 128)], xnhi[:, ts(k, TSB)],
                   start=False, stop=(k == 7),
                   inc=(s_mm if oc == 7 else None))
        p.wait("scalar", s_mm)
        for oc in range(8):
            op("scalar", "copy", out=base_s[:, ts(oc, TSB)],
               in_=ps_all[oc][:, 0:TSB], inc=(s_ac if oc == 7 else None))
        for h in range(H):
            p.wait("tensor", s_ac)
            p.wait("tensor", s_ve)
            for oc in range(8):
                for j in range(4):
                    op("tensor", "matmul",
                       ps_all[oc][:, j * 128:(j + 1) * 128],
                       w1s_s[:, ts(oc, 128)],
                       stt_fm[:, h * TSB + j * 128:h * TSB + (j + 1) * 128],
                       start=True, stop=True,
                       inc=(s_mm if oc == 7 and j == 3 else None))
            p.wait("vector", s_mm)
            for oc in range(8):
                op("vector", "tensor_tensor", out=hdn[:, ts(oc, TSB)],
                   in0=base_s[:, ts(oc, TSB)], in1=ps_all[oc][:, 0:TSB],
                   op=ALU.add, inc=(s_ve if oc == 7 else None))
            p.wait("scalar", s_ve)
            for oc in range(8):
                op("scalar", "activation", out=hdn[:, ts(oc, TSB)],
                   in_=hdn[:, ts(oc, TSB)], func=AF.Gelu,
                   bias=b1_s[:, oc:oc + 1], inc=(s_ac if oc == 7 else None))
            p.wait("tensor", s_ac)
            for oc in range(8):
                op("tensor", "matmul", ps_w[3][0:4, 0:TSB],
                   w2_s[:, ts(oc, 4)], hdn[:, ts(oc, TSB)],
                   start=(oc == 0), stop=(oc == 7),
                   inc=(s_mm if oc == 7 else None))
            p.wait("vector", s_mm)
            op("vector", "tensor_scalar", out=lgs[:],
               in0=ps_w[3][0:4, 0:TSB], scalar1=stat_c[0:4, 2:3],
               scalar2=stat_c[0:4, 3:4], op0=ALU.mult, op1=ALU.add,
               inc=s_ve)
            p.wait("tensor", s_ve)
            for bk in range(4):
                op("tensor", "transpose", ps_w[2][:, ts(bk, 4)],
                   lgs[:, ts(bk, 128)], ident_s[0:4, 0:4],
                   inc=(s_mm if bk == 3 else None))
            p.wait("vector", s_mm)
            op("vector", "tensor_copy", out=lgtm[:, ts(h, 16)],
               in_=ps_w[2][:, 0:16], inc=s_ve)
            p.wait("tensor", s_ve)
        softmax_ocombine(sl)
        p.barrier(bar)
    if stop_after == "PH3":
        return _finish()

    qrhs5 = w_hi[:, 16384:24576]
    otm_hi = w_hi[:, 0:4096]
    otm_lo = w_hi[:, 4096:8192]
    ofm_hi = xnhi
    ofm_lo = xnlo
    y1 = pout
    y1sq = big3
    cent = w_hiF[:, 4096:8192]
    aa = v_out
    ld(qrhs5.rearrange("p (a c) -> p a c", a=8),
       qrhs_d.rearrange("a b c -> b a c"))
    for tl in range(8):
        t0 = tl * TSB
        ld(otm_hi.rearrange("p (a c) -> p a c", a=4),
           o_d[0][t0:t0 + TSB, :].rearrange("(a b) c -> b a c", b=128))
        ld(otm_lo.rearrange("p (a c) -> p a c", a=4),
           o_d[1][t0:t0 + TSB, :].rearrange("(a b) c -> b a c", b=128))
        ldq.wait_all("tensor")
        for pair in range(2):
            src = otm_hi if pair == 0 else otm_lo
            dst = ofm_hi if pair == 0 else ofm_lo
            for j in range(4):
                for oc in range(8):
                    op("tensor", "transpose",
                       ps_w[(j * 8 + oc) % 4][:, 0:64].bitcast(BF16)[:, 0:128],
                       src[:, j * 1024 + oc * 128:j * 1024 + (oc + 1) * 128],
                       identb[:], inc=s_mm)
                    eng = "vector" if (j * 8 + oc) % 2 == 0 else "scalar"
                    esm = s_ve if (j * 8 + oc) % 2 == 0 else s_ac
                    p.wait(eng, s_mm)
                    meth = "tensor_copy" if eng == "vector" else "copy"
                    op(eng, meth,
                       out=dst[:, oc * TSB + j * 128:oc * TSB + (j + 1) * 128],
                       in_=ps_w[(j * 8 + oc) % 4][:, 0:64]
                       .bitcast(BF16)[:, 0:128], inc=esm)
                    p.wait("tensor", esm)
        for k in range(8):
            p.wait("sync", s_mm)
            ld(whh[0][:], wo_hi_d[k])
            ld(whh[1][:], wo_lo_d[k])
            ldq.wait_all("tensor")
            for oc in range(8):
                op("tensor", "matmul", ps_all[oc][:, 0:TSB],
                   whh[0][:, ts(oc, 128)], ofm_hi[:, ts(k, TSB)],
                   start=(k == 0), stop=False)
                op("tensor", "matmul", ps_all[oc][:, 0:TSB],
                   whh[0][:, ts(oc, 128)], ofm_lo[:, ts(k, TSB)],
                   start=False, stop=False)
                op("tensor", "matmul", ps_all[oc][:, 0:TSB],
                   whh[1][:, ts(oc, 128)], ofm_hi[:, ts(k, TSB)],
                   start=False, stop=(k == 7),
                   inc=(s_mm if oc == 7 else None))
        x3 = x_fm.rearrange("(a b) c -> b a c", b=128)
        ld(xfm_t[:].rearrange("p (a c) -> p a c", a=4),
           x3[:, 0:4, TAIL + t0:TAIL + t0 + TSB])
        ldq.wait_all("vector")
        p.wait("vector", s_mm)
        for oc in range(4):
            op("vector", "scalar_tensor_tensor", out=y1[:, ts(oc, TSB)],
               in0=ps_all[oc][:, 0:TSB], scalar=1.0,
               in1=xfm_t[:, ts(oc, TSB)], op0=ALU.mult, op1=ALU.add,
               inc=(s_ve if oc == 3 else None))
        p.wait("sync", s_ve)
        ld(xfm_t[:].rearrange("p (a c) -> p a c", a=4),
           x3[:, 4:8, TAIL + t0:TAIL + t0 + TSB])
        ldq.wait_all("vector")
        for oc in range(4, 8):
            op("vector", "scalar_tensor_tensor", out=y1[:, ts(oc, TSB)],
               in0=ps_all[oc][:, 0:TSB], scalar=1.0,
               in1=xfm_t[:, ts(oc - 4, TSB)], op0=ALU.mult, op1=ALU.add,
               inc=(s_ve if oc == 7 else None))
        if "d_y1" in dbg:
            p.wait("scalar", s_ve)
            st(dbg["d_y1"][tl], y1[:, 0:4096])
            stq.wait_all("scalar")
            stq.wait_all("vector")
        p.wait("scalar", s_ve)
        for oc in range(8):
            op("scalar", "activation", out=y1sq[:, ts(oc, TSB)],
               in_=y1[:, ts(oc, TSB)], func=AF.Square,
               inc=(s_ac if oc == 7 else None))
        p.wait("tensor", s_ve)
        p.wait("tensor", s_ac)
        for oc in range(8):
            op("tensor", "matmul", ps_w[0][:, 0:TSB], ones_s[:],
               y1[:, ts(oc, TSB)], start=(oc == 0), stop=(oc == 7))
            op("tensor", "matmul", ps_w[1][:, 0:TSB], ones_s[:],
               y1sq[:, ts(oc, TSB)], start=(oc == 0), stop=(oc == 7),
               inc=(s_mm if oc == 7 else None))
        p.wait("scalar", s_mm)
        op("scalar", "activation", out=mu_b[:], in_=ps_w[0][:, 0:TSB],
           func=AF.Copy, scale=1.0 / D)
        op("scalar", "activation", out=rs_b[:], in_=ps_w[1][:, 0:TSB],
           func=AF.Copy, scale=1.0 / D, inc=s_ac)
        p.wait("vector", s_ac)
        op("vector", "tensor_tensor", out=t1[:], in0=mu_b[:], in1=mu_b[:],
           op=ALU.mult)
        op("vector", "tensor_tensor", out=rs_b[:], in0=rs_b[:], in1=t1[:],
           op=ALU.subtract, inc=s_ve)
        p.wait("scalar", s_ve)
        op("scalar", "activation", out=rs_b[:], in_=rs_b[:], func=AF.Ln,
           bias=EPS_LN)
        op("scalar", "activation", out=rs_b[:], in_=rs_b[:], func=AF.Exp,
           scale=-0.5, inc=s_ac)
        p.wait("vector", s_ac)
        for oc in range(8):
            op("vector", "tensor_tensor", out=cent[:, ts(oc, TSB)],
               in0=y1[:, ts(oc, TSB)], in1=mu_b[:], op=ALU.subtract,
               inc=(s_ve if oc == 7 else None))
        p.wait("scalar", s_ve)
        for oc in range(8):
            op("scalar", "activation", out=aa[:, ts(oc, TSB)],
               in_=cent[:, ts(oc, TSB)], func=AF.Abs,
               inc=(s_ac if oc == 7 else None))
        p.wait("vector", s_ac)
        for j in range(4):
            op("vector", "tensor_tensor", out=aa[:, ts(j, TSB)],
               in0=aa[:, ts(2 * j, TSB)], in1=aa[:, ts(2 * j + 1, TSB)],
               op=ALU.max)
        for j in range(2):
            op("vector", "tensor_tensor", out=aa[:, ts(j, TSB)],
               in0=aa[:, ts(2 * j, TSB)], in1=aa[:, ts(2 * j + 1, TSB)],
               op=ALU.max)
        op("vector", "tensor_tensor", out=aa[:, 0:TSB], in0=aa[:, 0:TSB],
           in1=aa[:, TSB:2 * TSB], op=ALU.max, inc=s_ve)
        for bk in range(4):
            p.wait("tensor", s_ve)
            p.wait("tensor", s_ac)
            op("tensor", "transpose", ps_w[2][:, 0:128],
               aa[:, bk * 128:(bk + 1) * 128], ident_s[:], inc=s_mm)
            p.wait("vector", s_mm)
            op("vector", "reduce_max", out=stat_c[:, bk:bk + 1],
               in_=ps_w[2][:, 0:128], axis=AX.X, inc=s_ve)
        p.wait("tensor", s_ac)
        p.wait("tensor", s_ve)
        for bk in range(4):
            op("tensor", "transpose", ps_w[2][0:1, ts(bk, 128)],
               stat_c[:, bk:bk + 1], ident_s[:],
               inc=(s_mm if bk == 3 else None))
        p.wait("scalar", s_mm)
        op("scalar", "copy", out=l2row[:, 0:512], in_=ps_w[2][0:1, 0:512],
           inc=s_ac)
        p.wait("tensor", s_ac)
        for bk in range(4):
            op("tensor", "matmul", ps_w[3][:, ts(bk, 128)], ones_s[0:1, :],
               l2row[0:1, ts(bk, 128)], start=True, stop=True,
               inc=(s_mm if bk == 3 else None))
        p.wait("vector", s_mm)
        op("vector", "tensor_copy", out=t1[:], in_=ps_w[3][:, 0:TSB],
           inc=s_ve)
        op("vector", "tensor_tensor", out=t1[:], in0=t1[:], in1=rs_b[:],
           op=ALU.mult)
        op("vector", "tensor_scalar", out=t1[:], in0=t1[:], scalar1=1e-12,
           scalar2=None, op0=ALU.max, inc=s_ve)
        p.wait("scalar", s_ve)
        op("scalar", "activation", out=t1[:], in_=t1[:], func=AF.Ln,
           inc=s_ac)
        p.wait("scalar", s_ac)
        op("scalar", "activation", out=mu_b[:], in_=t1[:], func=AF.Exp,
           scale=-1.0, bias=LN127)
        op("scalar", "activation", out=t1[:], in_=t1[:], func=AF.Exp,
           scale=1.0, bias=-LN127, inc=s_ac)
        p.wait("vector", s_ac)
        op("vector", "tensor_tensor", out=mu_b[:], in0=mu_b[:], in1=rs_b[:],
           op=ALU.mult, inc=s_ve)
        p.wait("vector", s_ve)
        for oc in range(8):
            op("vector", "tensor_tensor", out=cent[:, ts(oc, TSB)],
               in0=cent[:, ts(oc, TSB)], in1=mu_b[:], op=ALU.mult)
            round_halfup("vector", ofm_hi[:, ts(oc, TSB)],
                         cent[:, ts(oc, TSB)], sq2[:, 0:TSB],
                         sq2[:, TSB:2 * TSB])
        op("vector", "engine_nop", inc=s_ve)
        if "d_qlhs" in dbg:
            p.wait("scalar", s_ve)
            for oc in range(8):
                op("scalar", "activation", out=y1sq[:, ts(oc, TSB)][:, 0:TSB],
                   in_=ofm_hi[:, ts(oc, TSB)], func=AF.Copy,
                   inc=(s_ac if oc == 7 else None))
            p.wait("scalar", s_ac)
            st(dbg["d_qlhs"][tl], y1sq[:, 0:4096])
            stq.wait_all("scalar")
            stq.wait_all("vector")
        p.wait("tensor", s_ve)
        for oc in range(8):
            for k in range(8):
                op("tensor", "matmul", ps_all[oc][:, 0:TSB],
                   qrhs5[:, k * D + oc * 128:k * D + (oc + 1) * 128],
                   ofm_hi[:, ts(k, TSB)], start=(k == 0), stop=(k == 7),
                   inc=(s_mm if k == 7 else None))
        p.wait("vector", s_mm)
        for oc in range(8):
            op("vector", "tensor_scalar", out=y1sq[:, ts(oc, TSB)],
               in0=ps_all[oc][:, 0:TSB], scalar1=rsc_c[:, oc:oc + 1],
               scalar2=None, op0=ALU.mult)
            op("vector", "tensor_tensor", out=y1sq[:, ts(oc, TSB)],
               in0=y1sq[:, ts(oc, TSB)], in1=t1[:], op=ALU.mult)
            op("vector", "tensor_tensor", out=y1sq[:, ts(oc, TSB)],
               in0=y1sq[:, ts(oc, TSB)], in1=y1[:, ts(oc, TSB)], op=ALU.add,
               inc=(s_ve if oc == 7 else None))
        p.wait("scalar", s_ve)
        for oc in range(8):
            st(out_fm[ts(oc, 128), t0:t0 + TSB], y1sq[:, ts(oc, TSB)])
        stq.wait_all("sync")
        p.barrier(bar)

    return _finish()



def make_fir_toeplitz(fir_long, fir_short):
    T = np.zeros((D, 128, 128), dtype=np.float32)
    wl = fir_long.reshape(D, 64)
    ws = fir_short.reshape(D, 5)
    for po in range(64):
        for k in range(64):
            pp = po + 1 + k
            if pp < 128:
                T[:, pp, po] = wl[:, k]
        for k in range(5):
            pp = po + 60 + k
            if pp < 128:
                T[:, pp, 64 + po] = ws[:, k]
    return T


def stage_inputs(inputs):
    x = np.ascontiguousarray(inputs["L0_attn_hidden_states"], np.float32)
    wq = np.asarray(inputs["L0_attn_q_proj_w"], np.float32)
    wk = np.asarray(inputs["L0_attn_k_proj_w"], np.float32)
    wv = np.asarray(inputs["L0_attn_v_proj_w"], np.float32)
    bp = np.asarray(inputs["L0_attn_b_proj_w"], np.float32)
    wqkvb = np.ascontiguousarray(
        np.concatenate([wq.T, wk.T, wv.T], axis=1))
    wbeta = np.zeros((D, 32), np.float32)
    wbeta[:, 0:4] = bp.T
    convw = np.ascontiguousarray(np.concatenate(
        [np.asarray(inputs["L0_attn_q_conv_w"], np.float32),
         np.asarray(inputs["L0_attn_k_conv_w"], np.float32),
         np.asarray(inputs["L0_attn_v_conv_w"], np.float32)], axis=0))
    fir_toep = make_fir_toeplitz(
        np.asarray(inputs["L0_attn_fir_long_w"], np.float32),
        np.asarray(inputs["L0_attn_fir_short_w"], np.float32))
    fw1 = np.asarray(inputs["L0_attn_fuse_w1"], np.float32)
    shared = dict(
        wqkvb=wqkvb, wbeta=wbeta, convw=convw, fir_toep=fir_toep,
        w1h=np.ascontiguousarray(fw1[:, :D].T),
        w1s=np.ascontiguousarray(fw1[:, D:].T),
        b1=np.ascontiguousarray(np.asarray(
            inputs["L0_attn_fuse_b1"], np.float32).reshape(8, 128).T),
        w2=np.ascontiguousarray(
            np.asarray(inputs["L0_attn_fuse_w2"], np.float32).T),
        b2=np.asarray(inputs["L0_attn_fuse_b2"], np.float32),
        wo=np.ascontiguousarray(
            np.asarray(inputs["L0_attn_o_proj_w"], np.float32).T),
        ffn_rhs=np.ascontiguousarray(inputs["L0_ffn_rhs"], np.float32),
        o_norm=np.asarray(inputs["L0_attn_o_norm_w"], np.float32),
        scal=np.array([inputs["L0_attn_logit_temp"][0],
                       inputs["L0_attn_res_scale_short"][0],
                       inputs["L0_attn_res_scale_long"][0], 0.0], np.float32),
        ident=np.eye(128, dtype=np.float32),
        cvec=np.tile(np.array([1e-5, 1e-12, 1e-4, 1.0, -0.5, -1.0,
                               1.0 / 1024, LN127, -LN127, 0.5, -1e-12, 2.0,
                               0.0, 3.0, 4.0, 5.0], np.float32), (128, 1)),
        ones=np.ones((128, 128), np.float32),
        mask_low=(-np.tril(np.ones((128, 128), np.float32), -1)),
        mask_up0=np.triu(np.ones((128, 128), np.float32), 0),
    )
    in_maps = []
    for c in range(8):
        b, half = c // 2, c % 2
        buf = np.zeros((TBUF, D), np.float32)
        if half == 0:
            buf[TAIL:] = x[b, :TAIL]
        else:
            buf[:] = x[b]
        m = dict(shared)
        m["x_tm"] = buf
        m["x_fm"] = np.ascontiguousarray(buf.T)
        in_maps.append(m)
    return in_maps


_NC_CACHE = {}


def kernel(**inputs):
    in_maps = stage_inputs(inputs)
    if "nc" not in _NC_CACHE:
        _NC_CACHE["nc"] = build_program()[0]
    nc = _NC_CACHE["nc"]
    res = run_bass_kernel_spmd(nc, in_maps, core_ids=list(range(8)))
    out = np.empty((B, L, D), np.float32)
    for c in range(8):
        b, half = c // 2, c % 2
        out[b, half * TAIL:(half + 1) * TAIL] = res.results[c]["out_fm"].T
    return out
